# version 05
# baseline (speedup 1.0000x reference)
import contextlib
import numpy as np
import ml_dtypes
import concourse.bass as bass
import concourse.mybir as mybir
from concourse.bass_utils import run_bass_kernel_spmd

F32 = mybir.dt.float32
BF16 = mybir.dt.bfloat16
I32 = mybir.dt.int32
AF = mybir.ActivationFunctionType
ALU = mybir.AluOpType
AX = mybir.AxisListType

NCORES = 8
S = 8192
D = 1024
H = 16
HD = 64
NT = 16384
TLOC = 2048
CAP = 384
NE = 8
FPAD = 2816
NFC = FPAD // 128
VPAD = 16384
EPS = 1e-5
NBIS = 31
KSEL = 2048
VT_BLKS = 32


def build_program(dbg_names=(), upto=99, skip=(), nhp=8, nkc=64):
    nc = bass.Bass("TRN2", target_bir_lowering=False)
    ctx = contextlib.ExitStack()
    dbg_names = set(dbg_names)
    dbg_tensors = {}

    def din(name, shape, dt=F32):
        return nc.dram_tensor(name, shape, dt, kind="ExternalInput").ap()

    x_in = din("x_loc", [TLOC, D])
    wqt_in = din("wqt", [D, D])
    wkt_in = din("wkt", [D, D])
    wvt_in = din("wvt", [D, D])
    wr_in = din("wr", [D, NE])
    br_in = din("br", [NE, 1])
    wup_in = din("wup", [4, D, FPAD])
    bup_in = din("bup", [4, FPAD])
    wdn_in = din("wdn", [NE, FPAD, D])
    bdn_in = din("bdn", [NE, D])
    btail_in = din("btail", [H, 32, 128, 128])
    crow_in = din("crow", [1, D])
    grow_in = din("grow", [1, D])
    brow_in = din("brow", [1, D])
    ident_in = din("ident", [128, 128])
    trishift_in = din("trishift", [128, 128])
    onehotD_in = din("onehotD", [128, 128])
    lstrict_in = din("lstrict", [128, 128])
    onescol_in = din("onescol", [128, 1])
    iota_slot_in = din("iota_slot", [128, CAP])
    tloc_p_in = din("tloc_p", [128, 1])
    bexw_in = din("bexw", [1, 64 * 16])
    selm_in = din("selm", [16, 16 * 128])
    vtrows_in = din("vtrows", [128, VT_BLKS], I32)
    selrows_in = din("selrows", [128, 16], I32)

    out = nc.dram_tensor("out", [TLOC, D], F32, kind="ExternalOutput").ap()

    xn_dram = nc.dram_tensor("xn_dram", [TLOC, D], BF16)
    k_loc = nc.dram_tensor("k_loc", [TLOC, D], BF16)
    v_loc = nc.dram_tensor("v_loc", [TLOC, D], BF16)
    k_full = nc.dram_tensor("k_full", [S, D], BF16)
    v_pad = nc.dram_tensor("v_pad", [VPAD, D], BF16)
    vt_dram = nc.dram_tensor("vt_dram", [VT_BLKS * 128, D], BF16)
    probs_dram = nc.dram_tensor("probs_dram", [TLOC, NE], F32)
    probs_all = nc.dram_tensor("probs_all", [NT, NE], F32)
    sel_dram = nc.dram_tensor("sel_dram", [NT, NE], F32)
    selloc_dram = nc.dram_tensor("selloc_dram", [TLOC, NE], F32)
    y_dram = nc.dram_tensor("y_dram", [TLOC, D], F32)

    sbt = lambda name, shape, dt: ctx.enter_context(nc.sbuf_tensor(name, shape, dt))

    ident_bf = sbt("c_identb", [128, 128], BF16)
    ident_f = sbt("c_identf", [128, 128], F32)
    trishift = sbt("c_trish", [128, 128], BF16)
    onehotD = sbt("c_ohD", [128, 128], BF16)
    lstrict = sbt("c_lstr", [128, 128], BF16)
    onescol = sbt("c_ones", [128, 1], BF16)
    onesrow_f = sbt("c_onesrf", [1, 128], F32)
    iota_slot = sbt("c_iotas", [128, CAP], F32)
    tloc_p = sbt("c_tlocp", [128, 1], BF16)
    bexw = sbt("c_bexw", [128, 64, 16], BF16)
    selm = sbt("c_selm", [16, 16 * 128], BF16)
    vtrows = sbt("c_vtrows", [128, VT_BLKS], I32)
    selrows = sbt("c_selrows", [128, 16], I32)
    ctile = sbt("c_ctile", [128, D], F32)
    gtile = sbt("c_gtile", [128, D], F32)
    btile = sbt("c_btile", [128, D], F32)
    brcol = sbt("c_brcol", [NE, 1], F32)

    stat_a = sbt("stat_a", [128, 16], F32)
    stat_b = sbt("stat_b", [128, 16], F32)
    stat_c = sbt("stat_c", [128, 16], F32)
    stat_d = sbt("stat_d", [128, 16], F32)
    stat_e = sbt("stat_e", [128, 16], F32)
    scr1k = sbt("scr1k", [128, D], F32)
    scr1kb = sbt("scr1kb", [128, D], BF16)
    idxi = sbt("idxi", [128, 3, NE], I32)
    gval = sbt("gval", [128, 3, NE], F32)
    attn_tok = sbt("attn_tok", [128, 16, D], BF16)

    class Sem:
        __slots__ = ("h", "n")
        def __init__(self, name):
            self.h = ctx.enter_context(nc.semaphore(name))
            self.n = 0
        def inc(self, instr, k=1):
            instr.then_inc(self.h, k)
            self.n += k
            return self.n

    dS = Sem("dS")
    dG = Sem("dG")
    dA = Sem("dA")
    dC = Sem("dC")
    dX = [Sem("dX0"), Sem("dX1"), Sem("dX2"), Sem("dX3")]
    dXN = [Sem("dXN0"), Sem("dXN1"), Sem("dXN2"), Sem("dXN3")]
    dKV = [Sem("dKV0"), Sem("dKV1")]
    dPR = Sem("dPR")
    dZ = Sem("dZ")
    dPG = Sem("dPG")
    dSEL = Sem("dSEL")
    dKT = [Sem("dKT0"), Sem("dKT1")]
    dVT = Sem("dVT")
    dBF = Sem("dBF")
    dPS = [Sem("dPS0"), Sem("dPS1"), Sem("dPS2"), Sem("dPS3")]
    dP2 = [Sem("dP20"), Sem("dP21"), Sem("dP22")]
    dFX = Sem("dFX")
    dVC = Sem("dVC")
    dSL = Sem("dSL")
    dGG = Sem("dGG")
    dBD = Sem("dBD")
    dWU = [Sem("dWU0"), Sem("dWU1")]
    dWD = [Sem("dWD0"), Sem("dWD1")]
    dXG = Sem("dXG")
    dY = Sem("dY")
    sPE = Sem("sPE")
    sACT = Sem("sACT")
    sVE = Sem("sVE")
    sPO = Sem("sPO")
    sCC = Sem("sCC")

    pA = ctx.enter_context(nc.psum_tensor("pA", [128, 1024], F32))
    pB = ctx.enter_context(nc.psum_tensor("pB", [128, 1024], F32))
    pC = ctx.enter_context(nc.psum_tensor("pC", [128, 512], F32))
    pD = ctx.enter_context(nc.psum_tensor("pD", [128, 512], F32))
    pBF = ctx.enter_context(nc.psum_tensor("pBF", [128, 1024], BF16))
    pBF2 = ctx.enter_context(nc.psum_tensor("pBF2", [128, 1024], BF16))

    block = ctx.enter_context(nc.Block())

    ctxQ = contextlib.ExitStack()
    qt = ctxQ.enter_context(nc.sbuf_tensor("qt", [128, 8, TLOC], BF16))

    ctxP = contextlib.ExitStack()
    sbtP = lambda name, shape, dt: ctxP.enter_context(nc.sbuf_tensor(name, shape, dt))
    xrot = sbtP("xrot", [128, 4, D], F32)
    xnrot = sbtP("xnrot", [128, 4, D], BF16)
    xnT = sbtP("xnT", [128, 8, TLOC], BF16)
    wbuf = sbtP("wbuf", [128, 8, D], BF16)
    wr_sb = sbtP("wr_sb", [128, 8, NE], BF16)
    kv_ev = sbtP("kv_ev", [128, 2, 512], BF16)
    logit_sb = sbtP("logit_sb", [NE, TLOC], F32)
    ltok = sbtP("ltok", [128, 16, NE], F32)
    ptok = sbtP("ptok", [128, 16, NE], F32)
    lred = sbtP("lred", [128, 16], F32)
    zt = sbtP("zt", [128, 1024], BF16)
    zf32 = sbtP("zf32", [128, 1024], F32)

    @block.sync
    def _(s):
        dC.inc(s.dma_start(out=ident_f[:], in_=ident_in), 16)
        dC.inc(s.dma_start(out=onesrow_f[:],
                           in_=onescol_in.rearrange("p o -> o p")[0:1, 0:128]), 16)
        dC.inc(s.dma_start(out=iota_slot[:], in_=iota_slot_in), 16)
        dC.inc(s.dma_start(out=ctile[:], in_=crow_in.partition_broadcast(128)), 16)
        dC.inc(s.dma_start(out=gtile[:], in_=grow_in.partition_broadcast(128)), 16)
        dC.inc(s.dma_start(out=btile[:], in_=brow_in.partition_broadcast(128)), 16)
    LD_X = dC.n

    @block.gpsimd
    def _(g):
        dG.inc(g.dma_start(out=ident_bf[:], in_=ident_in), 16)
        dG.inc(g.dma_start(out=trishift[:], in_=trishift_in), 16)
        dG.inc(g.dma_start(out=onehotD[:], in_=onehotD_in), 16)
        dG.inc(g.dma_start(out=lstrict[:], in_=lstrict_in), 16)
        dG.inc(g.dma_start(out=onescol[:], in_=onescol_in), 16)
        dG.inc(g.dma_start(out=tloc_p[:], in_=tloc_p_in), 16)
        dG.inc(g.dma_start(out=bexw[:], in_=bexw_in.rearrange("o (k a) -> o k a", a=16)
                           .partition_broadcast(128)), 16)
        dG.inc(g.dma_start(out=selm[:], in_=selm_in), 16)
        dG.inc(g.dma_start(out=vtrows[:], in_=vtrows_in), 16)
        dG.inc(g.dma_start(out=selrows[:], in_=selrows_in), 16)
        dG.inc(g.dma_start(out=brcol[:], in_=br_in), 16)
        dG.inc(g.dma_start(out=wbuf[:], in_=wqt_in.rearrange("(c p) o -> p c o", p=128)), 16)
        dG.inc(g.dma_start(out=wr_sb[:], in_=wr_in.rearrange("(c p) o -> p c o", p=128)), 16)
    LD_W = dG.n

    xv = x_in.rearrange("(a p) d -> p a d", p=128)
    DS_L0 = dS.n
    VE_L0s = sVE.n
    ACT_L0 = sACT.n

    for i in range(16):
        @block.sync
        def _(s, i=i):
            if i == 0:
                s.wait_ge(dC.h, LD_X)
            if i >= 4:
                s.wait_ge(sVE.h, VE_L0s + (i - 3))
                s.wait_ge(sACT.h, ACT_L0 + (i - 3))
            ins = s.dma_start(out=xrot[:, i % 4, :], in_=xv[:, i, :])
            dX[i % 4].inc(ins, 16)
        dsn = dX[i % 4].n

        @block.vector
        def _(v, i=i, dsn=dsn):
            v.wait_ge(dX[i % 4].h, dsn)
            sVE.inc(v.reduce_sum(stat_a[:, i:i+1], xrot[:, i % 4, :],
                                 axis=AX.X, negate=True))

        @block.scalar
        def _(a, i=i, dsn=dsn):
            a.wait_ge(dX[i % 4].h, dsn)
            if i > 0:
                a.wait_ge(sACT.h, sACT.n)
            sACT.inc(a.activation(scr1k[:], xrot[:, i % 4, :], AF.Square,
                                  accum_out=stat_b[:, i:i+1]))
    VE_NS = sVE.n
    ACT_SQ = sACT.n

    @block.vector
    def _(v):
        v.wait_ge(sACT.h, ACT_SQ)
        v.wait_ge(sVE.h, VE_NS)
        sVE.inc(v.tensor_scalar(stat_c[:], stat_a[:], 1.0 / D, None, ALU.mult))
        v.wait_ge(sVE.h, sVE.n)
        sVE.inc(v.tensor_mul(stat_d[:], stat_c[:], stat_c[:]))
        v.wait_ge(sVE.h, sVE.n)
        sVE.inc(v.tensor_scalar(stat_a[:], stat_b[:], 1.0 / D, None, ALU.mult))
        v.wait_ge(sVE.h, sVE.n)
        sVE.inc(v.tensor_sub(stat_d[:], stat_a[:], stat_d[:]))
    VE_VAR = sVE.n

    @block.vector
    def _(v):
        v.wait_ge(sVE.h, VE_VAR)
        sVE.inc(v.tensor_scalar(stat_d[:], stat_d[:], EPS, None, ALU.add))
    VE_VEPS = sVE.n

    @block.scalar
    def _(a):
        a.wait_ge(sVE.h, VE_VEPS)
        sACT.inc(a.activation(stat_d[:], stat_d[:], AF.Ln))
        a.wait_ge(sACT.h, sACT.n)
        sACT.inc(a.activation(stat_d[:], stat_d[:], AF.Exp, scale=-0.5))
    ACT_RSTD = sACT.n

    @block.vector
    def _(v):
        v.wait_ge(sACT.h, ACT_RSTD)
        sVE.inc(v.tensor_mul(stat_e[:], stat_c[:], stat_d[:]))
    VE_MRS = sVE.n

    DS_L2 = dS.n
    ACT_X0 = sACT.n
    PE_T0 = sPE.n
    VE_T0 = sVE.n
    for i in range(16):
        @block.sync
        def _(s, i=i):
            if i < 4:
                s.wait_ge(sACT.h, ACT_SQ)
                s.wait_ge(sVE.h, VE_NS)
            else:
                s.wait_ge(sACT.h, ACT_X0 + (i - 3))
            ins = s.dma_start(out=xrot[:, i % 4, :], in_=xv[:, i, :])
            dX[i % 4].inc(ins, 16)
        dsn = dX[i % 4].n

        @block.scalar
        def _(a, i=i, dsn=dsn, dxn_w=(dXN[i % 4].n if i >= 4 else 0)):
            a.wait_ge(dX[i % 4].h, dsn)
            if i == 0:
                a.wait_ge(sVE.h, VE_MRS)
            if i >= 4:
                a.wait_ge(sPE.h, PE_T0 + (i - 3) * 8)
                a.wait_ge(dXN[i % 4].h, dxn_w)
            sACT.inc(a.activation(xnrot[:, i % 4, :], xrot[:, i % 4, :], AF.Identity,
                                  bias=stat_e[:, i:i+1], scale=stat_d[:, i:i+1]))
        acn = sACT.n

        @block.sync
        def _(s, i=i, acn=acn):
            s.wait_ge(sACT.h, acn)
            ins = s.dma_start(out=xn_dram.ap().rearrange("(a p) d -> p a d", p=128)[:, i, :],
                              in_=xnrot[:, i % 4, :])
            dXN[i % 4].inc(ins, 16)

        @block.tensor
        def _(t, i=i, acn=acn):
            t.wait_ge(sACT.h, acn)
            if i == 0:
                t.wait_ge(dG.h, LD_W)
            for c in range(8):
                k = i * 8 + c
                buf = pBF if k % 2 == 0 else pBF2
                if k >= 2:
                    t.wait_ge(sVE.h, VE_T0 + k - 1)
                sPE.inc(t.transpose(buf[:, 0:128], xnrot[:, i % 4, c*128:(c+1)*128],
                                    ident_bf[:]))

        @block.vector
        def _(v, i=i):
            for c in range(8):
                k = i * 8 + c
                buf = pBF if k % 2 == 0 else pBF2
                v.wait_ge(sPE.h, PE_T0 + k + 1)
                sVE.inc(v.tensor_copy(xnT[:, c, i*128:(i+1)*128], buf[:, 0:128]))
    PE_XNT, VE_XNT = sPE.n, sVE.n
    ST_XN = dS.n

    PE_P0, VE_P0 = sPE.n, sVE.n

    @block.tensor
    def _(t):
        t.wait_ge(sVE.h, VE_XNT)
        t.wait_ge(dG.h, LD_W)
        for n in range(32):
            dc, tb = divmod(n, 4)
            buf = pC if n % 2 == 0 else pD
            if n >= 2:
                t.wait_ge(sVE.h, VE_P0 + n - 1)
            for k in range(8):
                ins = t.matmul(buf[:], wbuf[:, k, dc*128:(dc+1)*128],
                               xnT[:, k, tb*512:(tb+1)*512],
                               start=(k == 0), stop=(k == 7))
            sPE.inc(ins)

    @block.vector
    def _(v):
        for n in range(32):
            dc, tb = divmod(n, 4)
            buf = pC if n % 2 == 0 else pD
            v.wait_ge(sPE.h, PE_P0 + n + 1)
            sVE.inc(v.tensor_copy(qt[:, dc, tb*512:(tb+1)*512], buf[:]))
    PE_QT, VE_QT = sPE.n, sVE.n

    @block.gpsimd
    def _(g):
        g.wait_ge(sPE.h, PE_QT)
        ins = g.dma_start(out=wbuf[:], in_=wkt_in.rearrange("(c p) o -> p c o", p=128))
        dG.inc(ins, 16)
    LD_WK = dG.n

    def kv_pass(tgt, ld_wait):
        pe0, ve0 = sPE.n, sVE.n
        dkv0 = [dKV[0].n, dKV[1].n]

        @block.tensor
        def _(t):
            t.wait_ge(dG.h, ld_wait)
            t.wait_ge(sVE.h, ve0)
            for n in range(32):
                tc, ob = divmod(n, 2)
                buf = pC if n % 2 == 0 else pD
                if n >= 2:
                    t.wait_ge(sVE.h, ve0 + n - 1)
                for k in range(8):
                    ins = t.matmul(buf[:], xnT[:, k, tc*128:(tc+1)*128],
                                   wbuf[:, k, ob*512:(ob+1)*512],
                                   start=(k == 0), stop=(k == 7))
                sPE.inc(ins)

        @block.vector
        def _(v):
            for n in range(32):
                buf = pC if n % 2 == 0 else pD
                v.wait_ge(sPE.h, pe0 + n + 1)
                if n >= 2:
                    v.wait_ge(dKV[n % 2].h, dkv0[n % 2] + (n - 2) // 2 * 16 + 16)
                else:
                    v.wait_ge(dKV[n % 2].h, dkv0[n % 2])
                sVE.inc(v.tensor_copy(kv_ev[:, n % 2, :], buf[:]))

        @block.sync
        def _(s):
            for n in range(32):
                tc, ob = divmod(n, 2)
                s.wait_ge(sVE.h, ve0 + n + 1)
                ins = s.dma_start(out=tgt.ap()[tc*128:(tc+1)*128, ob*512:(ob+1)*512],
                                  in_=kv_ev[:, n % 2, :])
                dKV[n % 2].inc(ins, 16)

    kv_pass(k_loc, LD_WK)
    PE_K = sPE.n

    @block.gpsimd
    def _(g):
        g.wait_ge(sPE.h, PE_K)
        ins = g.dma_start(out=wbuf[:], in_=wvt_in.rearrange("(c p) o -> p c o", p=128))
        dG.inc(ins, 16)
    LD_WV = dG.n

    kv_pass(v_loc, LD_WV)
    ST_KV0, ST_KV1 = dKV[0].n, dKV[1].n

    PE_R0, ACT_R0 = sPE.n, sACT.n

    @block.tensor
    def _(t):
        for tb in range(4):
            if tb >= 1:
                t.wait_ge(sACT.h, ACT_R0 + tb)
            for k in range(8):
                ins = t.matmul(pA[0:8, 0:512], wr_sb[:, k, :],
                               xnT[:, k, tb*512:(tb+1)*512],
                               start=(k == 0), stop=(k == 7))
            sPE.inc(ins)

    @block.scalar
    def _(a):
        for tb in range(4):
            a.wait_ge(sPE.h, PE_R0 + tb + 1)
            sACT.inc(a.activation(logit_sb[:, tb*512:(tb+1)*512], pA[0:8, 0:512],
                                  AF.Identity, bias=brcol[:]))
    ACT_RT = sACT.n

    PE_L0, VE_L0 = sPE.n, sVE.n

    @block.tensor
    def _(t):
        t.wait_ge(sACT.h, ACT_RT)
        for tc in range(16):
            if tc >= 1:
                t.wait_ge(sVE.h, VE_L0 + tc)
            sPE.inc(t.transpose(pB[:, 0:8], logit_sb[:, tc*128:(tc+1)*128],
                                ident_f[0:8, 0:8]))

    @block.vector
    def _(v):
        for tc in range(16):
            v.wait_ge(sPE.h, PE_L0 + tc + 1)
            sVE.inc(v.tensor_copy(ltok[:, tc, :], pB[:, 0:8]))
        v.wait_ge(sVE.h, sVE.n)
        for i in range(16):
            ins = v.reduce_max(lred[:, i:i+1], ltok[:, i, :], axis=AX.X, negate=True)
        sVE.inc(ins)
    VE_LT = sVE.n

    @block.scalar
    def _(a):
        a.wait_ge(sVE.h, VE_LT)
        for i in range(16):
            ins = a.activation(ptok[:, i, :], ltok[:, i, :], AF.Exp, bias=lred[:, i:i+1])
        sACT.inc(ins)
    ACT_PEXP = sACT.n

    @block.vector
    def _(v):
        v.wait_ge(sACT.h, ACT_PEXP)
        for i in range(16):
            ins = v.reduce_sum(lred[:, i:i+1], ptok[:, i, :], axis=AX.X)
        sVE.inc(ins)
        v.wait_ge(sVE.h, sVE.n)
        sVE.inc(v.reciprocal(lred[:], lred[:]))
        v.wait_ge(sVE.h, sVE.n)
        for i in range(16):
            ins = v.tensor_scalar(ptok[:, i, :], ptok[:, i, :], lred[:, i:i+1],
                                  None, ALU.mult)
        sVE.inc(ins)
    VE_PROB = sVE.n

    @block.sync
    def _(s):
        s.wait_ge(sVE.h, VE_PROB)
        ins = s.dma_start(out=probs_dram.ap().rearrange("(a p) e -> p a e", p=128),
                          in_=ptok[:])
        dPR.inc(ins, 16)
    ST_PROBS = dPR.n

    @block.gpsimd
    def _(g):
        g.wait_ge(dKV[0].h, ST_KV0)
        g.wait_ge(dKV[1].h, ST_KV1)
        sCC.inc(g.collective_compute(
            "AllGather", ALU.bypass,
            replica_groups=[[0, 1, 2, 3], [4, 5, 6, 7]],
            ins=[k_loc.ap().opt()], outs=[k_full.ap().opt()]))
        sCC.inc(g.collective_compute(
            "AllGather", ALU.bypass,
            replica_groups=[[0, 1, 2, 3], [4, 5, 6, 7]],
            ins=[v_loc.ap().opt()], outs=[v_pad.ap()[0:S, :].opt()]))
        g.wait_ge(dPR.h, ST_PROBS)
        sCC.inc(g.collective_compute(
            "AllGather", ALU.bypass,
            replica_groups=[list(range(8))],
            ins=[probs_dram.ap().opt()], outs=[probs_all.ap().opt()]))
    CC_KV = 2
    CC_ALL = sCC.n

    @block.vector
    def _(v):
        sVE.inc(v.memset(zt[:], 0.0))
        sVE.inc(v.memset(zf32[:], 0.0))
    VE_ZT = sVE.n

    @block.sync
    def _(s):
        s.wait_ge(sVE.h, VE_ZT)
        for blk in range(16):
            ins = s.dma_start(
                out=v_pad.ap()[S + blk*512:S + (blk+1)*512, :]
                    .rearrange("(n p) d -> p n d", p=128),
                in_=zt[:].unsqueeze(1).to_broadcast([128, 4, 1024]))
            dZ.inc(ins, 16)
        for blk in range(4):
            ins = s.dma_start(
                out=y_dram.ap()[blk*512:(blk+1)*512, :]
                    .rearrange("(n p) d -> p n d", p=128),
                in_=zf32[:].unsqueeze(1).to_broadcast([128, 4, 1024]))
            dZ.inc(ins, 16)
    ST_ZE = dZ.n

    vtcp = sbtP("vtcp", [128, 2, D], BF16)
    @block.gpsimd
    def _(g):
        g.wait_ge(sCC.h, CC_KV)
        g.wait_ge(dZ.h, ST_ZE)
        g.wait_ge(dG.h, dG.n)
        for b in range(VT_BLKS):
            ins = g.indirect_dma_start(
                out=vtcp[:, b % 2, :], out_offset=None,
                in_=v_pad.ap(),
                in_offset=bass.IndirectOffsetOnAxis(ap=vtrows[:, b:b+1], axis=0))
            dVC.inc(ins, 16)
            g.wait_ge(dVC.h, dVC.n)
            ins = g.dma_start(out=vt_dram.ap()[b*128:(b+1)*128, :],
                              in_=vtcp[:, b % 2, :])
            dVC.inc(ins, 16)
        g.wait_ge(dVC.h, dVC.n)
    CP_VT = dVC.n

    ctxP.close()
    ctxS = contextlib.ExitStack()
    sbtS = lambda name, shape, dt: ctxS.enter_context(nc.sbuf_tensor(name, shape, dt))
    pg = sbtS("pg", [128, 128, NE], F32)
    b_lo = sbtS("b_lo", [128, NE], I32)
    b_hi = sbtS("b_hi", [128, NE], I32)
    b_mid = sbtS("b_mid", [128, NE], I32)
    b_t1 = sbtS("b_t1", [128, NE], I32)
    b_t2 = sbtS("b_t2", [128, NE], I32)
    b_ge = sbtS("b_ge", [128, NE], F32)
    b_gei = sbtS("b_gei", [128, NE], I32)
    b_nge = sbtS("b_nge", [128, NE], I32)
    b_cntb = sbtS("b_cntb", [128, NE], BF16)
    b_cnt8 = sbtS("b_cnt8", [NE, 1], F32)
    b_cntrow = sbtS("b_cntrow", [1, NE], F32)
    b_cntall = sbtS("b_cntall", [128, NE], F32)
    b_maskb = sbtS("b_maskb", [128, 128, NE], BF16)
    b_rtake = sbtS("b_rtake", [128, NE], F32)
    b_theta = sbtS("b_theta", [128, NE], F32)
    b_one = sbtS("b_one", [128, NE], I32)
    b_kself = sbtS("b_kself", [128, NE], F32)

    @block.sync
    def _(s):
        s.wait_ge(sCC.h, CC_ALL)
        s.wait_ge(dVC.h, CP_VT)
        for j in range(4):
            s.wait_ge(dXN[j].h, dXN[j].n)
        s.wait_ge(dKV[0].h, dKV[0].n)
        s.wait_ge(dKV[1].h, dKV[1].n)
        ins = s.dma_start(out=pg[:],
                          in_=probs_all.ap().rearrange("(p a) e -> p a e", p=128))
        dPG.inc(ins, 16)
    LD_PG = dPG.n

    @block.gpsimd
    def _(g):
        g.wait_ge(dPG.h, LD_PG)
        sPO.inc(g.engine_nop())
    PO_INIT = sPO.n

    @block.vector
    def _(v):
        v.wait_ge(dPG.h, LD_PG)
        sVE.inc(v.memset(b_lo[:], 0))
        sVE.inc(v.memset(b_kself[:], float(KSEL)))
    VE_INIT = sVE.n

    def count_round(op, thr_ap):
        po_c = sPO.n

        @block.vector
        def _(v):
            v.wait_ge(sPO.h, po_c)
            v.wait_ge(sVE.h, sVE.n)
            sVE.inc(v.tensor_tensor(
                b_maskb[:], pg[:],
                thr_ap.unsqueeze(1).to_broadcast([128, 128, NE]), op))
            v.wait_ge(sVE.h, sVE.n)
            with nc.allow_low_precision(reason="counts <=128 are bf16-exact"):
                sVE.inc(v.tensor_reduce(b_cntb[:], b_maskb[:].rearrange("p a e -> p e a"),
                                        axis=AX.X, op=ALU.add))
        ve_c = sVE.n

        @block.tensor
        def _(t):
            t.wait_ge(sVE.h, ve_c)
            sPE.inc(t.matmul(pA[0:8, 0:1], b_cntb[:], onescol[:], start=True, stop=True))
        pe1 = sPE.n

        @block.scalar
        def _(a):
            a.wait_ge(sPE.h, pe1)
            sACT.inc(a.activation(b_cnt8[:], pA[0:8, 0:1], AF.Copy))
        ac1 = sACT.n

        @block.tensor
        def _(t):
            t.wait_ge(sACT.h, ac1)
            sPE.inc(t.transpose(pA[0:1, 8:16], b_cnt8[:], ident_f[0:8, 0:8]))
        pe2 = sPE.n

        @block.scalar
        def _(a):
            a.wait_ge(sPE.h, pe2)
            sACT.inc(a.activation(b_cntrow[:], pA[0:1, 8:16], AF.Copy))
        ac2 = sACT.n

        @block.tensor
        def _(t):
            t.wait_ge(sACT.h, ac2)
            sPE.inc(t.matmul(pB[:, 0:8], onesrow_f[:], b_cntrow[:], start=True, stop=True))
        pe3 = sPE.n

        @block.scalar
        def _(a):
            a.wait_ge(sPE.h, pe3)
            sACT.inc(a.activation(b_cntall[:], pB[:, 0:8], AF.Copy))
        return sACT.n

    for r_ in range(30):
        kbit = 29 - r_
        @block.vector
        def _(v, kbit=kbit, first=(r_ == 0)):
            if first:
                v.wait_ge(sPO.h, PO_INIT)
            v.wait_ge(sVE.h, sVE.n)
            sVE.inc(v.memset(b_t1[:], 1 << kbit))
            v.wait_ge(sVE.h, sVE.n)
            sVE.inc(v.tensor_tensor(b_mid[:], b_lo[:], b_t1[:], ALU.bitwise_or))
        actn = count_round(ALU.is_ge, b_mid[:].bitcast(F32))
        @block.vector
        def _(v, actn=actn):
            v.wait_ge(sACT.h, actn)
            v.wait_ge(sVE.h, sVE.n)
            sVE.inc(v.tensor_tensor(b_ge[:], b_cntall[:], b_kself[:], ALU.is_ge))
            v.wait_ge(sVE.h, sVE.n)
            sVE.inc(v.tensor_copy(b_gei[:], b_ge[:]))
            v.wait_ge(sVE.h, sVE.n)
            sVE.inc(v.select(b_lo[:], b_gei[:], b_mid[:], b_lo[:], add_drain=True))

    @block.vector
    def _(v):
        v.wait_ge(sVE.h, sVE.n)
        sVE.inc(v.tensor_copy(b_theta[:], b_lo[:].bitcast(F32)))
    actn = count_round(ALU.is_gt, b_theta[:])
    @block.vector
    def _(v, actn=actn):
        v.wait_ge(sACT.h, actn)
        v.wait_ge(sVE.h, sVE.n)
        sVE.inc(v.tensor_tensor(b_rtake[:], b_kself[:], b_cntall[:], ALU.subtract))
    PO_RT = sPO.n
    VE_RT = sVE.n

    g_gt = sbtS("g_gt", [128, 128, NE], BF16)
    g_tie = sbtS("g_tie", [128, 128, NE], BF16)
    g_scan = sbtS("g_scan", [128, 128, NE], F32)
    g_cnt = sbtS("g_cnt", [128, NE], BF16)
    g_base = sbtS("g_base", [128, NE], F32)
    g_sel = sbtS("g_sel", [128, 128, NE], F32)

    @block.vector
    def _(v):
        v.wait_ge(sVE.h, VE_RT)
        v.wait_ge(sVE.h, sVE.n)
        th = b_theta[:].unsqueeze(1).to_broadcast([128, 128, NE])
        sVE.inc(v.tensor_tensor(g_gt[:], pg[:], th, ALU.is_gt))
        sVE.inc(v.tensor_tensor(g_tie[:], pg[:], th, ALU.is_equal))
    PO_TIE = sPO.n
    VE_TIE = sVE.n

    @block.vector
    def _(v):
        v.wait_ge(sVE.h, VE_TIE)
        with nc.allow_low_precision(reason="counts <=128 are bf16-exact"):
            sVE.inc(v.tensor_reduce(g_cnt[:], g_tie[:].rearrange("p a e -> p e a"),
                                    axis=AX.X, op=ALU.add))
    VE_GCNT = sVE.n

    @block.tensor
    def _(t):
        t.wait_ge(sVE.h, VE_GCNT)
        sPE.inc(t.matmul(pA[:, 0:8], lstrict[:], g_cnt[:], start=True, stop=True))
    PE_TP = sPE.n

    @block.scalar
    def _(a):
        a.wait_ge(sPE.h, PE_TP)
        sACT.inc(a.activation(g_base[:], pA[:, 0:8], AF.Copy))
    ACT_TB = sACT.n

    @block.vector
    def _(v):
        v.wait_ge(sVE.h, VE_TIE)
        for e in range(NE):
            ins = v.tensor_tensor_scan(
                g_scan[:].rearrange("p a e -> p e a")[:, e, :],
                g_tie[:].rearrange("p a e -> p e a")[:, e, :],
                g_tie[:].rearrange("p a e -> p e a")[:, e, :],
                0.0, ALU.add, ALU.bypass)
        sVE.inc(ins)
        v.wait_ge(sVE.h, sVE.n)
        sVE.inc(v.tensor_sub(g_scan[:], g_scan[:], g_tie[:]))
        v.wait_ge(sVE.h, sVE.n)
        v.wait_ge(sACT.h, ACT_TB)
        for e in range(NE):
            ins = v.tensor_scalar(g_scan[:].rearrange("p a e -> p e a")[:, e, :],
                                  g_scan[:].rearrange("p a e -> p e a")[:, e, :],
                                  g_base[:, e:e+1], None, ALU.add)
        sVE.inc(ins)
        v.wait_ge(sVE.h, sVE.n)
        for e in range(NE):
            ins = v.tensor_scalar(g_scan[:].rearrange("p a e -> p e a")[:, e, :],
                                  g_scan[:].rearrange("p a e -> p e a")[:, e, :],
                                  b_rtake[:, e:e+1], None, ALU.is_lt)
        sVE.inc(ins)
        v.wait_ge(sVE.h, sVE.n)
        sVE.inc(v.tensor_mul(g_scan[:], g_scan[:], g_tie[:]))
        v.wait_ge(sVE.h, sVE.n)
        sVE.inc(v.tensor_add(g_sel[:], g_scan[:], g_gt[:]))
    VE_SEL = sVE.n

    @block.sync
    def _(s):
        s.wait_ge(sVE.h, VE_SEL)
        ins = s.dma_start(out=sel_dram.ap().rearrange("(p a) e -> p a e", p=128),
                          in_=g_sel[:])
        dSEL.inc(ins, 16)
    ST_SEL = dSEL.n



    selL = sbtS("selL", [128, 16, NE], F32)
    slot2 = sbtS("slot2", [128, 16, NE], F32)
    lcnt = sbtS("lcnt", [128, NE], BF16)
    lbase = sbtS("lbase", [128, NE], F32)

    @block.gpsimd
    def _(g):
        g.wait_ge(dSEL.h, ST_SEL)
        for a_ in range(16):
            ins = g.indirect_dma_start(
                out=selL[:, a_, :], out_offset=None,
                in_=sel_dram.ap(),
                in_offset=bass.IndirectOffsetOnAxis(ap=selrows[:, a_:a_+1], axis=0))
            dSL.inc(ins, 16)
    LD_SELL_G = dSL.n

    @block.vector
    def _(v):
        v.wait_ge(dSL.h, LD_SELL_G)
        with nc.allow_low_precision(reason="counts <=16 are bf16-exact"):
            sVE.inc(v.tensor_reduce(lcnt[:], selL[:].rearrange("p a e -> p e a"),
                                    axis=AX.X, op=ALU.add))
    VE_LC = sVE.n

    @block.tensor
    def _(t):
        t.wait_ge(sVE.h, VE_LC)
        sPE.inc(t.matmul(pA[:, 0:8], lstrict[:], lcnt[:], start=True, stop=True))
    PE_LB = sPE.n

    @block.scalar
    def _(a):
        a.wait_ge(sPE.h, PE_LB)
        sACT.inc(a.activation(lbase[:], pA[:, 0:8], AF.Copy))
    ACT_LB = sACT.n

    @block.vector
    def _(v):
        v.wait_ge(dSL.h, LD_SELL_G)
        for e in range(NE):
            ins = v.tensor_tensor_scan(
                slot2[:].rearrange("p a e -> p e a")[:, e, :],
                selL[:].rearrange("p a e -> p e a")[:, e, :],
                selL[:].rearrange("p a e -> p e a")[:, e, :],
                0.0, ALU.add, ALU.bypass)
        sVE.inc(ins)
        v.wait_ge(sVE.h, sVE.n)
        sVE.inc(v.tensor_sub(slot2[:], slot2[:], selL[:]))
        v.wait_ge(sVE.h, sVE.n)
        v.wait_ge(sACT.h, ACT_LB)
        for e in range(NE):
            ins = v.tensor_scalar(slot2[:].rearrange("p a e -> p e a")[:, e, :],
                                  slot2[:].rearrange("p a e -> p e a")[:, e, :],
                                  lbase[:, e:e+1], None, ALU.add)
        sVE.inc(ins)
        v.wait_ge(sVE.h, sVE.n)
        sVE.inc(v.tensor_scalar(g_scan[:, 0:16, :], slot2[:], float(CAP), None, ALU.is_lt))
        v.wait_ge(sVE.h, sVE.n)
        sVE.inc(v.tensor_mul(selL[:], selL[:], g_scan[:, 0:16, :]))
        v.wait_ge(sVE.h, sVE.n)
        sVE.inc(v.tensor_scalar(slot2[:], slot2[:], 9999.0, None, ALU.subtract))
        v.wait_ge(sVE.h, sVE.n)
        sVE.inc(v.tensor_mul(slot2[:], slot2[:], selL[:]))
        v.wait_ge(sVE.h, sVE.n)
        sVE.inc(v.tensor_scalar(slot2[:], slot2[:], 9999.0, None, ALU.add))
    VE_SLOT = sVE.n

    oh = sbtS("oh", [128, 16, CAP], BF16)
    rhsmm = sbtS("rhsmm", [128, 16, 3], BF16)
    pgrow = sbtS("pgrow", [128, NE], F32)
    idxf = sbtS("idxf", [128, 1], F32)
    idx3 = sbtS("idx3", [128, 3], F32)

    @block.vector
    def _(v):
        v.wait_ge(sVE.h, VE_SLOT)
        for a_ in range(16):
            v.tensor_copy(rhsmm[:, a_, 0:1], tloc_p[:])
            v.memset(rhsmm[:, a_, 1:2], float(a_))
        ins = v.memset(rhsmm[:, :, 2:3], 1.0)
        sVE.inc(ins)
    VE_RHS = sVE.n

    for e in range(NE):
        ve_oh0 = sVE.n
        @block.vector
        def _(v, e=e):
            if e == 0:
                v.wait_ge(sVE.h, VE_RHS)
            else:
                v.wait_ge(sPE.h, sPE.n)
            for a_ in range(16):
                ins = v.tensor_scalar(oh[:, a_, :], iota_slot[:], slot2[:, a_, e:e+1],
                                      None, ALU.is_equal)
            sVE.inc(ins)
        ve_oh = sVE.n

        for sc in range(3):
            @block.tensor
            def _(t, e=e, sc=sc, ve_oh=ve_oh):
                t.wait_ge(sVE.h, ve_oh)
                if sc > 0:
                    t.wait_ge(sVE.h, sVE.n)
                for a_ in range(16):
                    ins = t.matmul(pC[:, 0:3], oh[:, a_, sc*128:(sc+1)*128],
                                   rhsmm[:, a_, :], start=(a_ == 0), stop=(a_ == 15))
                sPE.inc(ins)
            pen = sPE.n

            @block.vector
            def _(v, e=e, sc=sc, pen=pen):
                v.wait_ge(sPE.h, pen)
                sVE.inc(v.tensor_copy(idx3[:], pC[:, 0:3]))
                v.wait_ge(sVE.h, sVE.n)
                sVE.inc(v.scalar_tensor_tensor(idxf[:], idx3[:, 1:2], 128.0, idx3[:, 0:1],
                                               ALU.mult, ALU.add))
                v.wait_ge(sVE.h, sVE.n)
                sVE.inc(v.tensor_copy(idxi[:, sc, e:e+1], idxf[:]))
                sVE.inc(v.tensor_copy(gval[:, sc, e:e+1], idx3[:, 2:3]))
    VE_IDX = sVE.n
    PE_IDX = sPE.n

    for e in range(NE):
        for sc in range(3):
            @block.gpsimd
            def _(g, e=e, sc=sc):
                if e == 0 and sc == 0:
                    g.wait_ge(sVE.h, VE_IDX)
                g.wait_ge(sVE.h, sVE.n)
                ins = g.indirect_dma_start(
                    out=pgrow[:], out_offset=None,
                    in_=probs_dram.ap(),
                    in_offset=bass.IndirectOffsetOnAxis(ap=idxi[:, sc, e:e+1], axis=0))
                dGG.inc(ins, 16)
            dgn = dGG.n

            @block.vector
            def _(v, e=e, sc=sc, dgn=dgn):
                v.wait_ge(dGG.h, dgn)
                sVE.inc(v.tensor_mul(gval[:, sc, e:e+1], gval[:, sc, e:e+1],
                                     pgrow[:, e:e+1]))
    VE_GV = sVE.n
    DG_SEL_END = dGG.n

    if "sel" in dbg_names:
        od = nc.dram_tensor("dbg_sel", [128, 16, NE], F32, kind="ExternalOutput").ap()
        @block.scalar
        def _(a):
            a.wait_ge(sVE.h, VE_GV)
            dA.inc(a.dma_start(out=od, in_=selL[:]), 16)
        od2 = nc.dram_tensor("dbg_slot", [128, 16, NE], F32, kind="ExternalOutput").ap()
        @block.scalar
        def _(a):
            a.wait_ge(sVE.h, VE_GV)
            dA.inc(a.dma_start(out=od2, in_=slot2[:]), 16)
        od3 = nc.dram_tensor("dbg_idx", [128, 3, NE], I32, kind="ExternalOutput").ap()
        @block.scalar
        def _(a):
            a.wait_ge(sVE.h, VE_GV)
            dA.inc(a.dma_start(out=od3, in_=idxi[:]), 16)
        od4 = nc.dram_tensor("dbg_gval", [128, 3, NE], F32, kind="ExternalOutput").ap()
        @block.scalar
        def _(a):
            a.wait_ge(sVE.h, VE_GV)
            dA.inc(a.dma_start(out=od4, in_=gval[:]), 16)

    if "probs" in dbg_names:
        od = nc.dram_tensor("dbg_probs", [128, 16, NE], F32, kind="ExternalOutput").ap()
        @block.scalar
        def _(a):
            a.wait_ge(sVE.h, VE_PROB)
            dA.inc(a.dma_start(out=od, in_=ptok[:]), 16)
    if "qt" in dbg_names:
        od = nc.dram_tensor("dbg_qt", [128, 8, TLOC], BF16, kind="ExternalOutput").ap()
        @block.scalar
        def _(a):
            a.wait_ge(sVE.h, VE_QT)
            dA.inc(a.dma_start(out=od, in_=qt[:]), 16)
    if "theta" in dbg_names:
        od = nc.dram_tensor("dbg_theta", [128, NE], F32, kind="ExternalOutput").ap()
        @block.scalar
        def _(a):
            a.wait_ge(sVE.h, VE_RT)
            dA.inc(a.dma_start(out=od, in_=b_theta[:]), 16)
        od2 = nc.dram_tensor("dbg_rtake", [128, NE], F32, kind="ExternalOutput").ap()
        @block.scalar
        def _(a):
            a.wait_ge(sVE.h, VE_RT)
            dA.inc(a.dma_start(out=od2, in_=b_rtake[:]), 16)

    if upto <= 6:
        @block.scalar
        def _(a):
            a.wait_ge(dA.h, dA.n)
            a.wait_ge(dVC.h, dVC.n)
            a.wait_ge(dGG.h, dGG.n)
            a.wait_ge(sCC.h, sCC.n)
            a.wait_ge(sVE.h, sVE.n)
        ctxS.close()
        ctxQ.close()
        ctx.close()
        return nc

    ctxS.close()

    ctxB = contextlib.ExitStack()
    sbtB = lambda name, shape, dt: ctxB.enter_context(nc.sbuf_tensor(name, shape, dt))
    ktb = [sbtB("ktb0", [128, S], BF16), sbtB("ktb1", [128, S], BF16)]
    vtb = [sbtB("vtb0", [128, 64, 130], BF16), sbtB("vtb1", [128, 64, 130], BF16)]
    ebuf = [sbtB("ebuf0", [128, 1024], BF16), sbtB("ebuf1", [128, 1024], BF16)]
    vtail = sbtB("vtail", [128, VT_BLKS, 128], BF16)
    bfp = sbtB("bfp", [128, 32 * 128], F32)
    btl = [sbtB("btl0", [128, 32, 128], BF16), sbtB("btl1", [128, 32, 128], BF16)]
    zrow = sbtB("zrow", [1, 1024], F32)
    pvs = sbtB("pvs", [64, 1024], BF16)
    rz = sbtB("rz", [128, 8], F32)
    ttmp = sbtB("ttmp", [128, 128], F32)

    @block.vector
    def _(v):
        v.wait_ge(sVE.h, sVE.n)
        v.memset(attn_tok[:], 0.0)
        for bi in range(2):
            v.memset(vtb[bi][:, :, 64:65], 1.0)
            ins = v.memset(vtb[bi][:, :, 129:130], 1.0)
        sVE.inc(ins)
    VE_ONES = sVE.n

    kt_ready = []
    bt_ready = []
    vtl_ready = []
    hp_pe_end = []
    exp_base = sACT.n
    qk_base = sPE.n

    exp_n = [0]
    pe_n = lambda: sPE.n
    tail_ve_done = [sVE.n]

    def emit_loads(hp, wait_pe):
        bi = hp % 2
        @block.sync
        def _(s, hp=hp, bi=bi, wait_pe=wait_pe):
            s.wait_ge(sCC.h, CC_KV)
            s.wait_ge(dZ.h, ST_ZE)
            s.wait_ge(sVE.h, VE_GV)
            s.wait_ge(sPE.h, PE_IDX)
            s.wait_ge(dGG.h, DG_SEL_END)
            s.wait_ge(dSEL.h, ST_SEL)
            s.wait_ge(dPG.h, LD_PG)
            if wait_pe is not None:
                s.wait_ge(sPE.h, wait_pe)
            ins = s.dma_start_transpose(ktb[bi][:], k_full.ap()[:, hp*128:(hp+1)*128])
            dKT[bi].inc(ins, 16)
            ins = s.dma_start(
                out=vtb[bi][:, :, 0:64],
                in_=v_pad.ap()[0:S, hp*128:hp*128+64]
                    .rearrange("(kc p) d -> p kc d", p=128))
            dKT[bi].inc(ins, 16)
            ins = s.dma_start(
                out=vtb[bi][:, :, 65:129],
                in_=v_pad.ap()[0:S, hp*128+64:hp*128+128]
                    .rearrange("(kc p) d -> p kc d", p=128))
            dKT[bi].inc(ins, 16)
        kt_ready.append(dKT[bi].n)

    def emit_bt(hp, wait_pe_tail):
        for hh in range(2):
            ds_pre = dS.n
            @block.sync
            def _(s, hp=hp, hh=hh, wait_pe_tail=wait_pe_tail):
                if hh == 0 and wait_pe_tail is not None:
                    s.wait_ge(sPE.h, wait_pe_tail)
                if hh == 1:
                    s.wait_ge(sVE.h, sVE.n)
                h = hp * 2 + hh
                ins = s.dma_start(
                    out=bfp[:],
                    in_=btail_in[h].rearrange("m u r -> u m r"))
                dBF.inc(ins, 16)
            dsn = dBF.n
            @block.vector
            def _(v, hh=hh, dsn=dsn):
                v.wait_ge(dBF.h, dsn)
                sVE.inc(v.tensor_copy(btl[hh][:], bfp[:].rearrange("p (m r) -> p m r", r=128)))
        @block.sync
        def _(s, hp=hp):
            s.wait_ge(dVC.h, CP_VT)
            ins = s.dma_start(
                out=vtail[:],
                in_=vt_dram.ap().rearrange("(w p) d -> p w d", p=128)
                    [:, :, hp*128:(hp+1)*128])
            dVT.inc(ins, 16)
        vtl_ready.append(dVT.n)
        bt_ready.append(sVE.n)

    emit_loads(0, None)
    emit_loads(1, None)

    PE_ATT0 = sPE.n
    carry_rz_ve = None
    carry_pvcopy_ve = None
    for hp in range(nhp):
        bi = hp % 2
        emit_bt(hp, hp_pe_end[hp - 1] if hp >= 1 else None)

        qb_rz_ve = carry_rz_ve
        qb_pvcopy_ve = carry_pvcopy_ve
        for qb in (range(4) if "qb" not in skip else range(0)):
            pe_qk_ids = []
            exp_ids = []
            pv_ids = []
            qk0 = sPE.n
            @block.tensor
            def _(t, hp=hp, qb=qb, bi=bi, qb_rz_ve=qb_rz_ve, qb_pvcopy_ve=qb_pvcopy_ve):
                t.wait_ge(dKT[bi].h, kt_ready[hp])
                t.wait_ge(sVE.h, VE_ONES)
                if hp == 0 and qb == 0:
                    pass
                for kc in range(nkc):
                    sb_ = pA if kc % 2 == 0 else pB
                    if kc >= 2:
                        t.wait_ge(sACT.h, exp_base + exp_n[0] + kc - 1)
                    elif qb_rz_ve is not None and kc == 0:
                        t.wait_ge(sVE.h, qb_rz_ve)
                    t.matmul(sb_[:, 0:512], ktb[bi][0:64, kc*128:(kc+1)*128],
                             qt[0:64, hp, qb*512:(qb+1)*512],
                             start=True, stop=True, tile_position=(0, 0))
                    ins = t.matmul(sb_[:, 512:1024], ktb[bi][64:128, kc*128:(kc+1)*128],
                                   qt[64:128, hp, qb*512:(qb+1)*512],
                                   start=True, stop=True, tile_position=(64, 0))
                    sPE.inc(ins)
                    if kc >= 1 and "pv" not in skip:
                        t.wait_ge(sACT.h, exp_base + exp_n[0] + kc)
                        eb = ebuf[(kc - 1) % 2]
                        if kc == 1 and qb_pvcopy_ve is not None:
                            t.wait_ge(sVE.h, qb_pvcopy_ve)
                        t.matmul(pC[0:65, :], vtb[bi][:, kc-1, 0:65], eb[:, 0:512],
                                 start=(kc == 1), stop=False)
                        ins = t.matmul(pD[0:65, :], vtb[bi][:, kc-1, 65:130], eb[:, 512:1024],
                                       start=(kc == 1), stop=False)
                        sPE.inc(ins)
                if "pv" in skip:
                    return
                t.wait_ge(sACT.h, exp_base + exp_n[0] + nkc)
                eb = ebuf[(nkc - 1) % 2]
                t.matmul(pC[0:65, :], vtb[bi][:, nkc - 1, 0:65], eb[:, 0:512],
                         start=False, stop=True)
                ins = t.matmul(pD[0:65, :], vtb[bi][:, nkc - 1, 65:130], eb[:, 512:1024],
                               start=False, stop=True)
                sPE.inc(ins)
            pe_qb_end = sPE.n

            @block.scalar
            def _(a, hp=hp, qb=qb, qk0=qk0):
                for kc in range(nkc):
                    sb_ = pA if kc % 2 == 0 else pB
                    a.wait_ge(sPE.h, qk0 + (2 * kc + 1 if "pv" not in skip else kc + 1))
                    ins = a.activation(ebuf[kc % 2][:], sb_[:], AF.Exp, scale=1.0 / 32.0)
                    sACT.inc(ins)
                    exp_n[0] += 1
            if "pv" in skip or "bnd" in skip:
                @block.vector
                def _(v, pe_qb_end=pe_qb_end):
                    v.wait_ge(sPE.h, pe_qb_end)
                    sVE.inc(v.tensor_copy(zrow[0:1, 0:512], pC[64:65, :] if "pv" not in skip else ebuf[0][0:1, 0:512]))
                qb_pvcopy_ve = sVE.n
                qb_rz_ve = sVE.n
                continue
            ve0 = sVE.n
            @block.vector
            def _(v, pe_qb_end=pe_qb_end):
                v.wait_ge(sPE.h, pe_qb_end)
                v.tensor_copy(zrow[0:1, 0:512], pC[64:65, :])
                sVE.inc(v.tensor_copy(zrow[0:1, 512:1024], pD[64:65, :]))
                v.tensor_copy(pvs[0:64, 0:512], pC[0:64, :])
                sVE.inc(v.tensor_copy(pvs[0:64, 512:1024], pD[0:64, :]))
            ve_zpv = sVE.n
            qb_pvcopy_ve = ve_zpv

            pe_z0 = sPE.n
            if "brz" in skip:
                @block.vector
                def _(v, ve_zpv=ve_zpv):
                    v.wait_ge(sVE.h, ve_zpv)
                    sVE.inc(v.memset(rz[:], 1.0))
                ve_rz = sVE.n
                qb_rz_ve = ve_rz
            else:
                @block.tensor
                def _(t, ve_zpv=ve_zpv):
                    t.wait_ge(sVE.h, ve_zpv)
                    for c in range(8):
                        ins = t.transpose(pA[:, 8+c:9+c],
                                          zrow[0:1, c*128:(c+1)*128], ident_f[0:1, 0:1])
                    sPE.inc(ins)
                pe_z = sPE.n

                @block.vector
                def _(v, pe_z=pe_z):
                    v.wait_ge(sPE.h, pe_z)
                    sVE.inc(v.reciprocal(rz[:], pA[:, 8:16]))
                ve_rz = sVE.n
                qb_rz_ve = ve_rz

            for sub in (range(4) if "bst" not in skip else range(0)):
                pe_s0 = sPE.n
                @block.tensor
                def _(t, sub=sub, ve_zpv=ve_zpv):
                    t.wait_ge(sVE.h, ve_zpv)
                    if sub >= 2:
                        t.wait_ge(sVE.h, sVE.n)
                    pb_ = pBF if sub % 2 == 0 else pBF2
                    t.transpose(pb_[:, 0:64], pvs[0:64, sub*128:(sub+1)*128],
                                ident_bf[0:64, 0:64])
                    ins = t.transpose(pb_[:, 64:128], pvs[0:64, 512+sub*128:512+(sub+1)*128],
                                      ident_bf[0:64, 0:64])
                    sPE.inc(ins)
                pe_sT = sPE.n

                @block.vector
                def _(v, hp=hp, qb=qb, sub=sub, pe_sT=pe_sT, ve_rz=ve_rz):
                    v.wait_ge(sPE.h, pe_sT)
                    v.wait_ge(sVE.h, ve_rz)
                    pb_ = pBF if sub % 2 == 0 else pBF2
                    at = qb * 4 + sub
                    v.scalar_tensor_tensor(
                        attn_tok[:, at, hp*128:hp*128+64], pb_[:, 0:64],
                        rz[:, sub:sub+1], attn_tok[:, at, hp*128:hp*128+64],
                        ALU.mult, ALU.add)
                    ins = v.scalar_tensor_tensor(
                        attn_tok[:, at, hp*128+64:hp*128+128], pb_[:, 64:128],
                        rz[:, 4+sub:5+sub], attn_tok[:, at, hp*128+64:hp*128+128],
                        ALU.mult, ALU.add)
                    sVE.inc(ins)

        @block.tensor
        def _(t, hp=hp):
            t.wait_ge(dVT.h, vtl_ready[hp])
            t.wait_ge(sVE.h, bt_ready[hp])
        for a_ in (range(16) if "tail" not in skip else range(0)):
            pe_t0 = sPE.n
            @block.tensor
            def _(t, hp=hp, a_=a_):
                if a_ >= 1:
                    t.wait_ge(sVE.h, sVE.n)
                nm = 32 - a_
                for m in range(nm):
                    t.matmul(pB[0:64, 0:128], vtail[:, a_+m, 0:64], btl[0][:, m, :],
                             start=(m == 0), stop=(m == nm - 1), tile_position=(0, 0))
                    ins = t.matmul(pB[64:128, 512:640], vtail[:, a_+m, 64:128], btl[1][:, m, :],
                                   start=(m == 0), stop=(m == nm - 1), tile_position=(0, 64))
                sPE.inc(ins)
            pe_tm = sPE.n

            @block.vector
            def _(v, pe_tm=pe_tm):
                v.wait_ge(sPE.h, pe_tm)
                v.tensor_copy(ttmp[0:64, :], pB[0:64, 0:128])
                sVE.inc(v.tensor_copy(ttmp[64:128, :], pB[64:128, 512:640]))
            ve_tc = sVE.n

            @block.tensor
            def _(t, ve_tc=ve_tc):
                t.wait_ge(sVE.h, ve_tc)
                sPE.inc(t.transpose(pA[:, 256:384], ttmp[:], ident_f[:]))
            pe_tt = sPE.n

            @block.vector
            def _(v, hp=hp, a_=a_, pe_tt=pe_tt):
                v.wait_ge(sPE.h, pe_tt)
                sVE.inc(v.tensor_add(attn_tok[:, a_, hp*128:(hp+1)*128],
                                     attn_tok[:, a_, hp*128:(hp+1)*128],
                                     pA[:, 256:384]))
        hp_pe_end.append(sPE.n)
        carry_rz_ve = sVE.n
        carry_pvcopy_ve = qb_pvcopy_ve
        if hp + 2 < 8:
            emit_loads(hp + 2, hp_pe_end[hp])
    ATT_VE_END = sVE.n
    ATT_PE_END = sPE.n

    if upto == 7:
        od7 = nc.dram_tensor("dbg_attn", [128, 16, D], BF16, kind="ExternalOutput").ap()
        @block.scalar
        def _(a):
            a.wait_ge(sVE.h, ATT_VE_END)
            a.wait_ge(sPE.h, ATT_PE_END)
            dA.inc(a.dma_start(out=od7, in_=attn_tok[:]), 16)
            a.wait_ge(dA.h, dA.n)
        ctxB.close()
        ctxQ.close()
        ctx.close()
        return nc

    ctxB.close()
    ctxQ.close()

    ctxC = contextlib.ExitStack()
    sbtC = lambda name, shape, dt: ctxC.enter_context(nc.sbuf_tensor(name, shape, dt))
    vps = sbtC("vps", [128, 4, D], BF16)
    bex_sb = sbtC("bex_sb", [16, D], BF16)

    @block.sync
    def _(s):
        s.wait_ge(sPE.h, ATT_PE_END)
        s.wait_ge(sVE.h, ATT_VE_END)
        for kb in range(4):
            ins = s.dma_start(
                out=vps[:, kb % 4, :],
                in_=v_pad.ap()[kb*128:(kb+1)*128, :])
            dPS[kb % 4].inc(ins, 16)
    PE_BEX0 = sPE.n
    for kb in range(64):
        if kb >= 4:
            @block.sync
            def _(s, kb=kb):
                s.wait_ge(sPE.h, PE_BEX0 + (kb - 4) + 1)
                ins = s.dma_start(out=vps[:, kb % 4, :],
                                  in_=v_pad.ap()[kb*128:(kb+1)*128, :])
                dPS[kb % 4].inc(ins, 16)
        @block.tensor
        def _(t, kb=kb, dw=dPS[kb % 4].n):
            t.wait_ge(dPS[kb % 4].h, dw)
            t.matmul(pA[0:16, 0:512], bexw[:, kb, :], vps[:, kb % 4, 0:512],
                     start=(kb == 0), stop=(kb == 63))
            ins = t.matmul(pA[0:16, 512:1024], bexw[:, kb, :], vps[:, kb % 4, 512:1024],
                           start=(kb == 0), stop=(kb == 63))
            sPE.inc(ins)
    PE_BEX = sPE.n

    @block.scalar
    def _(a):
        a.wait_ge(sPE.h, PE_BEX)
        sACT.inc(a.activation(bex_sb[:], pA[0:16, :], AF.Copy))
    ACT_BEX = sACT.n

    vps2 = sbtC("vps2", [128, 3, D], BF16)
    @block.sync
    def _(s):
        s.wait_ge(sACT.h, ACT_BEX)
        for i in range(3):
            ins = s.dma_start(out=vps2[:, i, :],
                              in_=vt_dram.ap()[i*128:(i+1)*128, :])
            dP2[i].inc(ins, 16)
    PE_PS0 = sPE.n
    VE_PS0 = sVE.n
    for a_ in range(16):
        if a_ >= 1 and a_ + 2 <= 17:
            @block.sync
            def _(s, a_=a_):
                s.wait_ge(sPE.h, PE_PS0 + a_)
                ins = s.dma_start(out=vps2[:, (a_+2) % 3, :],
                                  in_=vt_dram.ap()[(a_+2)*128:(a_+3)*128, :])
                dP2[(a_+2) % 3].inc(ins, 16)
        @block.tensor
        def _(t, a_=a_, dwa=dP2[a_ % 3].n, dwb=dP2[(a_+1) % 3].n):
            t.wait_ge(dP2[a_ % 3].h, dwa)
            t.wait_ge(dP2[(a_+1) % 3].h, dwb)
            if a_ >= 1:
                t.wait_ge(sVE.h, VE_PS0 + a_ * 4)
            for ob in range(2):
                t.matmul(pB[:, ob*512:(ob+1)*512], trishift[:],
                         vps2[:, a_ % 3, ob*512:(ob+1)*512], start=True, stop=False)
                t.matmul(pB[:, ob*512:(ob+1)*512], onehotD[:],
                         vps2[:, (a_+1) % 3, ob*512:(ob+1)*512], start=False, stop=False)
                ins = t.matmul(pB[:, ob*512:(ob+1)*512],
                               selm[:, a_*128:(a_+1)*128],
                               bex_sb[:, ob*512:(ob+1)*512], start=False, stop=True)
            sPE.inc(ins)
        pe_ps = sPE.n

        @block.vector
        def _(v, a_=a_, pe_ps=pe_ps):
            v.wait_ge(sPE.h, pe_ps)
            for ob in range(2):
                v.wait_ge(sVE.h, sVE.n)
                sVE.inc(v.tensor_mul(scr1k[:, ob*512:(ob+1)*512], pB[:, ob*512:(ob+1)*512],
                                     ctile[:, ob*512:(ob+1)*512]))
                v.wait_ge(sVE.h, sVE.n)
                ins = v.tensor_add(attn_tok[:, a_, ob*512:(ob+1)*512],
                                   attn_tok[:, a_, ob*512:(ob+1)*512],
                                   scr1k[:, ob*512:(ob+1)*512])
                sVE.inc(ins)
    VE_PS = sVE.n

    @block.vector
    def _(v):
        v.wait_ge(sVE.h, VE_PS)
        for i in range(16):
            ins = v.reduce_sum(stat_a[:, i:i+1], attn_tok[:, i, :], axis=AX.X, negate=True)
        sVE.inc(ins)
    VE_NS2 = sVE.n

    @block.scalar
    def _(a):
        a.wait_ge(sVE.h, VE_PS)
        for i in range(16):
            ins = a.activation(scr1k[:], attn_tok[:, i, :], AF.Square,
                               accum_out=stat_b[:, i:i+1])
            sACT.inc(ins)
            a.wait_ge(sACT.h, sACT.n)
    ACT_SQ2 = sACT.n

    @block.vector
    def _(v):
        v.wait_ge(sACT.h, ACT_SQ2)
        v.wait_ge(sVE.h, VE_NS2)
        sVE.inc(v.tensor_scalar(stat_c[:], stat_a[:], 1.0 / D, None, ALU.mult))
        v.wait_ge(sVE.h, sVE.n)
        sVE.inc(v.tensor_mul(stat_d[:], stat_c[:], stat_c[:]))
        v.wait_ge(sVE.h, sVE.n)
        sVE.inc(v.tensor_scalar(stat_a[:], stat_b[:], 1.0 / D, None, ALU.mult))
        v.wait_ge(sVE.h, sVE.n)
        sVE.inc(v.tensor_sub(stat_d[:], stat_a[:], stat_d[:]))
    VE_VAR2 = sVE.n

    @block.vector
    def _(v):
        v.wait_ge(sVE.h, VE_VAR2)
        sVE.inc(v.tensor_scalar(stat_d[:], stat_d[:], EPS, None, ALU.add))
    VE_VEPS2 = sVE.n

    @block.scalar
    def _(a):
        a.wait_ge(sVE.h, VE_VEPS2)
        sACT.inc(a.activation(stat_d[:], stat_d[:], AF.Ln))
        a.wait_ge(sACT.h, sACT.n)
        sACT.inc(a.activation(stat_d[:], stat_d[:], AF.Exp, scale=-0.5))
    ACT_RSTD2 = sACT.n

    @block.vector
    def _(v):
        v.wait_ge(sACT.h, ACT_RSTD2)
        sVE.inc(v.tensor_mul(stat_e[:], stat_c[:], stat_d[:]))
    VE_MRS2 = sVE.n

    for i in range(16):
        @block.scalar
        def _(a, i=i):
            a.wait_ge(sVE.h, VE_MRS2)
            if i > 0:
                a.wait_ge(sVE.h, sVE.n)
            sACT.inc(a.activation(scr1kb[:], attn_tok[:, i, :], AF.Identity,
                                  bias=stat_e[:, i:i+1], scale=stat_d[:, i:i+1]))
        acn = sACT.n

        @block.vector
        def _(v, i=i, acn=acn):
            v.wait_ge(sACT.h, acn)
            sVE.inc(v.tensor_mul(scr1k[:], scr1kb[:], gtile[:]))
            v.wait_ge(sVE.h, sVE.n)
            ins = v.tensor_add(attn_tok[:, i, :], scr1k[:], btile[:])
            sVE.inc(ins)
    VE_LN2 = sVE.n

    xnsel = sbtC("xnsel", [128, D], BF16)
    xnselT = sbtC("xnselT", [128, 8, CAP], BF16)
    midT = sbtC("midT", [128, NFC + 1, CAP], BF16)
    wupb = [sbtC("wupb0", [128, 8, 128], BF16), sbtC("wupb1", [128, 8, 128], BF16)]
    wdnb = [sbtC("wdnb0", [128, D], BF16), sbtC("wdnb1", [128, D], BF16)]
    bdnrow = sbtC("bdnrow", [1, D], BF16)
    yout = sbtC("yout", [128, D], F32)

    @block.vector
    def _(v):
        v.wait_ge(sVE.h, sVE.n)
        sVE.inc(v.memset(midT[0:1, NFC, :], 1.0))
    VE_MID1 = sVE.n

    @block.gpsimd
    def _(g):
        g.wait_ge(sPE.h, ATT_PE_END)
        g.wait_ge(sVE.h, ATT_VE_END)

    bupt = sbtC("bupt", [NFC, 128], F32)
    bupall = sbtC("bupall", [128, 4, NFC], F32)
    for gi in range(4):
        @block.gpsimd
        def _(g, gi=gi):
            g.wait_ge(dG.h, dG.n)
            if gi > 0:
                g.wait_ge(sPE.h, sPE.n)
            ins = g.dma_start(out=bupt[:],
                              in_=bup_in[gi:gi+1, :].rearrange("o (c p) -> (o c) p", p=128))
            dG.inc(ins, 16)
        dgn = dG.n

        @block.tensor
        def _(t, gi=gi, dgn=dgn):
            t.wait_ge(dG.h, dgn)
            if gi > 0:
                t.wait_ge(sVE.h, sVE.n)
            sPE.inc(t.transpose(pA[:, 0:NFC], bupt[:], ident_f[0:NFC, 0:NFC]))
        pen = sPE.n

        @block.vector
        def _(v, gi=gi, pen=pen):
            v.wait_ge(sPE.h, pen)
            sVE.inc(v.tensor_copy(bupall[:, gi, :], pA[:, 0:NFC]))
    VE_BUP = sVE.n

    for e in range(NE):
        ge_ = e // 2
        @block.gpsimd
        def _(g, e=e, ge_=ge_):
            if e > 0:
                g.wait_ge(sPE.h, sPE.n)
            ins = g.dma_start(out=bdnrow[:], in_=bdn_in[e:e+1, :])
            dBD.inc(ins, 16)
        dg_b = dBD.n

        for sc in range(3):
            @block.gpsimd
            def _(g, e=e, sc=sc):
                g.wait_ge(sVE.h, VE_GV)
                if e == 0 and sc == 0:
                    for j in range(4):
                        g.wait_ge(dXN[j].h, dXN[j].n)
                else:
                    g.wait_ge(sPE.h, sPE.n)
                ins = g.indirect_dma_start(
                    out=xnsel[:], out_offset=None,
                    in_=xn_dram.ap(),
                    in_offset=bass.IndirectOffsetOnAxis(ap=idxi[:, sc, e:e+1], axis=0))
                dXG.inc(ins, 16)
            dg_x = dXG.n

            @block.tensor
            def _(t, e=e, sc=sc, dg_x=dg_x):
                t.wait_ge(dXG.h, dg_x)
                if e == 0 and sc == 0:
                    t.wait_ge(sVE.h, VE_MID1)
                for c in range(8):
                    t.wait_ge(sVE.h, sVE.n)
                    buf = pBF if c % 2 == 0 else pBF2
                    ins = t.transpose(buf[:, 0:128], xnsel[:, c*128:(c+1)*128], ident_bf[:])
                    sPE.inc(ins)
                    @block.vector
                    def _(v, sc=sc, c=c, pen=sPE.n, buf=buf):
                        v.wait_ge(sPE.h, pen)
                        sVE.inc(v.tensor_copy(xnselT[:, c, sc*128:(sc+1)*128],
                                              buf[:, 0:128]))
        ve_xt = sVE.n

        for fc in range(NFC):
            wbu = wupb[fc % 2]
            @block.gpsimd
            def _(g, e=e, ge_=ge_, fc=fc, wbu=wbu):
                if fc >= 2:
                    g.wait_ge(sPE.h, sPE.n)
                ins = g.dma_start(
                    out=wbu[:],
                    in_=wup_in[ge_:ge_+1, :, fc*128:(fc+1)*128]
                        .rearrange("o (c p) f -> p (o c) f", p=128))
                dWU[fc % 2].inc(ins, 16)
            dg_w = dWU[fc % 2].n

            @block.tensor
            def _(t, e=e, fc=fc, wbu=wbu, dg_w=dg_w, ve_xt=ve_xt):
                t.wait_ge(dWU[fc % 2].h, dg_w)
                t.wait_ge(sVE.h, ve_xt)
                if fc >= 2:
                    t.wait_ge(sACT.h, sACT.n)
                buf = pC if fc % 2 == 0 else pD
                for k in range(8):
                    ins = t.matmul(buf[:, 0:CAP], wbu[:, k, :], xnselT[:, k, :],
                                   start=(k == 0), stop=(k == 7))
                sPE.inc(ins)
            pe_up = sPE.n

            @block.scalar
            def _(a, fc=fc, pe_up=pe_up, ge_=ge_):
                a.wait_ge(sPE.h, pe_up)
                buf = pC if fc % 2 == 0 else pD
                sACT.inc(a.activation(midT[:, fc, :], buf[:, 0:CAP], AF.Silu,
                                      bias=bupall[:, ge_, fc:fc+1]))
        act_up = sACT.n

        for sc in range(3):
            for ob in range(2):
                @block.tensor
                def _(t, e=e, sc=sc, ob=ob, act_up=act_up):
                    t.wait_ge(sACT.h, act_up)
                for fc in range(NFC):
                    wbd = wdnb[fc % 2]
                    @block.gpsimd
                    def _(g, e=e, fc=fc, wbd=wbd, sc=sc, ob=ob):
                        if fc >= 2 or (sc, ob) != (0, 0):
                            g.wait_ge(sPE.h, sPE.n)
                        ins = g.dma_start(
                            out=wbd[:],
                            in_=wdn_in.rearrange("e f h -> (e f) h")
                                [e*FPAD + fc*128: e*FPAD + (fc+1)*128, :])
                        dWD[fc % 2].inc(ins, 16)
                    dg_w = dWD[fc % 2].n

                    @block.tensor
                    def _(t, sc=sc, ob=ob, fc=fc, wbd=wbd, dg_w=dg_w):
                        t.wait_ge(dWD[fc % 2].h, dg_w)
                        if fc == 0:
                            t.wait_ge(sVE.h, sVE.n)
                        ins = t.matmul(pB[:, 0:512], midT[:, fc, sc*128:(sc+1)*128],
                                       wbd[:, ob*512:(ob+1)*512],
                                       start=(fc == 0), stop=False)
                        sPE.inc(ins)
                @block.tensor
                def _(t, sc=sc, ob=ob, dg_b=dg_b):
                    t.wait_ge(dBD.h, dg_b)
                    ins = t.matmul(pB[:, 0:512], midT[0:1, NFC, sc*128:(sc+1)*128],
                                   bdnrow[0:1, ob*512:(ob+1)*512],
                                   start=False, stop=True)
                    sPE.inc(ins)
                pe_dn = sPE.n

                @block.vector
                def _(v, e=e, sc=sc, ob=ob, pe_dn=pe_dn, dgn=dY.n):
                    v.wait_ge(sPE.h, pe_dn)
                    if ob == 0:
                        v.wait_ge(dY.h, dgn)
                    ins = v.tensor_scalar(yout[:, ob*512:(ob+1)*512], pB[:, 0:512],
                                          gval[:, sc, e:e+1], None, ALU.mult)
                    sVE.inc(ins)
            ve_y = sVE.n

            @block.gpsimd
            def _(g, e=e, sc=sc, ve_y=ve_y):
                g.wait_ge(sVE.h, ve_y)
                g.wait_ge(dZ.h, ST_ZE)
                g.wait_ge(dY.h, dY.n)
                ins = g.indirect_dma_start(
                    out=y_dram.ap(),
                    out_offset=bass.IndirectOffsetOnAxis(ap=idxi[:, sc, e:e+1], axis=0),
                    in_=yout[:], in_offset=None,
                    compute_op=ALU.add)
                dY.inc(ins, 16)
    DG_FFN = dY.n

    xch = sbtC("xch", [128, 4, D], F32)
    ych = sbtC("ych", [128, 4, D], F32)
    och = sbtC("och", [128, 4, D], F32)

    for ch in range(4):
        @block.sync
        def _(s, ch=ch):
            s.wait_ge(dY.h, DG_FFN)
            if ch > 0:
                s.wait_ge(dA.h, dA.n)
            ins = s.dma_start(out=xch[:],
                              in_=x_in.rearrange("(a p) d -> p a d", p=128)
                              [:, ch*4:(ch+1)*4, :])
            dFX.inc(ins, 16)
            ins = s.dma_start(out=ych[:],
                              in_=y_dram.ap().rearrange("(a p) d -> p a d", p=128)
                              [:, ch*4:(ch+1)*4, :])
            dFX.inc(ins, 16)
        ds_f = dFX.n

        @block.vector
        def _(v, ch=ch, ds_f=ds_f):
            v.wait_ge(dFX.h, ds_f)
            v.wait_ge(sVE.h, VE_LN2)
            for j in range(4):
                v.wait_ge(sVE.h, sVE.n)
                sVE.inc(v.tensor_add(och[:, j, :], xch[:, j, :], ych[:, j, :]))
                v.wait_ge(sVE.h, sVE.n)
                ins = v.tensor_add(och[:, j, :], och[:, j, :],
                                   attn_tok[:, ch*4+j, :])
                sVE.inc(ins)
        ve_f = sVE.n

        @block.scalar
        def _(a, ch=ch, ve_f=ve_f):
            a.wait_ge(sVE.h, ve_f)
            ins = a.dma_start(out=out.rearrange("(a p) d -> p a d", p=128)
                              [:, ch*4:(ch+1)*4, :],
                              in_=och[:])
            dA.inc(ins, 16)

    @block.scalar
    def _(a):
        a.wait_ge(dA.h, dA.n)
        a.wait_ge(dY.h, dY.n)
        a.wait_ge(sCC.h, sCC.n)

    ctxC.close()
    ctx.close()
    return nc


def _host_prep(inputs):
    x = np.ascontiguousarray(np.asarray(inputs["L0_attn_x"], dtype=np.float32).reshape(NT, D))
    Wq = np.asarray(inputs["L0_attn_Wq"], np.float32)
    Wk = np.asarray(inputs["L0_attn_Wk"], np.float32)
    Wv = np.asarray(inputs["L0_attn_Wv"], np.float32)
    T = np.asarray(inputs["L0_attn_rel_bias_table"], np.float32)
    g2 = np.asarray(inputs["L0_attn_ln_gamma"], np.float32)
    b2 = np.asarray(inputs["L0_attn_ln_beta"], np.float32)
    Wr = np.asarray(inputs["L0_ffn_Wr"], np.float32)
    brv = np.asarray(inputs["L0_ffn_br"], np.float32)
    W_up = np.asarray(inputs["L0_ffn_W_up"], np.float32)
    b_up = np.asarray(inputs["L0_ffn_b_up"], np.float32)
    W_down = np.asarray(inputs["L0_ffn_W_down"], np.float32)
    b_down = np.asarray(inputs["L0_ffn_b_down"], np.float32)

    wqt = np.ascontiguousarray(Wq.T)
    wkt = np.ascontiguousarray(Wk.T)
    wvt = np.ascontiguousarray(Wv.T)
    wup = np.zeros((4, D, FPAD), np.float32); wup[:, :, :2730] = W_up
    bup = np.zeros((4, FPAD), np.float32); bup[:, :2730] = b_up
    wdn = np.zeros((NE, FPAD, D), np.float32); wdn[:, :2730, :] = W_down
    uu_, rr_ = np.meshgrid(np.arange(128), np.arange(128), indexing="ij")
    btail = np.zeros((H, 32, 128, 128), np.float32)
    for m_ in range(32):
        idxm = 4095 - 128 * m_ - uu_ + rr_
        valid = (idxm >= 0) & (idxm <= 4093)
        idxc = np.clip(idxm, 0, 4093)
        btail[:, m_, :, :] = np.where(valid[None], T[idxc, :].transpose(2, 0, 1), 0.0)
    crow = np.ascontiguousarray(np.repeat(T[4094, :], HD)[None, :].astype(np.float32))
    ident = np.eye(128, dtype=np.float32)
    uu, rr = np.meshgrid(np.arange(128), np.arange(128), indexing="ij")
    trishift = (uu <= rr + 1).astype(np.float32)
    onehotD = np.zeros((128, 128), np.float32); onehotD[0, 127] = 1.0
    lstrict = (uu < rr).astype(np.float32)
    onescol = np.ones((128, 1), np.float32)
    iota_slot = np.broadcast_to(np.arange(CAP, dtype=np.float32), (128, CAP)).copy()
    tloc_p = np.arange(128, dtype=np.float32)[:, None].copy()
    selmv = np.zeros((16, 16 * 128), np.float32)
    for a in range(16):
        selmv[a, a*128:(a+1)*128] = 1.0

    in_maps = []
    for c in range(NCORES):
        cp = c % 4
        C0 = 16 * cp + 32
        bexw = np.zeros((64, 16), np.float32)
        for a in range(16):
            bexw[:, a] = (np.arange(64) < C0 + a).astype(np.float32)
        vtrows = ((C0 + np.arange(VT_BLKS, dtype=np.int32))[None, :] * 128
                  + np.arange(128, dtype=np.int32)[:, None]).astype(np.int32).copy()
        selrows = (2048 * c + np.arange(16, dtype=np.int32)[None, :] * 128
                   + np.arange(128, dtype=np.int32)[:, None]).astype(np.int32).copy()
        in_maps.append(dict(
            x_loc=x[TLOC*c:TLOC*(c+1)],
            wqt=wqt, wkt=wkt, wvt=wvt, wr=Wr, br=brv[:, None].copy(),
            wup=wup, bup=bup, wdn=wdn, bdn=b_down,
            btail=btail, crow=crow,
            grow=np.ascontiguousarray(g2[None, :]),
            brow=np.ascontiguousarray(b2[None, :]),
            ident=ident, trishift=trishift, onehotD=onehotD,
            lstrict=lstrict, onescol=onescol, iota_slot=iota_slot,
            tloc_p=tloc_p, bexw=bexw.reshape(1, -1).copy(), selm=selmv,
            vtrows=vtrows, selrows=selrows,
        ))
    return in_maps


def kernel(**inputs):
    in_maps = _host_prep(inputs)
    nc = build_program()
    res = run_bass_kernel_spmd(nc, in_maps, core_ids=list(range(NCORES)))
    shards = [np.asarray(res.results[c]["out"]) for c in range(NCORES)]
    return np.concatenate(shards, axis=0).reshape(2, S, D)


# version 06
# speedup vs baseline: 1.3428x; 1.3428x over previous
import contextlib
import numpy as np
import ml_dtypes
import concourse.bass as bass
import concourse.mybir as mybir
from concourse.bass_utils import run_bass_kernel_spmd

F32 = mybir.dt.float32
BF16 = mybir.dt.bfloat16
I32 = mybir.dt.int32
AF = mybir.ActivationFunctionType
ALU = mybir.AluOpType
AX = mybir.AxisListType

NCORES = 8
S = 8192
D = 1024
H = 16
HD = 64
NT = 16384
TLOC = 2048
CAP = 384
NE = 8
FPAD = 2816
NFC = FPAD // 128
VPAD = 16384
EPS = 1e-5
NBIS = 31
KSEL = 2048
VT_BLKS = 32


def build_program(dbg_names=(), upto=99, skip=(), nhp=8, nkc=64):
    nc = bass.Bass("TRN2", target_bir_lowering=False)
    ctx = contextlib.ExitStack()
    dbg_names = set(dbg_names)
    dbg_tensors = {}

    def din(name, shape, dt=F32):
        return nc.dram_tensor(name, shape, dt, kind="ExternalInput").ap()

    x_in = din("x_loc", [TLOC, D])
    wqt_in = din("wqt", [D, D])
    wkt_in = din("wkt", [D, D])
    wvt_in = din("wvt", [D, D])
    wr_in = din("wr", [D, NE])
    br_in = din("br", [NE, 1])
    wup_in = din("wup", [4, D, FPAD])
    bup_in = din("bup", [4, FPAD])
    wdn_in = din("wdn", [NE, FPAD, D])
    bdn_in = din("bdn", [NE, D])
    btail_in = din("btail", [H, 32, 128, 128])
    crow_in = din("crow", [1, D])
    grow_in = din("grow", [1, D])
    brow_in = din("brow", [1, D])
    ident_in = din("ident", [128, 128])
    trishift_in = din("trishift", [128, 128])
    onehotD_in = din("onehotD", [128, 128])
    lstrict_in = din("lstrict", [128, 128])
    onescol_in = din("onescol", [128, 1])
    iota_slot_in = din("iota_slot", [128, CAP])
    tloc_p_in = din("tloc_p", [128, 1])
    bexw_in = din("bexw", [1, 64 * 16])
    selm_in = din("selm", [16, 16 * 128])
    vtrows_in = din("vtrows", [128, VT_BLKS], I32)
    selrows_in = din("selrows", [128, 16], I32)

    out = nc.dram_tensor("out", [TLOC, D], F32, kind="ExternalOutput").ap()

    xn_dram = nc.dram_tensor("xn_dram", [TLOC, D], BF16)
    k_loc = nc.dram_tensor("k_loc", [TLOC, D], BF16)
    v_loc = nc.dram_tensor("v_loc", [TLOC, D], BF16)
    k_full = nc.dram_tensor("k_full", [S, D], BF16)
    v_pad = nc.dram_tensor("v_pad", [VPAD, D], BF16)
    vt_dram = nc.dram_tensor("vt_dram", [VT_BLKS * 128, D], BF16)
    probs_dram = nc.dram_tensor("probs_dram", [TLOC, NE], F32)
    probs_all = nc.dram_tensor("probs_all", [NT, NE], F32)
    sel_dram = nc.dram_tensor("sel_dram", [NT, NE], F32)
    selloc_dram = nc.dram_tensor("selloc_dram", [TLOC, NE], F32)
    y_dram = nc.dram_tensor("y_dram", [TLOC, D], F32)

    sbt = lambda name, shape, dt: ctx.enter_context(nc.sbuf_tensor(name, shape, dt))

    ident_bf = sbt("c_identb", [128, 128], BF16)
    ident_f = sbt("c_identf", [128, 128], F32)
    trishift = sbt("c_trish", [128, 128], BF16)
    onehotD = sbt("c_ohD", [128, 128], BF16)
    lstrict = sbt("c_lstr", [128, 128], BF16)
    onescol = sbt("c_ones", [128, 1], BF16)
    onesrow_f = sbt("c_onesrf", [1, 128], F32)
    iota_slot = sbt("c_iotas", [128, CAP], F32)
    tloc_p = sbt("c_tlocp", [128, 1], BF16)
    bexw = sbt("c_bexw", [128, 64, 16], BF16)
    selm = sbt("c_selm", [16, 16 * 128], BF16)
    vtrows = sbt("c_vtrows", [128, VT_BLKS], I32)
    selrows = sbt("c_selrows", [128, 16], I32)
    ctile = sbt("c_ctile", [128, D], F32)
    gtile = sbt("c_gtile", [128, D], F32)
    btile = sbt("c_btile", [128, D], F32)
    brcol = sbt("c_brcol", [NE, 1], F32)

    stat_a = sbt("stat_a", [128, 16], F32)
    stat_b = sbt("stat_b", [128, 16], F32)
    stat_c = sbt("stat_c", [128, 16], F32)
    stat_d = sbt("stat_d", [128, 16], F32)
    stat_e = sbt("stat_e", [128, 16], F32)
    scr1k = sbt("scr1k", [128, D], F32)
    scr1kb = sbt("scr1kb", [128, D], BF16)
    idxi = sbt("idxi", [128, 3, NE], I32)
    gval = sbt("gval", [128, 3, NE], F32)
    attn_tok = sbt("attn_tok", [128, 16, D], BF16)

    class Sem:
        __slots__ = ("h", "n")
        def __init__(self, name):
            self.h = ctx.enter_context(nc.semaphore(name))
            self.n = 0
        def inc(self, instr, k=1):
            instr.then_inc(self.h, k)
            self.n += k
            return self.n

    dS = Sem("dS")
    dG = Sem("dG")
    dA = Sem("dA")
    dC = Sem("dC")
    dX = [Sem("dX0"), Sem("dX1"), Sem("dX2"), Sem("dX3")]
    dXN = [Sem("dXN0"), Sem("dXN1"), Sem("dXN2"), Sem("dXN3")]
    dKV = [Sem("dKV0"), Sem("dKV1")]
    dPR = Sem("dPR")
    dZ = Sem("dZ")
    dPG = Sem("dPG")
    dSEL = Sem("dSEL")
    dKT = [Sem("dKT0"), Sem("dKT1")]
    dVT = Sem("dVT")
    dBF = Sem("dBF")
    dPS = [Sem("dPS0"), Sem("dPS1"), Sem("dPS2"), Sem("dPS3")]
    dP2 = [Sem("dP20"), Sem("dP21"), Sem("dP22")]
    dFX = Sem("dFX")
    dVC = Sem("dVC")
    dSL = Sem("dSL")
    dGG = Sem("dGG")
    dBD = Sem("dBD")
    dWU = [Sem("dWU0"), Sem("dWU1")]
    dWD = [Sem("dWD0"), Sem("dWD1")]
    dXG = Sem("dXG")
    dY = Sem("dY")
    sPE = Sem("sPE")
    sACT = Sem("sACT")
    sVE = Sem("sVE")
    sPO = Sem("sPO")
    sCC = Sem("sCC")

    pS = ctx.enter_context(nc.psum_tensor("pS", [128, 2048], F32))
    pA = pS[:, 0:1024]
    pB = pS[:, 1024:2048]
    pC = ctx.enter_context(nc.psum_tensor("pC", [128, 512], F32))
    pD = ctx.enter_context(nc.psum_tensor("pD", [128, 512], F32))
    pBF = ctx.enter_context(nc.psum_tensor("pBF", [128, 1024], BF16))
    pBF2 = ctx.enter_context(nc.psum_tensor("pBF2", [128, 1024], BF16))

    block = ctx.enter_context(nc.Block())

    ctxQ = contextlib.ExitStack()
    qt = ctxQ.enter_context(nc.sbuf_tensor("qt", [128, 8, TLOC], BF16))

    ctxP = contextlib.ExitStack()
    sbtP = lambda name, shape, dt: ctxP.enter_context(nc.sbuf_tensor(name, shape, dt))
    xrot = sbtP("xrot", [128, 4, D], F32)
    xnrot = sbtP("xnrot", [128, 4, D], BF16)
    xnT = sbtP("xnT", [128, 8, TLOC], BF16)
    wbuf = sbtP("wbuf", [128, 8, D], BF16)
    wr_sb = sbtP("wr_sb", [128, 8, NE], BF16)
    kv_ev = sbtP("kv_ev", [128, 2, 512], BF16)
    logit_sb = sbtP("logit_sb", [NE, TLOC], F32)
    ltok = sbtP("ltok", [128, 16, NE], F32)
    ptok = sbtP("ptok", [128, 16, NE], F32)
    lred = sbtP("lred", [128, 16], F32)
    zt = sbtP("zt", [128, 1024], BF16)
    zf32 = sbtP("zf32", [128, 1024], F32)

    @block.sync
    def _(s):
        dC.inc(s.dma_start(out=ident_f[:], in_=ident_in), 16)
        dC.inc(s.dma_start(out=onesrow_f[:],
                           in_=onescol_in.rearrange("p o -> o p")[0:1, 0:128]), 16)
        dC.inc(s.dma_start(out=iota_slot[:], in_=iota_slot_in), 16)
        dC.inc(s.dma_start(out=ctile[:], in_=crow_in.partition_broadcast(128)), 16)
        dC.inc(s.dma_start(out=gtile[:], in_=grow_in.partition_broadcast(128)), 16)
        dC.inc(s.dma_start(out=btile[:], in_=brow_in.partition_broadcast(128)), 16)
    LD_X = dC.n

    @block.gpsimd
    def _(g):
        dG.inc(g.dma_start(out=ident_bf[:], in_=ident_in), 16)
        dG.inc(g.dma_start(out=trishift[:], in_=trishift_in), 16)
        dG.inc(g.dma_start(out=onehotD[:], in_=onehotD_in), 16)
        dG.inc(g.dma_start(out=lstrict[:], in_=lstrict_in), 16)
        dG.inc(g.dma_start(out=onescol[:], in_=onescol_in), 16)
        dG.inc(g.dma_start(out=tloc_p[:], in_=tloc_p_in), 16)
        dG.inc(g.dma_start(out=bexw[:], in_=bexw_in.rearrange("o (k a) -> o k a", a=16)
                           .partition_broadcast(128)), 16)
        dG.inc(g.dma_start(out=selm[:], in_=selm_in), 16)
        dG.inc(g.dma_start(out=vtrows[:], in_=vtrows_in), 16)
        dG.inc(g.dma_start(out=selrows[:], in_=selrows_in), 16)
        dG.inc(g.dma_start(out=brcol[:], in_=br_in), 16)
        dG.inc(g.dma_start(out=wbuf[:], in_=wqt_in.rearrange("(c p) o -> p c o", p=128)), 16)
        dG.inc(g.dma_start(out=wr_sb[:], in_=wr_in.rearrange("(c p) o -> p c o", p=128)), 16)
    LD_W = dG.n

    xv = x_in.rearrange("(a p) d -> p a d", p=128)
    DS_L0 = dS.n
    VE_L0s = sVE.n
    ACT_L0 = sACT.n

    for i in range(16):
        @block.sync
        def _(s, i=i):
            if i == 0:
                s.wait_ge(dC.h, LD_X)
            if i >= 4:
                s.wait_ge(sVE.h, VE_L0s + (i - 3))
                s.wait_ge(sACT.h, ACT_L0 + (i - 3))
            ins = s.dma_start(out=xrot[:, i % 4, :], in_=xv[:, i, :])
            dX[i % 4].inc(ins, 16)
        dsn = dX[i % 4].n

        @block.vector
        def _(v, i=i, dsn=dsn):
            v.wait_ge(dX[i % 4].h, dsn)
            sVE.inc(v.reduce_sum(stat_a[:, i:i+1], xrot[:, i % 4, :],
                                 axis=AX.X, negate=True))

        @block.scalar
        def _(a, i=i, dsn=dsn):
            a.wait_ge(dX[i % 4].h, dsn)
            if i > 0:
                a.wait_ge(sACT.h, sACT.n)
            sACT.inc(a.activation(scr1k[:], xrot[:, i % 4, :], AF.Square,
                                  accum_out=stat_b[:, i:i+1]))
    VE_NS = sVE.n
    ACT_SQ = sACT.n

    @block.vector
    def _(v):
        v.wait_ge(sACT.h, ACT_SQ)
        v.wait_ge(sVE.h, VE_NS)
        sVE.inc(v.tensor_scalar(stat_c[:], stat_a[:], 1.0 / D, None, ALU.mult))
        v.wait_ge(sVE.h, sVE.n)
        sVE.inc(v.tensor_mul(stat_d[:], stat_c[:], stat_c[:]))
        v.wait_ge(sVE.h, sVE.n)
        sVE.inc(v.tensor_scalar(stat_a[:], stat_b[:], 1.0 / D, None, ALU.mult))
        v.wait_ge(sVE.h, sVE.n)
        sVE.inc(v.tensor_sub(stat_d[:], stat_a[:], stat_d[:]))
    VE_VAR = sVE.n

    @block.vector
    def _(v):
        v.wait_ge(sVE.h, VE_VAR)
        sVE.inc(v.tensor_scalar(stat_d[:], stat_d[:], EPS, None, ALU.add))
    VE_VEPS = sVE.n

    @block.scalar
    def _(a):
        a.wait_ge(sVE.h, VE_VEPS)
        sACT.inc(a.activation(stat_d[:], stat_d[:], AF.Ln))
        a.wait_ge(sACT.h, sACT.n)
        sACT.inc(a.activation(stat_d[:], stat_d[:], AF.Exp, scale=-0.5))
    ACT_RSTD = sACT.n

    @block.vector
    def _(v):
        v.wait_ge(sACT.h, ACT_RSTD)
        sVE.inc(v.tensor_mul(stat_e[:], stat_c[:], stat_d[:]))
    VE_MRS = sVE.n

    DS_L2 = dS.n
    ACT_X0 = sACT.n
    PE_T0 = sPE.n
    VE_T0 = sVE.n
    for i in range(16):
        @block.sync
        def _(s, i=i):
            if i < 4:
                s.wait_ge(sACT.h, ACT_SQ)
                s.wait_ge(sVE.h, VE_NS)
            else:
                s.wait_ge(sACT.h, ACT_X0 + (i - 3))
            ins = s.dma_start(out=xrot[:, i % 4, :], in_=xv[:, i, :])
            dX[i % 4].inc(ins, 16)
        dsn = dX[i % 4].n

        @block.scalar
        def _(a, i=i, dsn=dsn, dxn_w=(dXN[i % 4].n if i >= 4 else 0)):
            a.wait_ge(dX[i % 4].h, dsn)
            if i == 0:
                a.wait_ge(sVE.h, VE_MRS)
            if i >= 4:
                a.wait_ge(sPE.h, PE_T0 + (i - 3) * 8)
                a.wait_ge(dXN[i % 4].h, dxn_w)
            sACT.inc(a.activation(xnrot[:, i % 4, :], xrot[:, i % 4, :], AF.Identity,
                                  bias=stat_e[:, i:i+1], scale=stat_d[:, i:i+1]))
        acn = sACT.n

        @block.sync
        def _(s, i=i, acn=acn):
            s.wait_ge(sACT.h, acn)
            ins = s.dma_start(out=xn_dram.ap().rearrange("(a p) d -> p a d", p=128)[:, i, :],
                              in_=xnrot[:, i % 4, :])
            dXN[i % 4].inc(ins, 16)

        @block.tensor
        def _(t, i=i, acn=acn):
            t.wait_ge(sACT.h, acn)
            if i == 0:
                t.wait_ge(dG.h, LD_W)
            for c in range(8):
                k = i * 8 + c
                buf = pBF if k % 2 == 0 else pBF2
                if k >= 2:
                    t.wait_ge(sVE.h, VE_T0 + k - 1)
                sPE.inc(t.transpose(buf[:, 0:128], xnrot[:, i % 4, c*128:(c+1)*128],
                                    ident_bf[:]))

        @block.vector
        def _(v, i=i):
            for c in range(8):
                k = i * 8 + c
                buf = pBF if k % 2 == 0 else pBF2
                v.wait_ge(sPE.h, PE_T0 + k + 1)
                sVE.inc(v.tensor_copy(xnT[:, c, i*128:(i+1)*128], buf[:, 0:128]))
    PE_XNT, VE_XNT = sPE.n, sVE.n
    ST_XN = dS.n

    PE_P0, VE_P0 = sPE.n, sVE.n

    @block.tensor
    def _(t):
        t.wait_ge(sVE.h, VE_XNT)
        t.wait_ge(dG.h, LD_W)
        for n in range(32):
            dc, tb = divmod(n, 4)
            buf = pC if n % 2 == 0 else pD
            if n >= 2:
                t.wait_ge(sVE.h, VE_P0 + n - 1)
            for k in range(8):
                ins = t.matmul(buf[:], wbuf[:, k, dc*128:(dc+1)*128],
                               xnT[:, k, tb*512:(tb+1)*512],
                               start=(k == 0), stop=(k == 7))
            sPE.inc(ins)

    @block.vector
    def _(v):
        for n in range(32):
            dc, tb = divmod(n, 4)
            buf = pC if n % 2 == 0 else pD
            v.wait_ge(sPE.h, PE_P0 + n + 1)
            sVE.inc(v.tensor_copy(qt[:, dc, tb*512:(tb+1)*512], buf[:]))
    PE_QT, VE_QT = sPE.n, sVE.n

    @block.gpsimd
    def _(g):
        g.wait_ge(sPE.h, PE_QT)
        ins = g.dma_start(out=wbuf[:], in_=wkt_in.rearrange("(c p) o -> p c o", p=128))
        dG.inc(ins, 16)
    LD_WK = dG.n

    def kv_pass(tgt, ld_wait):
        pe0, ve0 = sPE.n, sVE.n
        dkv0 = [dKV[0].n, dKV[1].n]

        @block.tensor
        def _(t):
            t.wait_ge(dG.h, ld_wait)
            t.wait_ge(sVE.h, ve0)
            for n in range(32):
                tc, ob = divmod(n, 2)
                buf = pC if n % 2 == 0 else pD
                if n >= 2:
                    t.wait_ge(sVE.h, ve0 + n - 1)
                for k in range(8):
                    ins = t.matmul(buf[:], xnT[:, k, tc*128:(tc+1)*128],
                                   wbuf[:, k, ob*512:(ob+1)*512],
                                   start=(k == 0), stop=(k == 7))
                sPE.inc(ins)

        @block.vector
        def _(v):
            for n in range(32):
                buf = pC if n % 2 == 0 else pD
                v.wait_ge(sPE.h, pe0 + n + 1)
                if n >= 2:
                    v.wait_ge(dKV[n % 2].h, dkv0[n % 2] + (n - 2) // 2 * 16 + 16)
                else:
                    v.wait_ge(dKV[n % 2].h, dkv0[n % 2])
                sVE.inc(v.tensor_copy(kv_ev[:, n % 2, :], buf[:]))

        @block.sync
        def _(s):
            for n in range(32):
                tc, ob = divmod(n, 2)
                s.wait_ge(sVE.h, ve0 + n + 1)
                ins = s.dma_start(out=tgt.ap()[tc*128:(tc+1)*128, ob*512:(ob+1)*512],
                                  in_=kv_ev[:, n % 2, :])
                dKV[n % 2].inc(ins, 16)

    kv_pass(k_loc, LD_WK)
    PE_K = sPE.n

    @block.gpsimd
    def _(g):
        g.wait_ge(sPE.h, PE_K)
        ins = g.dma_start(out=wbuf[:], in_=wvt_in.rearrange("(c p) o -> p c o", p=128))
        dG.inc(ins, 16)
    LD_WV = dG.n

    kv_pass(v_loc, LD_WV)
    ST_KV0, ST_KV1 = dKV[0].n, dKV[1].n

    PE_R0, ACT_R0 = sPE.n, sACT.n

    @block.tensor
    def _(t):
        for tb in range(4):
            if tb >= 1:
                t.wait_ge(sACT.h, ACT_R0 + tb)
            for k in range(8):
                ins = t.matmul(pA[0:8, 0:512], wr_sb[:, k, :],
                               xnT[:, k, tb*512:(tb+1)*512],
                               start=(k == 0), stop=(k == 7))
            sPE.inc(ins)

    @block.scalar
    def _(a):
        for tb in range(4):
            a.wait_ge(sPE.h, PE_R0 + tb + 1)
            sACT.inc(a.activation(logit_sb[:, tb*512:(tb+1)*512], pA[0:8, 0:512],
                                  AF.Identity, bias=brcol[:]))
    ACT_RT = sACT.n

    PE_L0, VE_L0 = sPE.n, sVE.n

    @block.tensor
    def _(t):
        t.wait_ge(sACT.h, ACT_RT)
        for tc in range(16):
            if tc >= 1:
                t.wait_ge(sVE.h, VE_L0 + tc)
            sPE.inc(t.transpose(pB[:, 0:8], logit_sb[:, tc*128:(tc+1)*128],
                                ident_f[0:8, 0:8]))

    @block.vector
    def _(v):
        for tc in range(16):
            v.wait_ge(sPE.h, PE_L0 + tc + 1)
            sVE.inc(v.tensor_copy(ltok[:, tc, :], pB[:, 0:8]))
        v.wait_ge(sVE.h, sVE.n)
        for i in range(16):
            ins = v.reduce_max(lred[:, i:i+1], ltok[:, i, :], axis=AX.X, negate=True)
        sVE.inc(ins)
    VE_LT = sVE.n

    @block.scalar
    def _(a):
        a.wait_ge(sVE.h, VE_LT)
        for i in range(16):
            ins = a.activation(ptok[:, i, :], ltok[:, i, :], AF.Exp, bias=lred[:, i:i+1])
        sACT.inc(ins)
    ACT_PEXP = sACT.n

    @block.vector
    def _(v):
        v.wait_ge(sACT.h, ACT_PEXP)
        for i in range(16):
            ins = v.reduce_sum(lred[:, i:i+1], ptok[:, i, :], axis=AX.X)
        sVE.inc(ins)
        v.wait_ge(sVE.h, sVE.n)
        sVE.inc(v.reciprocal(lred[:], lred[:]))
        v.wait_ge(sVE.h, sVE.n)
        for i in range(16):
            ins = v.tensor_scalar(ptok[:, i, :], ptok[:, i, :], lred[:, i:i+1],
                                  None, ALU.mult)
        sVE.inc(ins)
    VE_PROB = sVE.n

    @block.sync
    def _(s):
        s.wait_ge(sVE.h, VE_PROB)
        ins = s.dma_start(out=probs_dram.ap().rearrange("(a p) e -> p a e", p=128),
                          in_=ptok[:])
        dPR.inc(ins, 16)
    ST_PROBS = dPR.n

    @block.gpsimd
    def _(g):
        g.wait_ge(dKV[0].h, ST_KV0)
        g.wait_ge(dKV[1].h, ST_KV1)
        sCC.inc(g.collective_compute(
            "AllGather", ALU.bypass,
            replica_groups=[[0, 1, 2, 3], [4, 5, 6, 7]],
            ins=[k_loc.ap().opt()], outs=[k_full.ap().opt()]))
        sCC.inc(g.collective_compute(
            "AllGather", ALU.bypass,
            replica_groups=[[0, 1, 2, 3], [4, 5, 6, 7]],
            ins=[v_loc.ap().opt()], outs=[v_pad.ap()[0:S, :].opt()]))
        g.wait_ge(dPR.h, ST_PROBS)
        sCC.inc(g.collective_compute(
            "AllGather", ALU.bypass,
            replica_groups=[list(range(8))],
            ins=[probs_dram.ap().opt()], outs=[probs_all.ap().opt()]))
    CC_KV = 2
    CC_ALL = sCC.n

    @block.vector
    def _(v):
        sVE.inc(v.memset(zt[:], 0.0))
        sVE.inc(v.memset(zf32[:], 0.0))
    VE_ZT = sVE.n

    @block.sync
    def _(s):
        s.wait_ge(sVE.h, VE_ZT)
        for blk in range(16):
            ins = s.dma_start(
                out=v_pad.ap()[S + blk*512:S + (blk+1)*512, :]
                    .rearrange("(n p) d -> p n d", p=128),
                in_=zt[:].unsqueeze(1).to_broadcast([128, 4, 1024]))
            dZ.inc(ins, 16)
        for blk in range(4):
            ins = s.dma_start(
                out=y_dram.ap()[blk*512:(blk+1)*512, :]
                    .rearrange("(n p) d -> p n d", p=128),
                in_=zf32[:].unsqueeze(1).to_broadcast([128, 4, 1024]))
            dZ.inc(ins, 16)
    ST_ZE = dZ.n

    vtcp = sbtP("vtcp", [128, 2, D], BF16)
    @block.gpsimd
    def _(g):
        g.wait_ge(sCC.h, CC_KV)
        g.wait_ge(dZ.h, ST_ZE)
        g.wait_ge(dG.h, dG.n)
        for b in range(VT_BLKS):
            ins = g.indirect_dma_start(
                out=vtcp[:, b % 2, :], out_offset=None,
                in_=v_pad.ap(),
                in_offset=bass.IndirectOffsetOnAxis(ap=vtrows[:, b:b+1], axis=0))
            dVC.inc(ins, 16)
            g.wait_ge(dVC.h, dVC.n)
            ins = g.dma_start(out=vt_dram.ap()[b*128:(b+1)*128, :],
                              in_=vtcp[:, b % 2, :])
            dVC.inc(ins, 16)
        g.wait_ge(dVC.h, dVC.n)
    CP_VT = dVC.n

    ctxP.close()
    ctxS = contextlib.ExitStack()
    sbtS = lambda name, shape, dt: ctxS.enter_context(nc.sbuf_tensor(name, shape, dt))
    pg = sbtS("pg", [128, 128, NE], F32)
    b_lo = sbtS("b_lo", [128, NE], I32)
    b_hi = sbtS("b_hi", [128, NE], I32)
    b_mid = sbtS("b_mid", [128, NE], I32)
    b_t1 = sbtS("b_t1", [128, NE], I32)
    b_t2 = sbtS("b_t2", [128, NE], I32)
    b_ge = sbtS("b_ge", [128, NE], F32)
    b_gei = sbtS("b_gei", [128, NE], I32)
    b_nge = sbtS("b_nge", [128, NE], I32)
    b_cntb = sbtS("b_cntb", [128, NE], BF16)
    b_cnt8 = sbtS("b_cnt8", [NE, 1], F32)
    b_cntrow = sbtS("b_cntrow", [1, NE], F32)
    b_cntall = sbtS("b_cntall", [128, NE], F32)
    b_maskb = sbtS("b_maskb", [128, 128, NE], BF16)
    b_rtake = sbtS("b_rtake", [128, NE], F32)
    b_theta = sbtS("b_theta", [128, NE], F32)
    b_one = sbtS("b_one", [128, NE], I32)
    b_kself = sbtS("b_kself", [128, NE], F32)

    @block.sync
    def _(s):
        s.wait_ge(sCC.h, CC_ALL)
        s.wait_ge(dVC.h, CP_VT)
        for j in range(4):
            s.wait_ge(dXN[j].h, dXN[j].n)
        s.wait_ge(dKV[0].h, dKV[0].n)
        s.wait_ge(dKV[1].h, dKV[1].n)
        ins = s.dma_start(out=pg[:],
                          in_=probs_all.ap().rearrange("(p a) e -> p a e", p=128))
        dPG.inc(ins, 16)
    LD_PG = dPG.n

    @block.gpsimd
    def _(g):
        g.wait_ge(dPG.h, LD_PG)
        sPO.inc(g.engine_nop())
    PO_INIT = sPO.n

    @block.vector
    def _(v):
        v.wait_ge(dPG.h, LD_PG)
        sVE.inc(v.memset(b_lo[:], 0))
        sVE.inc(v.memset(b_kself[:], float(KSEL)))
    VE_INIT = sVE.n

    def count_round(op, thr_ap):
        po_c = sPO.n

        @block.vector
        def _(v):
            v.wait_ge(sPO.h, po_c)
            v.wait_ge(sVE.h, sVE.n)
            sVE.inc(v.tensor_tensor(
                b_maskb[:], pg[:],
                thr_ap.unsqueeze(1).to_broadcast([128, 128, NE]), op))
            v.wait_ge(sVE.h, sVE.n)
            with nc.allow_low_precision(reason="counts <=128 are bf16-exact"):
                sVE.inc(v.tensor_reduce(b_cntb[:], b_maskb[:].rearrange("p a e -> p e a"),
                                        axis=AX.X, op=ALU.add))
        ve_c = sVE.n

        @block.tensor
        def _(t):
            t.wait_ge(sVE.h, ve_c)
            sPE.inc(t.matmul(pA[0:8, 0:1], b_cntb[:], onescol[:], start=True, stop=True))
        pe1 = sPE.n

        @block.scalar
        def _(a):
            a.wait_ge(sPE.h, pe1)
            sACT.inc(a.activation(b_cnt8[:], pA[0:8, 0:1], AF.Copy))
        ac1 = sACT.n

        @block.tensor
        def _(t):
            t.wait_ge(sACT.h, ac1)
            sPE.inc(t.transpose(pA[0:1, 8:16], b_cnt8[:], ident_f[0:8, 0:8]))
        pe2 = sPE.n

        @block.scalar
        def _(a):
            a.wait_ge(sPE.h, pe2)
            sACT.inc(a.activation(b_cntrow[:], pA[0:1, 8:16], AF.Copy))
        ac2 = sACT.n

        @block.tensor
        def _(t):
            t.wait_ge(sACT.h, ac2)
            sPE.inc(t.matmul(pB[:, 0:8], onesrow_f[:], b_cntrow[:], start=True, stop=True))
        pe3 = sPE.n

        @block.scalar
        def _(a):
            a.wait_ge(sPE.h, pe3)
            sACT.inc(a.activation(b_cntall[:], pB[:, 0:8], AF.Copy))
        return sACT.n

    for r_ in range(30):
        kbit = 29 - r_
        @block.vector
        def _(v, kbit=kbit, first=(r_ == 0)):
            if first:
                v.wait_ge(sPO.h, PO_INIT)
            v.wait_ge(sVE.h, sVE.n)
            sVE.inc(v.memset(b_t1[:], 1 << kbit))
            v.wait_ge(sVE.h, sVE.n)
            sVE.inc(v.tensor_tensor(b_mid[:], b_lo[:], b_t1[:], ALU.bitwise_or))
        actn = count_round(ALU.is_ge, b_mid[:].bitcast(F32))
        @block.vector
        def _(v, actn=actn):
            v.wait_ge(sACT.h, actn)
            v.wait_ge(sVE.h, sVE.n)
            sVE.inc(v.tensor_tensor(b_ge[:], b_cntall[:], b_kself[:], ALU.is_ge))
            v.wait_ge(sVE.h, sVE.n)
            sVE.inc(v.tensor_copy(b_gei[:], b_ge[:]))
            v.wait_ge(sVE.h, sVE.n)
            sVE.inc(v.select(b_lo[:], b_gei[:], b_mid[:], b_lo[:], add_drain=True))

    @block.vector
    def _(v):
        v.wait_ge(sVE.h, sVE.n)
        sVE.inc(v.tensor_copy(b_theta[:], b_lo[:].bitcast(F32)))
    actn = count_round(ALU.is_gt, b_theta[:])
    @block.vector
    def _(v, actn=actn):
        v.wait_ge(sACT.h, actn)
        v.wait_ge(sVE.h, sVE.n)
        sVE.inc(v.tensor_tensor(b_rtake[:], b_kself[:], b_cntall[:], ALU.subtract))
    PO_RT = sPO.n
    VE_RT = sVE.n

    g_gt = sbtS("g_gt", [128, 128, NE], BF16)
    g_tie = sbtS("g_tie", [128, 128, NE], BF16)
    g_scan = sbtS("g_scan", [128, 128, NE], F32)
    g_cnt = sbtS("g_cnt", [128, NE], BF16)
    g_base = sbtS("g_base", [128, NE], F32)
    g_sel = sbtS("g_sel", [128, 128, NE], F32)

    @block.vector
    def _(v):
        v.wait_ge(sVE.h, VE_RT)
        v.wait_ge(sVE.h, sVE.n)
        th = b_theta[:].unsqueeze(1).to_broadcast([128, 128, NE])
        sVE.inc(v.tensor_tensor(g_gt[:], pg[:], th, ALU.is_gt))
        sVE.inc(v.tensor_tensor(g_tie[:], pg[:], th, ALU.is_equal))
    PO_TIE = sPO.n
    VE_TIE = sVE.n

    @block.vector
    def _(v):
        v.wait_ge(sVE.h, VE_TIE)
        with nc.allow_low_precision(reason="counts <=128 are bf16-exact"):
            sVE.inc(v.tensor_reduce(g_cnt[:], g_tie[:].rearrange("p a e -> p e a"),
                                    axis=AX.X, op=ALU.add))
    VE_GCNT = sVE.n

    @block.tensor
    def _(t):
        t.wait_ge(sVE.h, VE_GCNT)
        sPE.inc(t.matmul(pA[:, 0:8], lstrict[:], g_cnt[:], start=True, stop=True))
    PE_TP = sPE.n

    @block.scalar
    def _(a):
        a.wait_ge(sPE.h, PE_TP)
        sACT.inc(a.activation(g_base[:], pA[:, 0:8], AF.Copy))
    ACT_TB = sACT.n

    @block.vector
    def _(v):
        v.wait_ge(sVE.h, VE_TIE)
        for e in range(NE):
            ins = v.tensor_tensor_scan(
                g_scan[:].rearrange("p a e -> p e a")[:, e, :],
                g_tie[:].rearrange("p a e -> p e a")[:, e, :],
                g_tie[:].rearrange("p a e -> p e a")[:, e, :],
                0.0, ALU.add, ALU.bypass)
        sVE.inc(ins)
        v.wait_ge(sVE.h, sVE.n)
        sVE.inc(v.tensor_sub(g_scan[:], g_scan[:], g_tie[:]))
        v.wait_ge(sVE.h, sVE.n)
        v.wait_ge(sACT.h, ACT_TB)
        for e in range(NE):
            ins = v.tensor_scalar(g_scan[:].rearrange("p a e -> p e a")[:, e, :],
                                  g_scan[:].rearrange("p a e -> p e a")[:, e, :],
                                  g_base[:, e:e+1], None, ALU.add)
        sVE.inc(ins)
        v.wait_ge(sVE.h, sVE.n)
        for e in range(NE):
            ins = v.tensor_scalar(g_scan[:].rearrange("p a e -> p e a")[:, e, :],
                                  g_scan[:].rearrange("p a e -> p e a")[:, e, :],
                                  b_rtake[:, e:e+1], None, ALU.is_lt)
        sVE.inc(ins)
        v.wait_ge(sVE.h, sVE.n)
        sVE.inc(v.tensor_mul(g_scan[:], g_scan[:], g_tie[:]))
        v.wait_ge(sVE.h, sVE.n)
        sVE.inc(v.tensor_add(g_sel[:], g_scan[:], g_gt[:]))
    VE_SEL = sVE.n

    @block.sync
    def _(s):
        s.wait_ge(sVE.h, VE_SEL)
        ins = s.dma_start(out=sel_dram.ap().rearrange("(p a) e -> p a e", p=128),
                          in_=g_sel[:])
        dSEL.inc(ins, 16)
    ST_SEL = dSEL.n



    selL = sbtS("selL", [128, 16, NE], F32)
    slot2 = sbtS("slot2", [128, 16, NE], F32)
    lcnt = sbtS("lcnt", [128, NE], BF16)
    lbase = sbtS("lbase", [128, NE], F32)

    @block.gpsimd
    def _(g):
        g.wait_ge(dSEL.h, ST_SEL)
        for a_ in range(16):
            ins = g.indirect_dma_start(
                out=selL[:, a_, :], out_offset=None,
                in_=sel_dram.ap(),
                in_offset=bass.IndirectOffsetOnAxis(ap=selrows[:, a_:a_+1], axis=0))
            dSL.inc(ins, 16)
    LD_SELL_G = dSL.n

    @block.vector
    def _(v):
        v.wait_ge(dSL.h, LD_SELL_G)
        with nc.allow_low_precision(reason="counts <=16 are bf16-exact"):
            sVE.inc(v.tensor_reduce(lcnt[:], selL[:].rearrange("p a e -> p e a"),
                                    axis=AX.X, op=ALU.add))
    VE_LC = sVE.n

    @block.tensor
    def _(t):
        t.wait_ge(sVE.h, VE_LC)
        sPE.inc(t.matmul(pA[:, 0:8], lstrict[:], lcnt[:], start=True, stop=True))
    PE_LB = sPE.n

    @block.scalar
    def _(a):
        a.wait_ge(sPE.h, PE_LB)
        sACT.inc(a.activation(lbase[:], pA[:, 0:8], AF.Copy))
    ACT_LB = sACT.n

    @block.vector
    def _(v):
        v.wait_ge(dSL.h, LD_SELL_G)
        for e in range(NE):
            ins = v.tensor_tensor_scan(
                slot2[:].rearrange("p a e -> p e a")[:, e, :],
                selL[:].rearrange("p a e -> p e a")[:, e, :],
                selL[:].rearrange("p a e -> p e a")[:, e, :],
                0.0, ALU.add, ALU.bypass)
        sVE.inc(ins)
        v.wait_ge(sVE.h, sVE.n)
        sVE.inc(v.tensor_sub(slot2[:], slot2[:], selL[:]))
        v.wait_ge(sVE.h, sVE.n)
        v.wait_ge(sACT.h, ACT_LB)
        for e in range(NE):
            ins = v.tensor_scalar(slot2[:].rearrange("p a e -> p e a")[:, e, :],
                                  slot2[:].rearrange("p a e -> p e a")[:, e, :],
                                  lbase[:, e:e+1], None, ALU.add)
        sVE.inc(ins)
        v.wait_ge(sVE.h, sVE.n)
        sVE.inc(v.tensor_scalar(g_scan[:, 0:16, :], slot2[:], float(CAP), None, ALU.is_lt))
        v.wait_ge(sVE.h, sVE.n)
        sVE.inc(v.tensor_mul(selL[:], selL[:], g_scan[:, 0:16, :]))
        v.wait_ge(sVE.h, sVE.n)
        sVE.inc(v.tensor_scalar(slot2[:], slot2[:], 9999.0, None, ALU.subtract))
        v.wait_ge(sVE.h, sVE.n)
        sVE.inc(v.tensor_mul(slot2[:], slot2[:], selL[:]))
        v.wait_ge(sVE.h, sVE.n)
        sVE.inc(v.tensor_scalar(slot2[:], slot2[:], 9999.0, None, ALU.add))
    VE_SLOT = sVE.n

    oh = sbtS("oh", [128, 16, CAP], BF16)
    rhsmm = sbtS("rhsmm", [128, 16, 3], BF16)
    pgrow = sbtS("pgrow", [128, NE], F32)
    idxf = sbtS("idxf", [128, 1], F32)
    idx3 = sbtS("idx3", [128, 3], F32)

    @block.vector
    def _(v):
        v.wait_ge(sVE.h, VE_SLOT)
        for a_ in range(16):
            v.tensor_copy(rhsmm[:, a_, 0:1], tloc_p[:])
            v.memset(rhsmm[:, a_, 1:2], float(a_))
        ins = v.memset(rhsmm[:, :, 2:3], 1.0)
        sVE.inc(ins)
    VE_RHS = sVE.n

    for e in range(NE):
        ve_oh0 = sVE.n
        @block.vector
        def _(v, e=e):
            if e == 0:
                v.wait_ge(sVE.h, VE_RHS)
            else:
                v.wait_ge(sPE.h, sPE.n)
            for a_ in range(16):
                ins = v.tensor_scalar(oh[:, a_, :], iota_slot[:], slot2[:, a_, e:e+1],
                                      None, ALU.is_equal)
            sVE.inc(ins)
        ve_oh = sVE.n

        for sc in range(3):
            @block.tensor
            def _(t, e=e, sc=sc, ve_oh=ve_oh):
                t.wait_ge(sVE.h, ve_oh)
                if sc > 0:
                    t.wait_ge(sVE.h, sVE.n)
                for a_ in range(16):
                    ins = t.matmul(pC[:, 0:3], oh[:, a_, sc*128:(sc+1)*128],
                                   rhsmm[:, a_, :], start=(a_ == 0), stop=(a_ == 15))
                sPE.inc(ins)
            pen = sPE.n

            @block.vector
            def _(v, e=e, sc=sc, pen=pen):
                v.wait_ge(sPE.h, pen)
                sVE.inc(v.tensor_copy(idx3[:], pC[:, 0:3]))
                v.wait_ge(sVE.h, sVE.n)
                sVE.inc(v.scalar_tensor_tensor(idxf[:], idx3[:, 1:2], 128.0, idx3[:, 0:1],
                                               ALU.mult, ALU.add))
                v.wait_ge(sVE.h, sVE.n)
                sVE.inc(v.tensor_copy(idxi[:, sc, e:e+1], idxf[:]))
                sVE.inc(v.tensor_copy(gval[:, sc, e:e+1], idx3[:, 2:3]))
    VE_IDX = sVE.n
    PE_IDX = sPE.n

    for e in range(NE):
        for sc in range(3):
            @block.gpsimd
            def _(g, e=e, sc=sc):
                if e == 0 and sc == 0:
                    g.wait_ge(sVE.h, VE_IDX)
                g.wait_ge(sVE.h, sVE.n)
                ins = g.indirect_dma_start(
                    out=pgrow[:], out_offset=None,
                    in_=probs_dram.ap(),
                    in_offset=bass.IndirectOffsetOnAxis(ap=idxi[:, sc, e:e+1], axis=0))
                dGG.inc(ins, 16)
            dgn = dGG.n

            @block.vector
            def _(v, e=e, sc=sc, dgn=dgn):
                v.wait_ge(dGG.h, dgn)
                sVE.inc(v.tensor_mul(gval[:, sc, e:e+1], gval[:, sc, e:e+1],
                                     pgrow[:, e:e+1]))
    VE_GV = sVE.n
    DG_SEL_END = dGG.n

    if "sel" in dbg_names:
        od = nc.dram_tensor("dbg_sel", [128, 16, NE], F32, kind="ExternalOutput").ap()
        @block.scalar
        def _(a):
            a.wait_ge(sVE.h, VE_GV)
            dA.inc(a.dma_start(out=od, in_=selL[:]), 16)
        od2 = nc.dram_tensor("dbg_slot", [128, 16, NE], F32, kind="ExternalOutput").ap()
        @block.scalar
        def _(a):
            a.wait_ge(sVE.h, VE_GV)
            dA.inc(a.dma_start(out=od2, in_=slot2[:]), 16)
        od3 = nc.dram_tensor("dbg_idx", [128, 3, NE], I32, kind="ExternalOutput").ap()
        @block.scalar
        def _(a):
            a.wait_ge(sVE.h, VE_GV)
            dA.inc(a.dma_start(out=od3, in_=idxi[:]), 16)
        od4 = nc.dram_tensor("dbg_gval", [128, 3, NE], F32, kind="ExternalOutput").ap()
        @block.scalar
        def _(a):
            a.wait_ge(sVE.h, VE_GV)
            dA.inc(a.dma_start(out=od4, in_=gval[:]), 16)

    if "probs" in dbg_names:
        od = nc.dram_tensor("dbg_probs", [128, 16, NE], F32, kind="ExternalOutput").ap()
        @block.scalar
        def _(a):
            a.wait_ge(sVE.h, VE_PROB)
            dA.inc(a.dma_start(out=od, in_=ptok[:]), 16)
    if "qt" in dbg_names:
        od = nc.dram_tensor("dbg_qt", [128, 8, TLOC], BF16, kind="ExternalOutput").ap()
        @block.scalar
        def _(a):
            a.wait_ge(sVE.h, VE_QT)
            dA.inc(a.dma_start(out=od, in_=qt[:]), 16)
    if "theta" in dbg_names:
        od = nc.dram_tensor("dbg_theta", [128, NE], F32, kind="ExternalOutput").ap()
        @block.scalar
        def _(a):
            a.wait_ge(sVE.h, VE_RT)
            dA.inc(a.dma_start(out=od, in_=b_theta[:]), 16)
        od2 = nc.dram_tensor("dbg_rtake", [128, NE], F32, kind="ExternalOutput").ap()
        @block.scalar
        def _(a):
            a.wait_ge(sVE.h, VE_RT)
            dA.inc(a.dma_start(out=od2, in_=b_rtake[:]), 16)

    if upto <= 6:
        @block.scalar
        def _(a):
            a.wait_ge(dA.h, dA.n)
            a.wait_ge(dVC.h, dVC.n)
            a.wait_ge(dGG.h, dGG.n)
            a.wait_ge(sCC.h, sCC.n)
            a.wait_ge(sVE.h, sVE.n)
        ctxS.close()
        ctxQ.close()
        ctx.close()
        return nc

    ctxS.close()

    ctxB = contextlib.ExitStack()
    sbtB = lambda name, shape, dt: ctxB.enter_context(nc.sbuf_tensor(name, shape, dt))
    ktb = [sbtB("ktb0", [128, S], BF16), sbtB("ktb1", [128, S], BF16)]
    vtb = [sbtB("vtb0", [128, 64, 130], BF16), sbtB("vtb1", [128, 64, 130], BF16)]
    ebuf = [sbtB("ebuf0", [128, 2048], BF16), sbtB("ebuf1", [128, 2048], BF16)]
    vtail = sbtB("vtail", [128, VT_BLKS, 128], BF16)
    bfp = sbtB("bfp", [128, 16 * 128], F32)
    btl = [sbtB("btl0", [128, 32, 128], BF16), sbtB("btl1", [128, 32, 128], BF16)]
    zrow = sbtB("zrow", [1, 1024], F32)
    pvs = sbtB("pvs", [64, 1024], BF16)
    rz = sbtB("rz", [128, 8], F32)
    ttmp = sbtB("ttmp", [128, 128], F32)

    @block.vector
    def _(v):
        v.wait_ge(sVE.h, sVE.n)
        v.memset(attn_tok[:], 0.0)
        for bi in range(2):
            v.memset(vtb[bi][:, :, 64:65], 1.0)
            ins = v.memset(vtb[bi][:, :, 129:130], 1.0)
        sVE.inc(ins)
    VE_ONES = sVE.n

    kt_ready = []
    bt_ready = []
    vtl_ready = []
    hp_pe_end = []
    exp_base = sACT.n
    qk_base = sPE.n

    exp_n = [0]
    pe_n = lambda: sPE.n
    tail_ve_done = [sVE.n]

    def emit_loads(hp, wait_pe):
        bi = hp % 2
        @block.sync
        def _(s, hp=hp, bi=bi, wait_pe=wait_pe):
            s.wait_ge(sCC.h, CC_KV)
            s.wait_ge(dZ.h, ST_ZE)
            s.wait_ge(sVE.h, VE_GV)
            s.wait_ge(sPE.h, PE_IDX)
            s.wait_ge(dGG.h, DG_SEL_END)
            s.wait_ge(dSEL.h, ST_SEL)
            s.wait_ge(dPG.h, LD_PG)
            if wait_pe is not None:
                s.wait_ge(sPE.h, wait_pe)
            ins = s.dma_start_transpose(ktb[bi][:], k_full.ap()[:, hp*128:(hp+1)*128])
            dKT[bi].inc(ins, 16)
            ins = s.dma_start(
                out=vtb[bi][:, :, 0:64],
                in_=v_pad.ap()[0:S, hp*128:hp*128+64]
                    .rearrange("(kc p) d -> p kc d", p=128))
            dKT[bi].inc(ins, 16)
            ins = s.dma_start(
                out=vtb[bi][:, :, 65:129],
                in_=v_pad.ap()[0:S, hp*128+64:hp*128+128]
                    .rearrange("(kc p) d -> p kc d", p=128))
            dKT[bi].inc(ins, 16)
        kt_ready.append(dKT[bi].n)

    def emit_bt(hp, wait_pe_tail):
        for hh in range(2):
            for half in range(2):
                @block.sync
                def _(s, hp=hp, hh=hh, half=half, wait_pe_tail=wait_pe_tail):
                    if hh == 0 and half == 0 and wait_pe_tail is not None:
                        s.wait_ge(sPE.h, wait_pe_tail)
                    if hh + half > 0:
                        s.wait_ge(sVE.h, sVE.n)
                    h = hp * 2 + hh
                    ins = s.dma_start(
                        out=bfp[:],
                        in_=btail_in[h][16*half:16*half+16].rearrange("m u r -> u m r"))
                    dBF.inc(ins, 16)
                dsn = dBF.n
                @block.vector
                def _(v, hh=hh, half=half, dsn=dsn):
                    v.wait_ge(dBF.h, dsn)
                    sVE.inc(v.tensor_copy(
                        btl[hh][:, 16*half:16*half+16, :],
                        bfp[:].rearrange("p (m r) -> p m r", r=128)))
        @block.sync
        def _(s, hp=hp):
            s.wait_ge(dVC.h, CP_VT)
            ins = s.dma_start(
                out=vtail[:],
                in_=vt_dram.ap().rearrange("(w p) d -> p w d", p=128)
                    [:, :, hp*128:(hp+1)*128])
            dVT.inc(ins, 16)
        vtl_ready.append(dVT.n)
        bt_ready.append(sVE.n)

    emit_loads(0, None)
    emit_loads(1, None)

    PE_ATT0 = sPE.n
    carry_rz_ve = None
    carry_pvcopy_ve = None
    for hp in range(nhp):
        bi = hp % 2
        emit_bt(hp, hp_pe_end[hp - 1] if hp >= 1 else None)

        qb_rz_ve = carry_rz_ve
        qb_pvcopy_ve = carry_pvcopy_ve
        for qb in (range(4) if "qb" not in skip else range(0)):
            pe_qk_ids = []
            exp_ids = []
            pv_ids = []
            qk0 = sPE.n
            NP = nkc // 2
            @block.tensor
            def _(t, hp=hp, qb=qb, bi=bi, qb_rz_ve=qb_rz_ve, qb_pvcopy_ve=qb_pvcopy_ve,
                  NP=NP):
                t.wait_ge(dKT[bi].h, kt_ready[hp])
                t.wait_ge(sVE.h, VE_ONES)
                for p in range(NP):
                    if p >= 1:
                        t.wait_ge(sACT.h, exp_base + exp_n[0] + p)
                    elif qb_rz_ve is not None:
                        t.wait_ge(sVE.h, qb_rz_ve)
                    for kc, off in ((2*p, 0), (2*p + 1, 1024)):
                        t.matmul(pS[:, off:off+512], ktb[bi][0:64, kc*128:(kc+1)*128],
                                 qt[0:64, hp, qb*512:(qb+1)*512],
                                 start=True, stop=True, tile_position=(0, 0))
                        ins = t.matmul(pS[:, off+512:off+1024],
                                       ktb[bi][64:128, kc*128:(kc+1)*128],
                                       qt[64:128, hp, qb*512:(qb+1)*512],
                                       start=True, stop=True, tile_position=(64, 0))
                    sPE.inc(ins)
                    if p >= 1 and "pv" not in skip:
                        t.wait_ge(sACT.h, exp_base + exp_n[0] + p)
                        eb = ebuf[(p - 1) % 2]
                        if p == 1 and qb_pvcopy_ve is not None:
                            t.wait_ge(sVE.h, qb_pvcopy_ve)
                        for j, kc in enumerate((2*(p-1), 2*(p-1) + 1)):
                            t.matmul(pC[0:65, :], vtb[bi][:, kc, 0:65],
                                     eb[:, j*1024:j*1024+512],
                                     start=(kc == 0), stop=False)
                            ins = t.matmul(pD[0:65, :], vtb[bi][:, kc, 65:130],
                                           eb[:, j*1024+512:j*1024+1024],
                                           start=(kc == 0), stop=False)
                        sPE.inc(ins)
                if "pv" in skip:
                    return
                t.wait_ge(sACT.h, exp_base + exp_n[0] + NP)
                eb = ebuf[(NP - 1) % 2]
                for j, kc in enumerate((nkc - 2, nkc - 1)):
                    t.matmul(pC[0:65, :], vtb[bi][:, kc, 0:65],
                             eb[:, j*1024:j*1024+512],
                             start=False, stop=(kc == nkc - 1))
                    ins = t.matmul(pD[0:65, :], vtb[bi][:, kc, 65:130],
                                   eb[:, j*1024+512:j*1024+1024],
                                   start=False, stop=(kc == nkc - 1))
                sPE.inc(ins)
            pe_qb_end = sPE.n

            @block.scalar
            def _(a, hp=hp, qb=qb, qk0=qk0, NP=NP):
                for p in range(NP):
                    a.wait_ge(sPE.h, qk0 + (2 * p + 1 if "pv" not in skip else p + 1))
                    ins = a.activation(ebuf[p % 2][:], pS[:], AF.Exp, scale=1.0 / 32.0)
                    sACT.inc(ins)
                    exp_n[0] += 1
            if "pv" in skip or "bnd" in skip:
                @block.vector
                def _(v, pe_qb_end=pe_qb_end):
                    v.wait_ge(sPE.h, pe_qb_end)
                    sVE.inc(v.tensor_copy(zrow[0:1, 0:512], pC[64:65, :] if "pv" not in skip else ebuf[0][0:1, 0:512]))
                qb_pvcopy_ve = sVE.n
                qb_rz_ve = sVE.n
                continue
            ve0 = sVE.n
            @block.vector
            def _(v, pe_qb_end=pe_qb_end):
                v.wait_ge(sPE.h, pe_qb_end)
                v.tensor_copy(zrow[0:1, 0:512], pC[64:65, :])
                sVE.inc(v.tensor_copy(zrow[0:1, 512:1024], pD[64:65, :]))
                v.tensor_copy(pvs[0:64, 0:512], pC[0:64, :])
                sVE.inc(v.tensor_copy(pvs[0:64, 512:1024], pD[0:64, :]))
            ve_zpv = sVE.n
            qb_pvcopy_ve = ve_zpv

            pe_z0 = sPE.n
            if "brz" in skip:
                @block.vector
                def _(v, ve_zpv=ve_zpv):
                    v.wait_ge(sVE.h, ve_zpv)
                    sVE.inc(v.memset(rz[:], 1.0))
                ve_rz = sVE.n
                qb_rz_ve = ve_rz
            else:
                @block.tensor
                def _(t, ve_zpv=ve_zpv):
                    t.wait_ge(sVE.h, ve_zpv)
                    for c in range(8):
                        ins = t.transpose(pA[:, 8+c:9+c],
                                          zrow[0:1, c*128:(c+1)*128], ident_f[0:1, 0:1])
                    sPE.inc(ins)
                pe_z = sPE.n

                @block.vector
                def _(v, pe_z=pe_z):
                    v.wait_ge(sPE.h, pe_z)
                    sVE.inc(v.reciprocal(rz[:], pA[:, 8:16]))
                ve_rz = sVE.n
                qb_rz_ve = ve_rz

            for sub in (range(4) if "bst" not in skip else range(0)):
                pe_s0 = sPE.n
                @block.tensor
                def _(t, sub=sub, ve_zpv=ve_zpv):
                    t.wait_ge(sVE.h, ve_zpv)
                    if sub >= 2:
                        t.wait_ge(sVE.h, sVE.n)
                    pb_ = pBF if sub % 2 == 0 else pBF2
                    t.transpose(pb_[:, 0:64], pvs[0:64, sub*128:(sub+1)*128],
                                ident_bf[0:64, 0:64])
                    ins = t.transpose(pb_[:, 64:128], pvs[0:64, 512+sub*128:512+(sub+1)*128],
                                      ident_bf[0:64, 0:64])
                    sPE.inc(ins)
                pe_sT = sPE.n

                @block.vector
                def _(v, hp=hp, qb=qb, sub=sub, pe_sT=pe_sT, ve_rz=ve_rz):
                    v.wait_ge(sPE.h, pe_sT)
                    v.wait_ge(sVE.h, ve_rz)
                    pb_ = pBF if sub % 2 == 0 else pBF2
                    at = qb * 4 + sub
                    v.scalar_tensor_tensor(
                        attn_tok[:, at, hp*128:hp*128+64], pb_[:, 0:64],
                        rz[:, sub:sub+1], attn_tok[:, at, hp*128:hp*128+64],
                        ALU.mult, ALU.add)
                    ins = v.scalar_tensor_tensor(
                        attn_tok[:, at, hp*128+64:hp*128+128], pb_[:, 64:128],
                        rz[:, 4+sub:5+sub], attn_tok[:, at, hp*128+64:hp*128+128],
                        ALU.mult, ALU.add)
                    sVE.inc(ins)

        @block.tensor
        def _(t, hp=hp):
            t.wait_ge(dVT.h, vtl_ready[hp])
            t.wait_ge(sVE.h, bt_ready[hp])
        for a_ in (range(16) if "tail" not in skip else range(0)):
            pe_t0 = sPE.n
            @block.tensor
            def _(t, hp=hp, a_=a_):
                if a_ >= 1:
                    t.wait_ge(sVE.h, sVE.n)
                nm = 32 - a_
                for m in range(nm):
                    t.matmul(pB[0:64, 0:128], vtail[:, a_+m, 0:64], btl[0][:, m, :],
                             start=(m == 0), stop=(m == nm - 1), tile_position=(0, 0))
                    ins = t.matmul(pB[64:128, 512:640], vtail[:, a_+m, 64:128], btl[1][:, m, :],
                                   start=(m == 0), stop=(m == nm - 1), tile_position=(0, 64))
                sPE.inc(ins)
            pe_tm = sPE.n

            @block.vector
            def _(v, pe_tm=pe_tm):
                v.wait_ge(sPE.h, pe_tm)
                v.tensor_copy(ttmp[0:64, :], pB[0:64, 0:128])
                sVE.inc(v.tensor_copy(ttmp[64:128, :], pB[64:128, 512:640]))
            ve_tc = sVE.n

            @block.tensor
            def _(t, ve_tc=ve_tc):
                t.wait_ge(sVE.h, ve_tc)
                sPE.inc(t.transpose(pA[:, 256:384], ttmp[:], ident_f[:]))
            pe_tt = sPE.n

            @block.vector
            def _(v, hp=hp, a_=a_, pe_tt=pe_tt):
                v.wait_ge(sPE.h, pe_tt)
                sVE.inc(v.tensor_add(attn_tok[:, a_, hp*128:(hp+1)*128],
                                     attn_tok[:, a_, hp*128:(hp+1)*128],
                                     pA[:, 256:384]))
        hp_pe_end.append(sPE.n)
        carry_rz_ve = sVE.n
        carry_pvcopy_ve = qb_pvcopy_ve
        if hp + 2 < 8:
            emit_loads(hp + 2, hp_pe_end[hp])
    ATT_VE_END = sVE.n
    ATT_PE_END = sPE.n

    if upto == 7:
        od7 = nc.dram_tensor("dbg_attn", [128, 16, D], BF16, kind="ExternalOutput").ap()
        @block.scalar
        def _(a):
            a.wait_ge(sVE.h, ATT_VE_END)
            a.wait_ge(sPE.h, ATT_PE_END)
            dA.inc(a.dma_start(out=od7, in_=attn_tok[:]), 16)
            a.wait_ge(dA.h, dA.n)
        ctxB.close()
        ctxQ.close()
        ctx.close()
        return nc

    ctxB.close()
    ctxQ.close()

    ctxC = contextlib.ExitStack()
    sbtC = lambda name, shape, dt: ctxC.enter_context(nc.sbuf_tensor(name, shape, dt))
    vps = sbtC("vps", [128, 4, D], BF16)
    bex_sb = sbtC("bex_sb", [16, D], BF16)

    @block.sync
    def _(s):
        s.wait_ge(sPE.h, ATT_PE_END)
        s.wait_ge(sVE.h, ATT_VE_END)
        for kb in range(4):
            ins = s.dma_start(
                out=vps[:, kb % 4, :],
                in_=v_pad.ap()[kb*128:(kb+1)*128, :])
            dPS[kb % 4].inc(ins, 16)
    PE_BEX0 = sPE.n
    for kb in range(64):
        if kb >= 4:
            @block.sync
            def _(s, kb=kb):
                s.wait_ge(sPE.h, PE_BEX0 + (kb - 4) + 1)
                ins = s.dma_start(out=vps[:, kb % 4, :],
                                  in_=v_pad.ap()[kb*128:(kb+1)*128, :])
                dPS[kb % 4].inc(ins, 16)
        @block.tensor
        def _(t, kb=kb, dw=dPS[kb % 4].n):
            t.wait_ge(dPS[kb % 4].h, dw)
            t.matmul(pA[0:16, 0:512], bexw[:, kb, :], vps[:, kb % 4, 0:512],
                     start=(kb == 0), stop=(kb == 63))
            ins = t.matmul(pA[0:16, 512:1024], bexw[:, kb, :], vps[:, kb % 4, 512:1024],
                           start=(kb == 0), stop=(kb == 63))
            sPE.inc(ins)
    PE_BEX = sPE.n

    @block.scalar
    def _(a):
        a.wait_ge(sPE.h, PE_BEX)
        sACT.inc(a.activation(bex_sb[:], pA[0:16, :], AF.Copy))
    ACT_BEX = sACT.n

    vps2 = sbtC("vps2", [128, 3, D], BF16)
    @block.sync
    def _(s):
        s.wait_ge(sACT.h, ACT_BEX)
        for i in range(3):
            ins = s.dma_start(out=vps2[:, i, :],
                              in_=vt_dram.ap()[i*128:(i+1)*128, :])
            dP2[i].inc(ins, 16)
    PE_PS0 = sPE.n
    VE_PS0 = sVE.n
    for a_ in range(16):
        if a_ >= 1 and a_ + 2 <= 17:
            @block.sync
            def _(s, a_=a_):
                s.wait_ge(sPE.h, PE_PS0 + a_)
                ins = s.dma_start(out=vps2[:, (a_+2) % 3, :],
                                  in_=vt_dram.ap()[(a_+2)*128:(a_+3)*128, :])
                dP2[(a_+2) % 3].inc(ins, 16)
        @block.tensor
        def _(t, a_=a_, dwa=dP2[a_ % 3].n, dwb=dP2[(a_+1) % 3].n):
            t.wait_ge(dP2[a_ % 3].h, dwa)
            t.wait_ge(dP2[(a_+1) % 3].h, dwb)
            if a_ >= 1:
                t.wait_ge(sVE.h, VE_PS0 + a_ * 4)
            for ob in range(2):
                t.matmul(pB[:, ob*512:(ob+1)*512], trishift[:],
                         vps2[:, a_ % 3, ob*512:(ob+1)*512], start=True, stop=False)
                t.matmul(pB[:, ob*512:(ob+1)*512], onehotD[:],
                         vps2[:, (a_+1) % 3, ob*512:(ob+1)*512], start=False, stop=False)
                ins = t.matmul(pB[:, ob*512:(ob+1)*512],
                               selm[:, a_*128:(a_+1)*128],
                               bex_sb[:, ob*512:(ob+1)*512], start=False, stop=True)
            sPE.inc(ins)
        pe_ps = sPE.n

        @block.vector
        def _(v, a_=a_, pe_ps=pe_ps):
            v.wait_ge(sPE.h, pe_ps)
            for ob in range(2):
                v.wait_ge(sVE.h, sVE.n)
                sVE.inc(v.tensor_mul(scr1k[:, ob*512:(ob+1)*512], pB[:, ob*512:(ob+1)*512],
                                     ctile[:, ob*512:(ob+1)*512]))
                v.wait_ge(sVE.h, sVE.n)
                ins = v.tensor_add(attn_tok[:, a_, ob*512:(ob+1)*512],
                                   attn_tok[:, a_, ob*512:(ob+1)*512],
                                   scr1k[:, ob*512:(ob+1)*512])
                sVE.inc(ins)
    VE_PS = sVE.n

    @block.vector
    def _(v):
        v.wait_ge(sVE.h, VE_PS)
        for i in range(16):
            ins = v.reduce_sum(stat_a[:, i:i+1], attn_tok[:, i, :], axis=AX.X, negate=True)
        sVE.inc(ins)
    VE_NS2 = sVE.n

    @block.scalar
    def _(a):
        a.wait_ge(sVE.h, VE_PS)
        for i in range(16):
            ins = a.activation(scr1k[:], attn_tok[:, i, :], AF.Square,
                               accum_out=stat_b[:, i:i+1])
            sACT.inc(ins)
            a.wait_ge(sACT.h, sACT.n)
    ACT_SQ2 = sACT.n

    @block.vector
    def _(v):
        v.wait_ge(sACT.h, ACT_SQ2)
        v.wait_ge(sVE.h, VE_NS2)
        sVE.inc(v.tensor_scalar(stat_c[:], stat_a[:], 1.0 / D, None, ALU.mult))
        v.wait_ge(sVE.h, sVE.n)
        sVE.inc(v.tensor_mul(stat_d[:], stat_c[:], stat_c[:]))
        v.wait_ge(sVE.h, sVE.n)
        sVE.inc(v.tensor_scalar(stat_a[:], stat_b[:], 1.0 / D, None, ALU.mult))
        v.wait_ge(sVE.h, sVE.n)
        sVE.inc(v.tensor_sub(stat_d[:], stat_a[:], stat_d[:]))
    VE_VAR2 = sVE.n

    @block.vector
    def _(v):
        v.wait_ge(sVE.h, VE_VAR2)
        sVE.inc(v.tensor_scalar(stat_d[:], stat_d[:], EPS, None, ALU.add))
    VE_VEPS2 = sVE.n

    @block.scalar
    def _(a):
        a.wait_ge(sVE.h, VE_VEPS2)
        sACT.inc(a.activation(stat_d[:], stat_d[:], AF.Ln))
        a.wait_ge(sACT.h, sACT.n)
        sACT.inc(a.activation(stat_d[:], stat_d[:], AF.Exp, scale=-0.5))
    ACT_RSTD2 = sACT.n

    @block.vector
    def _(v):
        v.wait_ge(sACT.h, ACT_RSTD2)
        sVE.inc(v.tensor_mul(stat_e[:], stat_c[:], stat_d[:]))
    VE_MRS2 = sVE.n

    for i in range(16):
        @block.scalar
        def _(a, i=i):
            a.wait_ge(sVE.h, VE_MRS2)
            if i > 0:
                a.wait_ge(sVE.h, sVE.n)
            sACT.inc(a.activation(scr1kb[:], attn_tok[:, i, :], AF.Identity,
                                  bias=stat_e[:, i:i+1], scale=stat_d[:, i:i+1]))
        acn = sACT.n

        @block.vector
        def _(v, i=i, acn=acn):
            v.wait_ge(sACT.h, acn)
            sVE.inc(v.tensor_mul(scr1k[:], scr1kb[:], gtile[:]))
            v.wait_ge(sVE.h, sVE.n)
            ins = v.tensor_add(attn_tok[:, i, :], scr1k[:], btile[:])
            sVE.inc(ins)
    VE_LN2 = sVE.n

    xnsel = sbtC("xnsel", [128, D], BF16)
    xnselT = sbtC("xnselT", [128, 8, CAP], BF16)
    midT = sbtC("midT", [128, NFC + 1, CAP], BF16)
    wupb = [sbtC("wupb0", [128, 8, 128], BF16), sbtC("wupb1", [128, 8, 128], BF16)]
    wdnb = [sbtC("wdnb0", [128, D], BF16), sbtC("wdnb1", [128, D], BF16)]
    bdnrow = sbtC("bdnrow", [1, D], BF16)
    yout3 = sbtC("yout3", [128, 3, D], F32)

    @block.vector
    def _(v):
        v.wait_ge(sVE.h, sVE.n)
        sVE.inc(v.memset(midT[0:1, NFC, :], 1.0))
    VE_MID1 = sVE.n

    @block.gpsimd
    def _(g):
        g.wait_ge(sPE.h, ATT_PE_END)
        g.wait_ge(sVE.h, ATT_VE_END)

    bupt = sbtC("bupt", [NFC, 128], F32)
    bupall = sbtC("bupall", [128, 4, NFC], F32)
    for gi in range(4):
        @block.gpsimd
        def _(g, gi=gi):
            g.wait_ge(dG.h, dG.n)
            if gi > 0:
                g.wait_ge(sPE.h, sPE.n)
            ins = g.dma_start(out=bupt[:],
                              in_=bup_in[gi:gi+1, :].rearrange("o (c p) -> (o c) p", p=128))
            dG.inc(ins, 16)
        dgn = dG.n

        @block.tensor
        def _(t, gi=gi, dgn=dgn):
            t.wait_ge(dG.h, dgn)
            if gi > 0:
                t.wait_ge(sVE.h, sVE.n)
            sPE.inc(t.transpose(pA[:, 0:NFC], bupt[:], ident_f[0:NFC, 0:NFC]))
        pen = sPE.n

        @block.vector
        def _(v, gi=gi, pen=pen):
            v.wait_ge(sPE.h, pen)
            sVE.inc(v.tensor_copy(bupall[:, gi, :], pA[:, 0:NFC]))
    VE_BUP = sVE.n

    for e in range(NE):
        ge_ = e // 2
        @block.gpsimd
        def _(g, e=e, ge_=ge_):
            if e > 0:
                g.wait_ge(sPE.h, sPE.n)
            ins = g.dma_start(out=bdnrow[:], in_=bdn_in[e:e+1, :])
            dBD.inc(ins, 16)
        dg_b = dBD.n

        for sc in range(3):
            @block.gpsimd
            def _(g, e=e, sc=sc):
                g.wait_ge(sVE.h, VE_GV)
                if e == 0 and sc == 0:
                    for j in range(4):
                        g.wait_ge(dXN[j].h, dXN[j].n)
                else:
                    g.wait_ge(sPE.h, sPE.n)
                ins = g.indirect_dma_start(
                    out=xnsel[:], out_offset=None,
                    in_=xn_dram.ap(),
                    in_offset=bass.IndirectOffsetOnAxis(ap=idxi[:, sc, e:e+1], axis=0))
                dXG.inc(ins, 16)
            dg_x = dXG.n

            @block.tensor
            def _(t, e=e, sc=sc, dg_x=dg_x):
                t.wait_ge(dXG.h, dg_x)
                if e == 0 and sc == 0:
                    t.wait_ge(sVE.h, VE_MID1)
                for c in range(8):
                    t.wait_ge(sVE.h, sVE.n)
                    buf = pBF if c % 2 == 0 else pBF2
                    ins = t.transpose(buf[:, 0:128], xnsel[:, c*128:(c+1)*128], ident_bf[:])
                    sPE.inc(ins)
                    @block.vector
                    def _(v, sc=sc, c=c, pen=sPE.n, buf=buf):
                        v.wait_ge(sPE.h, pen)
                        sVE.inc(v.tensor_copy(xnselT[:, c, sc*128:(sc+1)*128],
                                              buf[:, 0:128]))
        ve_xt = sVE.n

        for fc in range(NFC):
            wbu = wupb[fc % 2]
            @block.gpsimd
            def _(g, e=e, ge_=ge_, fc=fc, wbu=wbu):
                if fc >= 2:
                    g.wait_ge(sPE.h, sPE.n)
                ins = g.dma_start(
                    out=wbu[:],
                    in_=wup_in[ge_:ge_+1, :, fc*128:(fc+1)*128]
                        .rearrange("o (c p) f -> p (o c) f", p=128))
                dWU[fc % 2].inc(ins, 16)
            dg_w = dWU[fc % 2].n

            @block.tensor
            def _(t, e=e, fc=fc, wbu=wbu, dg_w=dg_w, ve_xt=ve_xt, ve_now=sVE.n):
                t.wait_ge(dWU[fc % 2].h, dg_w)
                t.wait_ge(sVE.h, ve_xt)
                if fc < 2:
                    t.wait_ge(sVE.h, ve_now)
                if fc >= 2:
                    t.wait_ge(sACT.h, sACT.n)
                buf = pC if fc % 2 == 0 else pD
                for k in range(8):
                    ins = t.matmul(buf[:, 0:CAP], wbu[:, k, :], xnselT[:, k, :],
                                   start=(k == 0), stop=(k == 7))
                sPE.inc(ins)
            pe_up = sPE.n

            @block.scalar
            def _(a, fc=fc, pe_up=pe_up, ge_=ge_):
                a.wait_ge(sPE.h, pe_up)
                buf = pC if fc % 2 == 0 else pD
                sACT.inc(a.activation(midT[:, fc, :], buf[:, 0:CAP], AF.Silu,
                                      bias=bupall[:, ge_, fc:fc+1]))
        act_up = sACT.n

        accs = [pA[:, 0:512], pA[:, 512:1024], pB[:, 0:512], pB[:, 512:1024],
                pC[:, 0:512], pD[:, 0:512]]
        ve_prev_evac = sVE.n
        for fc in range(NFC):
            wbd = wdnb[fc % 2]
            @block.gpsimd
            def _(g, e=e, fc=fc, wbd=wbd):
                if fc >= 2:
                    g.wait_ge(sPE.h, sPE.n)
                elif e > 0:
                    g.wait_ge(sPE.h, sPE.n)
                ins = g.dma_start(
                    out=wbd[:],
                    in_=wdn_in.rearrange("e f h -> (e f) h")
                        [e*FPAD + fc*128: e*FPAD + (fc+1)*128, :])
                dWD[fc % 2].inc(ins, 16)
            dg_w = dWD[fc % 2].n

            @block.tensor
            def _(t, e=e, fc=fc, wbd=wbd, dg_w=dg_w, act_up=act_up,
                  ve_prev_evac=ve_prev_evac):
                t.wait_ge(dWD[fc % 2].h, dg_w)
                if fc == 0:
                    t.wait_ge(sACT.h, act_up)
                    t.wait_ge(sVE.h, ve_prev_evac)
                for sc in range(3):
                    for ob in range(2):
                        ins = t.matmul(accs[sc*2+ob], midT[:, fc, sc*128:(sc+1)*128],
                                       wbd[:, ob*512:(ob+1)*512],
                                       start=(fc == 0), stop=False)
                sPE.inc(ins)
        @block.tensor
        def _(t, e=e, dg_b=dg_b):
            t.wait_ge(dBD.h, dg_b)
            for sc in range(3):
                for ob in range(2):
                    ins = t.matmul(accs[sc*2+ob], midT[0:1, NFC, sc*128:(sc+1)*128],
                                   bdnrow[0:1, ob*512:(ob+1)*512],
                                   start=False, stop=True)
            sPE.inc(ins)
        pe_dn = sPE.n

        for sc in range(3):
            @block.vector
            def _(v, e=e, sc=sc, pe_dn=pe_dn, dgn=dY.n):
                v.wait_ge(sPE.h, pe_dn)
                if sc == 0:
                    v.wait_ge(dY.h, dgn)
                for ob in range(2):
                    ins = v.tensor_scalar(yout3[:, sc, ob*512:(ob+1)*512],
                                          accs[sc*2+ob],
                                          gval[:, sc, e:e+1], None, ALU.mult)
                sVE.inc(ins)
        ve_y3 = sVE.n

        @block.gpsimd
        def _(g, e=e, ve_y3=ve_y3):
            g.wait_ge(sVE.h, ve_y3)
            g.wait_ge(dZ.h, ST_ZE)
            for sc in range(3):
                g.wait_ge(dY.h, dY.n)
                ins = g.indirect_dma_start(
                    out=y_dram.ap(),
                    out_offset=bass.IndirectOffsetOnAxis(ap=idxi[:, sc, e:e+1], axis=0),
                    in_=yout3[:, sc, :], in_offset=None,
                    compute_op=ALU.add)
                dY.inc(ins, 16)
    DG_FFN = dY.n

    xch = sbtC("xch", [128, 4, D], F32)
    ych = sbtC("ych", [128, 4, D], F32)
    och = sbtC("och", [128, 4, D], F32)

    for ch in range(4):
        @block.sync
        def _(s, ch=ch):
            s.wait_ge(dY.h, DG_FFN)
            if ch > 0:
                s.wait_ge(dA.h, dA.n)
            ins = s.dma_start(out=xch[:],
                              in_=x_in.rearrange("(a p) d -> p a d", p=128)
                              [:, ch*4:(ch+1)*4, :])
            dFX.inc(ins, 16)
            ins = s.dma_start(out=ych[:],
                              in_=y_dram.ap().rearrange("(a p) d -> p a d", p=128)
                              [:, ch*4:(ch+1)*4, :])
            dFX.inc(ins, 16)
        ds_f = dFX.n

        @block.vector
        def _(v, ch=ch, ds_f=ds_f):
            v.wait_ge(dFX.h, ds_f)
            v.wait_ge(sVE.h, VE_LN2)
            for j in range(4):
                v.wait_ge(sVE.h, sVE.n)
                sVE.inc(v.tensor_add(och[:, j, :], xch[:, j, :], ych[:, j, :]))
                v.wait_ge(sVE.h, sVE.n)
                ins = v.tensor_add(och[:, j, :], och[:, j, :],
                                   attn_tok[:, ch*4+j, :])
                sVE.inc(ins)
        ve_f = sVE.n

        @block.scalar
        def _(a, ch=ch, ve_f=ve_f):
            a.wait_ge(sVE.h, ve_f)
            ins = a.dma_start(out=out.rearrange("(a p) d -> p a d", p=128)
                              [:, ch*4:(ch+1)*4, :],
                              in_=och[:])
            dA.inc(ins, 16)

    @block.scalar
    def _(a):
        a.wait_ge(dA.h, dA.n)
        a.wait_ge(dY.h, dY.n)
        a.wait_ge(sCC.h, sCC.n)

    ctxC.close()
    ctx.close()
    return nc


def _host_prep(inputs):
    x = np.ascontiguousarray(np.asarray(inputs["L0_attn_x"], dtype=np.float32).reshape(NT, D))
    Wq = np.asarray(inputs["L0_attn_Wq"], np.float32)
    Wk = np.asarray(inputs["L0_attn_Wk"], np.float32)
    Wv = np.asarray(inputs["L0_attn_Wv"], np.float32)
    T = np.asarray(inputs["L0_attn_rel_bias_table"], np.float32)
    g2 = np.asarray(inputs["L0_attn_ln_gamma"], np.float32)
    b2 = np.asarray(inputs["L0_attn_ln_beta"], np.float32)
    Wr = np.asarray(inputs["L0_ffn_Wr"], np.float32)
    brv = np.asarray(inputs["L0_ffn_br"], np.float32)
    W_up = np.asarray(inputs["L0_ffn_W_up"], np.float32)
    b_up = np.asarray(inputs["L0_ffn_b_up"], np.float32)
    W_down = np.asarray(inputs["L0_ffn_W_down"], np.float32)
    b_down = np.asarray(inputs["L0_ffn_b_down"], np.float32)

    wqt = np.ascontiguousarray(Wq.T)
    wkt = np.ascontiguousarray(Wk.T)
    wvt = np.ascontiguousarray(Wv.T)
    wup = np.zeros((4, D, FPAD), np.float32); wup[:, :, :2730] = W_up
    bup = np.zeros((4, FPAD), np.float32); bup[:, :2730] = b_up
    wdn = np.zeros((NE, FPAD, D), np.float32); wdn[:, :2730, :] = W_down
    uu_, rr_ = np.meshgrid(np.arange(128), np.arange(128), indexing="ij")
    btail = np.zeros((H, 32, 128, 128), np.float32)
    for m_ in range(32):
        idxm = 4095 - 128 * m_ - uu_ + rr_
        valid = (idxm >= 0) & (idxm <= 4093)
        idxc = np.clip(idxm, 0, 4093)
        btail[:, m_, :, :] = np.where(valid[None], T[idxc, :].transpose(2, 0, 1), 0.0)
    crow = np.ascontiguousarray(np.repeat(T[4094, :], HD)[None, :].astype(np.float32))
    ident = np.eye(128, dtype=np.float32)
    uu, rr = np.meshgrid(np.arange(128), np.arange(128), indexing="ij")
    trishift = (uu <= rr + 1).astype(np.float32)
    onehotD = np.zeros((128, 128), np.float32); onehotD[0, 127] = 1.0
    lstrict = (uu < rr).astype(np.float32)
    onescol = np.ones((128, 1), np.float32)
    iota_slot = np.broadcast_to(np.arange(CAP, dtype=np.float32), (128, CAP)).copy()
    tloc_p = np.arange(128, dtype=np.float32)[:, None].copy()
    selmv = np.zeros((16, 16 * 128), np.float32)
    for a in range(16):
        selmv[a, a*128:(a+1)*128] = 1.0

    in_maps = []
    for c in range(NCORES):
        cp = c % 4
        C0 = 16 * cp + 32
        bexw = np.zeros((64, 16), np.float32)
        for a in range(16):
            bexw[:, a] = (np.arange(64) < C0 + a).astype(np.float32)
        vtrows = ((C0 + np.arange(VT_BLKS, dtype=np.int32))[None, :] * 128
                  + np.arange(128, dtype=np.int32)[:, None]).astype(np.int32).copy()
        selrows = (2048 * c + np.arange(16, dtype=np.int32)[None, :] * 128
                   + np.arange(128, dtype=np.int32)[:, None]).astype(np.int32).copy()
        in_maps.append(dict(
            x_loc=x[TLOC*c:TLOC*(c+1)],
            wqt=wqt, wkt=wkt, wvt=wvt, wr=Wr, br=brv[:, None].copy(),
            wup=wup, bup=bup, wdn=wdn, bdn=b_down,
            btail=btail, crow=crow,
            grow=np.ascontiguousarray(g2[None, :]),
            brow=np.ascontiguousarray(b2[None, :]),
            ident=ident, trishift=trishift, onehotD=onehotD,
            lstrict=lstrict, onescol=onescol, iota_slot=iota_slot,
            tloc_p=tloc_p, bexw=bexw.reshape(1, -1).copy(), selm=selmv,
            vtrows=vtrows, selrows=selrows,
        ))
    return in_maps


def kernel(**inputs):
    in_maps = _host_prep(inputs)
    nc = build_program()
    res = run_bass_kernel_spmd(nc, in_maps, core_ids=list(range(NCORES)))
    shards = [np.asarray(res.results[c]["out"]) for c in range(NCORES)]
    return np.concatenate(shards, axis=0).reshape(2, S, D)


# version 08 (final)
# speedup vs baseline: 1.6699x; 1.2436x over previous
import contextlib
import numpy as np
import ml_dtypes
import concourse.bass as bass
import concourse.mybir as mybir
from concourse.bass_utils import run_bass_kernel_spmd

F32 = mybir.dt.float32
BF16 = mybir.dt.bfloat16
I32 = mybir.dt.int32
AF = mybir.ActivationFunctionType
ALU = mybir.AluOpType
AX = mybir.AxisListType

NCORES = 8
S = 8192
D = 1024
H = 16
HD = 64
NT = 16384
TLOC = 2048
CAP = 384
NE = 8
FPAD = 2816
NFC = FPAD // 128
VPAD = 16384
EPS = 1e-5
NBIS = 31
KSEL = 2048
VT_BLKS = 32


def build_program(dbg_names=(), upto=99, skip=(), nhp=8, nkc=64):
    nc = bass.Bass("TRN2", target_bir_lowering=False)
    ctx = contextlib.ExitStack()
    dbg_names = set(dbg_names)
    dbg_tensors = {}

    def din(name, shape, dt=F32):
        return nc.dram_tensor(name, shape, dt, kind="ExternalInput").ap()

    x_in = din("x_loc", [TLOC, D])
    wqt_in = din("wqt", [D, D])
    wkt_in = din("wkt", [D, D])
    wvt_in = din("wvt", [D, D])
    wr_in = din("wr", [D, NE])
    br_in = din("br", [NE, 1])
    wup_in = din("wup", [4, D, FPAD])
    bup_in = din("bup", [4, FPAD])
    wdn_in = din("wdn", [NE, FPAD, D])
    bdn_in = din("bdn", [NE, D])
    btail_in = din("btail", [H, 32, 128, 128])
    crow_in = din("crow", [1, D])
    grow_in = din("grow", [1, D])
    brow_in = din("brow", [1, D])
    ident_in = din("ident", [128, 128])
    trishift_in = din("trishift", [128, 128])
    onehotD_in = din("onehotD", [128, 128])
    lstrict_in = din("lstrict", [128, 128])
    onescol_in = din("onescol", [128, 1])
    iota_slot_in = din("iota_slot", [128, CAP])
    tloc_p_in = din("tloc_p", [128, 1])
    bexw_in = din("bexw", [1, 64 * 16])
    selm_in = din("selm", [16, 16 * 128])
    vtrows_in = din("vtrows", [128, VT_BLKS], I32)
    selrows_in = din("selrows", [128, 16], I32)

    out = nc.dram_tensor("out", [TLOC, D], F32, kind="ExternalOutput").ap()

    xn_dram = nc.dram_tensor("xn_dram", [TLOC, D], BF16)
    k_loc = nc.dram_tensor("k_loc", [TLOC, D], BF16)
    v_loc = nc.dram_tensor("v_loc", [TLOC, D], BF16)
    k_full = nc.dram_tensor("k_full", [S, D], BF16)
    v_pad = nc.dram_tensor("v_pad", [VPAD, D], BF16)
    vt_dram = nc.dram_tensor("vt_dram", [VT_BLKS * 128, D], BF16)
    probs_dram = nc.dram_tensor("probs_dram", [TLOC, NE], F32)
    probs_all = nc.dram_tensor("probs_all", [NT, NE], F32)
    sel_dram = nc.dram_tensor("sel_dram", [NT, NE], F32)
    selloc_dram = nc.dram_tensor("selloc_dram", [TLOC, NE], F32)
    y_dram = nc.dram_tensor("y_dram", [TLOC, D], F32)

    sbt = lambda name, shape, dt: ctx.enter_context(nc.sbuf_tensor(name, shape, dt))

    ident_bf = sbt("c_identb", [128, 128], BF16)
    ident_f = sbt("c_identf", [128, 128], F32)
    trishift = sbt("c_trish", [128, 128], BF16)
    onehotD = sbt("c_ohD", [128, 128], BF16)
    lstrict = sbt("c_lstr", [128, 128], BF16)
    onescol = sbt("c_ones", [128, 1], BF16)
    onesrow_f = sbt("c_onesrf", [1, 128], F32)
    iota_slot = sbt("c_iotas", [128, CAP], F32)
    tloc_p = sbt("c_tlocp", [128, 1], BF16)
    bexw = sbt("c_bexw", [128, 64, 16], BF16)
    selm = sbt("c_selm", [16, 16 * 128], BF16)
    vtrows = sbt("c_vtrows", [128, VT_BLKS], I32)
    selrows = sbt("c_selrows", [128, 16], I32)
    ctile = sbt("c_ctile", [128, D], F32)
    gtile = sbt("c_gtile", [128, D], F32)
    btile = sbt("c_btile", [128, D], F32)
    brcol = sbt("c_brcol", [NE, 1], F32)

    stat_a = sbt("stat_a", [128, 16], F32)
    stat_b = sbt("stat_b", [128, 16], F32)
    stat_c = sbt("stat_c", [128, 16], F32)
    stat_d = sbt("stat_d", [128, 16], F32)
    stat_e = sbt("stat_e", [128, 16], F32)
    scr1k = sbt("scr1k", [128, D], F32)
    scr1kb = sbt("scr1kb", [128, D], BF16)
    idxi = sbt("idxi", [128, 3, NE], I32)
    gval = sbt("gval", [128, 3, NE], F32)
    attn_tok = sbt("attn_tok", [128, 16, D], BF16)

    class Sem:
        __slots__ = ("h", "n")
        def __init__(self, name):
            self.h = ctx.enter_context(nc.semaphore(name))
            self.n = 0
        def inc(self, instr, k=1):
            instr.then_inc(self.h, k)
            self.n += k
            return self.n

    dS = Sem("dS")
    dG = Sem("dG")
    dA = Sem("dA")
    dC = Sem("dC")
    dX = [Sem("dX0"), Sem("dX1"), Sem("dX2"), Sem("dX3")]
    dXN = [Sem("dXN0"), Sem("dXN1"), Sem("dXN2"), Sem("dXN3")]
    dKV = [Sem("dKV0"), Sem("dKV1")]
    dPR = Sem("dPR")
    dZ = Sem("dZ")
    dPG = Sem("dPG")
    dSEL = Sem("dSEL")
    dKT = [Sem("dKT0"), Sem("dKT1")]
    dVT = Sem("dVT")
    dBF = Sem("dBF")
    dPS = [Sem("dPS0"), Sem("dPS1"), Sem("dPS2"), Sem("dPS3")]
    dP2 = [Sem("dP20"), Sem("dP21"), Sem("dP22")]
    dFX = Sem("dFX")
    dVC = Sem("dVC")
    dSL = Sem("dSL")
    dGG = Sem("dGG")
    dBD = Sem("dBD")
    dWU = [Sem("dWU0"), Sem("dWU1")]
    dWD = [Sem("dWD0"), Sem("dWD1")]
    dXG = Sem("dXG")
    dY = Sem("dY")
    sPE = Sem("sPE")
    sACT = Sem("sACT")
    sVE = Sem("sVE")
    sPO = Sem("sPO")
    sCC = Sem("sCC")

    pS = ctx.enter_context(nc.psum_tensor("pS", [128, 2048], F32))
    pA = pS[:, 0:1024]
    pB = pS[:, 1024:2048]
    pC = ctx.enter_context(nc.psum_tensor("pC", [128, 512], F32))
    pD = ctx.enter_context(nc.psum_tensor("pD", [128, 512], F32))
    pBF = ctx.enter_context(nc.psum_tensor("pBF", [128, 1024], BF16))
    pBF2 = ctx.enter_context(nc.psum_tensor("pBF2", [128, 1024], BF16))

    block = ctx.enter_context(nc.Block())

    ctxQ = contextlib.ExitStack()
    qt = ctxQ.enter_context(nc.sbuf_tensor("qt", [128, 8, TLOC], BF16))

    ctxP = contextlib.ExitStack()
    sbtP = lambda name, shape, dt: ctxP.enter_context(nc.sbuf_tensor(name, shape, dt))
    xrot = sbtP("xrot", [128, 4, D], F32)
    xnrot = sbtP("xnrot", [128, 4, D], BF16)
    xnT = sbtP("xnT", [128, 8, TLOC], BF16)
    wbuf = sbtP("wbuf", [128, 8, D], BF16)
    wr_sb = sbtP("wr_sb", [128, 8, NE], BF16)
    kv_ev = sbtP("kv_ev", [128, 2, 512], BF16)
    logit_sb = sbtP("logit_sb", [NE, TLOC], F32)
    ltok = sbtP("ltok", [128, 16, NE], F32)
    ptok = sbtP("ptok", [128, 16, NE], F32)
    lred = sbtP("lred", [128, 16], F32)
    zt = sbtP("zt", [128, 1024], BF16)
    zf32 = sbtP("zf32", [128, 1024], F32)

    @block.sync
    def _(s):
        dC.inc(s.dma_start(out=ident_f[:], in_=ident_in), 16)
        dC.inc(s.dma_start(out=onesrow_f[:],
                           in_=onescol_in.rearrange("p o -> o p")[0:1, 0:128]), 16)
        dC.inc(s.dma_start(out=iota_slot[:], in_=iota_slot_in), 16)
        dC.inc(s.dma_start(out=ctile[:], in_=crow_in.partition_broadcast(128)), 16)
        dC.inc(s.dma_start(out=gtile[:], in_=grow_in.partition_broadcast(128)), 16)
        dC.inc(s.dma_start(out=btile[:], in_=brow_in.partition_broadcast(128)), 16)
    LD_X = dC.n

    @block.gpsimd
    def _(g):
        dG.inc(g.dma_start(out=ident_bf[:], in_=ident_in), 16)
        dG.inc(g.dma_start(out=trishift[:], in_=trishift_in), 16)
        dG.inc(g.dma_start(out=onehotD[:], in_=onehotD_in), 16)
        dG.inc(g.dma_start(out=lstrict[:], in_=lstrict_in), 16)
        dG.inc(g.dma_start(out=onescol[:], in_=onescol_in), 16)
        dG.inc(g.dma_start(out=tloc_p[:], in_=tloc_p_in), 16)
        dG.inc(g.dma_start(out=bexw[:], in_=bexw_in.rearrange("o (k a) -> o k a", a=16)
                           .partition_broadcast(128)), 16)
        dG.inc(g.dma_start(out=selm[:], in_=selm_in), 16)
        dG.inc(g.dma_start(out=vtrows[:], in_=vtrows_in), 16)
        dG.inc(g.dma_start(out=selrows[:], in_=selrows_in), 16)
        dG.inc(g.dma_start(out=brcol[:], in_=br_in), 16)
        dG.inc(g.dma_start(out=wbuf[:], in_=wqt_in.rearrange("(c p) o -> p c o", p=128)), 16)
        dG.inc(g.dma_start(out=wr_sb[:], in_=wr_in.rearrange("(c p) o -> p c o", p=128)), 16)
    LD_W = dG.n

    xv = x_in.rearrange("(a p) d -> p a d", p=128)
    DS_L0 = dS.n
    VE_L0s = sVE.n
    ACT_L0 = sACT.n

    for i in range(16):
        @block.sync
        def _(s, i=i):
            if i == 0:
                s.wait_ge(dC.h, LD_X)
            if i >= 4:
                s.wait_ge(sVE.h, VE_L0s + (i - 3))
                s.wait_ge(sACT.h, ACT_L0 + (i - 3))
            ins = s.dma_start(out=xrot[:, i % 4, :], in_=xv[:, i, :])
            dX[i % 4].inc(ins, 16)
        dsn = dX[i % 4].n

        @block.vector
        def _(v, i=i, dsn=dsn):
            v.wait_ge(dX[i % 4].h, dsn)
            sVE.inc(v.reduce_sum(stat_a[:, i:i+1], xrot[:, i % 4, :],
                                 axis=AX.X, negate=True))

        @block.scalar
        def _(a, i=i, dsn=dsn):
            a.wait_ge(dX[i % 4].h, dsn)
            if i > 0:
                a.wait_ge(sACT.h, sACT.n)
            sACT.inc(a.activation(scr1k[:], xrot[:, i % 4, :], AF.Square,
                                  accum_out=stat_b[:, i:i+1]))
    VE_NS = sVE.n
    ACT_SQ = sACT.n

    @block.vector
    def _(v):
        v.wait_ge(sACT.h, ACT_SQ)
        v.wait_ge(sVE.h, VE_NS)
        sVE.inc(v.tensor_scalar(stat_c[:], stat_a[:], 1.0 / D, None, ALU.mult))
        v.wait_ge(sVE.h, sVE.n)
        sVE.inc(v.tensor_mul(stat_d[:], stat_c[:], stat_c[:]))
        v.wait_ge(sVE.h, sVE.n)
        sVE.inc(v.tensor_scalar(stat_a[:], stat_b[:], 1.0 / D, None, ALU.mult))
        v.wait_ge(sVE.h, sVE.n)
        sVE.inc(v.tensor_sub(stat_d[:], stat_a[:], stat_d[:]))
    VE_VAR = sVE.n

    @block.vector
    def _(v):
        v.wait_ge(sVE.h, VE_VAR)
        sVE.inc(v.tensor_scalar(stat_d[:], stat_d[:], EPS, None, ALU.add))
    VE_VEPS = sVE.n

    @block.scalar
    def _(a):
        a.wait_ge(sVE.h, VE_VEPS)
        sACT.inc(a.activation(stat_d[:], stat_d[:], AF.Ln))
        a.wait_ge(sACT.h, sACT.n)
        sACT.inc(a.activation(stat_d[:], stat_d[:], AF.Exp, scale=-0.5))
    ACT_RSTD = sACT.n

    @block.vector
    def _(v):
        v.wait_ge(sACT.h, ACT_RSTD)
        sVE.inc(v.tensor_mul(stat_e[:], stat_c[:], stat_d[:]))
    VE_MRS = sVE.n

    DS_L2 = dS.n
    ACT_X0 = sACT.n
    PE_T0 = sPE.n
    VE_T0 = sVE.n
    for i in range(16):
        @block.sync
        def _(s, i=i):
            if i < 4:
                s.wait_ge(sACT.h, ACT_SQ)
                s.wait_ge(sVE.h, VE_NS)
            else:
                s.wait_ge(sACT.h, ACT_X0 + (i - 3))
            ins = s.dma_start(out=xrot[:, i % 4, :], in_=xv[:, i, :])
            dX[i % 4].inc(ins, 16)
        dsn = dX[i % 4].n

        @block.scalar
        def _(a, i=i, dsn=dsn, dxn_w=(dXN[i % 4].n if i >= 4 else 0)):
            a.wait_ge(dX[i % 4].h, dsn)
            if i == 0:
                a.wait_ge(sVE.h, VE_MRS)
            if i >= 4:
                a.wait_ge(sPE.h, PE_T0 + (i - 3) * 8)
                a.wait_ge(dXN[i % 4].h, dxn_w)
            sACT.inc(a.activation(xnrot[:, i % 4, :], xrot[:, i % 4, :], AF.Identity,
                                  bias=stat_e[:, i:i+1], scale=stat_d[:, i:i+1]))
        acn = sACT.n

        @block.sync
        def _(s, i=i, acn=acn):
            s.wait_ge(sACT.h, acn)
            ins = s.dma_start(out=xn_dram.ap().rearrange("(a p) d -> p a d", p=128)[:, i, :],
                              in_=xnrot[:, i % 4, :])
            dXN[i % 4].inc(ins, 16)

        @block.tensor
        def _(t, i=i, acn=acn):
            t.wait_ge(sACT.h, acn)
            if i == 0:
                t.wait_ge(dG.h, LD_W)
            for c in range(8):
                k = i * 8 + c
                buf = pBF if k % 2 == 0 else pBF2
                if k >= 2:
                    t.wait_ge(sVE.h, VE_T0 + k - 1)
                sPE.inc(t.transpose(buf[:, 0:128], xnrot[:, i % 4, c*128:(c+1)*128],
                                    ident_bf[:]))

        @block.vector
        def _(v, i=i):
            for c in range(8):
                k = i * 8 + c
                buf = pBF if k % 2 == 0 else pBF2
                v.wait_ge(sPE.h, PE_T0 + k + 1)
                sVE.inc(v.tensor_copy(xnT[:, c, i*128:(i+1)*128], buf[:, 0:128]))
    PE_XNT, VE_XNT = sPE.n, sVE.n
    ST_XN = dS.n

    PE_P0, VE_P0 = sPE.n, sVE.n

    @block.tensor
    def _(t):
        t.wait_ge(sVE.h, VE_XNT)
        t.wait_ge(dG.h, LD_W)
        for n in range(32):
            dc, tb = divmod(n, 4)
            buf = pC if n % 2 == 0 else pD
            if n >= 2:
                t.wait_ge(sVE.h, VE_P0 + n - 1)
            for k in range(8):
                ins = t.matmul(buf[:], wbuf[:, k, dc*128:(dc+1)*128],
                               xnT[:, k, tb*512:(tb+1)*512],
                               start=(k == 0), stop=(k == 7))
            sPE.inc(ins)

    @block.vector
    def _(v):
        for n in range(32):
            dc, tb = divmod(n, 4)
            buf = pC if n % 2 == 0 else pD
            v.wait_ge(sPE.h, PE_P0 + n + 1)
            sVE.inc(v.tensor_copy(qt[:, dc, tb*512:(tb+1)*512], buf[:]))
    PE_QT, VE_QT = sPE.n, sVE.n

    @block.gpsimd
    def _(g):
        g.wait_ge(sPE.h, PE_QT)
        ins = g.dma_start(out=wbuf[:], in_=wkt_in.rearrange("(c p) o -> p c o", p=128))
        dG.inc(ins, 16)
    LD_WK = dG.n

    def kv_pass(tgt, ld_wait):
        pe0, ve0 = sPE.n, sVE.n
        dkv0 = [dKV[0].n, dKV[1].n]

        @block.tensor
        def _(t):
            t.wait_ge(dG.h, ld_wait)
            t.wait_ge(sVE.h, ve0)
            for n in range(32):
                tc, ob = divmod(n, 2)
                buf = pC if n % 2 == 0 else pD
                if n >= 2:
                    t.wait_ge(sVE.h, ve0 + n - 1)
                for k in range(8):
                    ins = t.matmul(buf[:], xnT[:, k, tc*128:(tc+1)*128],
                                   wbuf[:, k, ob*512:(ob+1)*512],
                                   start=(k == 0), stop=(k == 7))
                sPE.inc(ins)

        @block.vector
        def _(v):
            for n in range(32):
                buf = pC if n % 2 == 0 else pD
                v.wait_ge(sPE.h, pe0 + n + 1)
                if n >= 2:
                    v.wait_ge(dKV[n % 2].h, dkv0[n % 2] + (n - 2) // 2 * 16 + 16)
                else:
                    v.wait_ge(dKV[n % 2].h, dkv0[n % 2])
                sVE.inc(v.tensor_copy(kv_ev[:, n % 2, :], buf[:]))

        @block.sync
        def _(s):
            for n in range(32):
                tc, ob = divmod(n, 2)
                s.wait_ge(sVE.h, ve0 + n + 1)
                ins = s.dma_start(out=tgt.ap()[tc*128:(tc+1)*128, ob*512:(ob+1)*512],
                                  in_=kv_ev[:, n % 2, :])
                dKV[n % 2].inc(ins, 16)

    kv_pass(k_loc, LD_WK)
    PE_K = sPE.n

    @block.gpsimd
    def _(g):
        g.wait_ge(sPE.h, PE_K)
        ins = g.dma_start(out=wbuf[:], in_=wvt_in.rearrange("(c p) o -> p c o", p=128))
        dG.inc(ins, 16)
    LD_WV = dG.n

    kv_pass(v_loc, LD_WV)
    ST_KV0, ST_KV1 = dKV[0].n, dKV[1].n

    PE_R0, ACT_R0 = sPE.n, sACT.n

    @block.tensor
    def _(t):
        for tb in range(4):
            if tb >= 1:
                t.wait_ge(sACT.h, ACT_R0 + tb)
            for k in range(8):
                ins = t.matmul(pA[0:8, 0:512], wr_sb[:, k, :],
                               xnT[:, k, tb*512:(tb+1)*512],
                               start=(k == 0), stop=(k == 7))
            sPE.inc(ins)

    @block.scalar
    def _(a):
        for tb in range(4):
            a.wait_ge(sPE.h, PE_R0 + tb + 1)
            sACT.inc(a.activation(logit_sb[:, tb*512:(tb+1)*512], pA[0:8, 0:512],
                                  AF.Identity, bias=brcol[:]))
    ACT_RT = sACT.n

    PE_L0, VE_L0 = sPE.n, sVE.n

    @block.tensor
    def _(t):
        t.wait_ge(sACT.h, ACT_RT)
        for tc in range(16):
            if tc >= 1:
                t.wait_ge(sVE.h, VE_L0 + tc)
            sPE.inc(t.transpose(pB[:, 0:8], logit_sb[:, tc*128:(tc+1)*128],
                                ident_f[0:8, 0:8]))

    @block.vector
    def _(v):
        for tc in range(16):
            v.wait_ge(sPE.h, PE_L0 + tc + 1)
            sVE.inc(v.tensor_copy(ltok[:, tc, :], pB[:, 0:8]))
        v.wait_ge(sVE.h, sVE.n)
        for i in range(16):
            ins = v.reduce_max(lred[:, i:i+1], ltok[:, i, :], axis=AX.X, negate=True)
        sVE.inc(ins)
    VE_LT = sVE.n

    @block.scalar
    def _(a):
        a.wait_ge(sVE.h, VE_LT)
        for i in range(16):
            ins = a.activation(ptok[:, i, :], ltok[:, i, :], AF.Exp, bias=lred[:, i:i+1])
        sACT.inc(ins)
    ACT_PEXP = sACT.n

    @block.vector
    def _(v):
        v.wait_ge(sACT.h, ACT_PEXP)
        for i in range(16):
            ins = v.reduce_sum(lred[:, i:i+1], ptok[:, i, :], axis=AX.X)
        sVE.inc(ins)
        v.wait_ge(sVE.h, sVE.n)
        sVE.inc(v.reciprocal(lred[:], lred[:]))
        v.wait_ge(sVE.h, sVE.n)
        for i in range(16):
            ins = v.tensor_scalar(ptok[:, i, :], ptok[:, i, :], lred[:, i:i+1],
                                  None, ALU.mult)
        sVE.inc(ins)
    VE_PROB = sVE.n

    @block.sync
    def _(s):
        s.wait_ge(sVE.h, VE_PROB)
        ins = s.dma_start(out=probs_dram.ap().rearrange("(a p) e -> p a e", p=128),
                          in_=ptok[:])
        dPR.inc(ins, 16)
    ST_PROBS = dPR.n

    @block.gpsimd
    def _(g):
        g.wait_ge(dKV[0].h, ST_KV0)
        g.wait_ge(dKV[1].h, ST_KV1)
        sCC.inc(g.collective_compute(
            "AllGather", ALU.bypass,
            replica_groups=[[0, 1, 2, 3], [4, 5, 6, 7]],
            ins=[k_loc.ap().opt()], outs=[k_full.ap().opt()]))
        sCC.inc(g.collective_compute(
            "AllGather", ALU.bypass,
            replica_groups=[[0, 1, 2, 3], [4, 5, 6, 7]],
            ins=[v_loc.ap().opt()], outs=[v_pad.ap()[0:S, :].opt()]))
        g.wait_ge(dPR.h, ST_PROBS)
        sCC.inc(g.collective_compute(
            "AllGather", ALU.bypass,
            replica_groups=[list(range(8))],
            ins=[probs_dram.ap().opt()], outs=[probs_all.ap().opt()]))
    CC_KV = 2
    CC_ALL = sCC.n

    @block.vector
    def _(v):
        sVE.inc(v.memset(zt[:], 0.0))
        sVE.inc(v.memset(zf32[:], 0.0))
    VE_ZT = sVE.n

    @block.sync
    def _(s):
        s.wait_ge(sVE.h, VE_ZT)
        for blk in range(16):
            ins = s.dma_start(
                out=v_pad.ap()[S + blk*512:S + (blk+1)*512, :]
                    .rearrange("(n p) d -> p n d", p=128),
                in_=zt[:].unsqueeze(1).to_broadcast([128, 4, 1024]))
            dZ.inc(ins, 16)
        for blk in range(4):
            ins = s.dma_start(
                out=y_dram.ap()[blk*512:(blk+1)*512, :]
                    .rearrange("(n p) d -> p n d", p=128),
                in_=zf32[:].unsqueeze(1).to_broadcast([128, 4, 1024]))
            dZ.inc(ins, 16)
    ST_ZE = dZ.n

    vtcp = sbtP("vtcp", [128, 2, D], BF16)
    @block.gpsimd
    def _(g):
        g.wait_ge(sCC.h, CC_KV)
        g.wait_ge(dZ.h, ST_ZE)
        g.wait_ge(dG.h, dG.n)
        for b in range(VT_BLKS):
            ins = g.indirect_dma_start(
                out=vtcp[:, b % 2, :], out_offset=None,
                in_=v_pad.ap(),
                in_offset=bass.IndirectOffsetOnAxis(ap=vtrows[:, b:b+1], axis=0))
            dVC.inc(ins, 16)
            g.wait_ge(dVC.h, dVC.n)
            ins = g.dma_start(out=vt_dram.ap()[b*128:(b+1)*128, :],
                              in_=vtcp[:, b % 2, :])
            dVC.inc(ins, 16)
        g.wait_ge(dVC.h, dVC.n)
    CP_VT = dVC.n

    ctxP.close()
    ctxS = contextlib.ExitStack()
    sbtS = lambda name, shape, dt: ctxS.enter_context(nc.sbuf_tensor(name, shape, dt))
    pg = sbtS("pg", [128, 128, NE], F32)
    b_lo = sbtS("b_lo", [128, NE], I32)
    b_hi = sbtS("b_hi", [128, NE], I32)
    b_mid = sbtS("b_mid", [128, NE], I32)
    b_t1 = sbtS("b_t1", [128, NE], I32)
    b_t2 = sbtS("b_t2", [128, NE], I32)
    b_ge = sbtS("b_ge", [128, NE], F32)
    b_gei = sbtS("b_gei", [128, NE], I32)
    b_nge = sbtS("b_nge", [128, NE], I32)
    b_cntb = sbtS("b_cntb", [128, NE], BF16)
    b_cnt8 = sbtS("b_cnt8", [NE, 1], F32)
    b_cntrow = sbtS("b_cntrow", [1, NE], F32)
    b_cntall = sbtS("b_cntall", [128, NE], F32)
    b_maskb = sbtS("b_maskb", [128, 128, NE], BF16)
    b_rtake = sbtS("b_rtake", [128, NE], F32)
    b_theta = sbtS("b_theta", [128, NE], F32)
    b_one = sbtS("b_one", [128, NE], I32)
    b_kself = sbtS("b_kself", [128, NE], F32)

    @block.sync
    def _(s):
        s.wait_ge(sCC.h, CC_ALL)
        s.wait_ge(dVC.h, CP_VT)
        for j in range(4):
            s.wait_ge(dXN[j].h, dXN[j].n)
        s.wait_ge(dKV[0].h, dKV[0].n)
        s.wait_ge(dKV[1].h, dKV[1].n)
        ins = s.dma_start(out=pg[:],
                          in_=probs_all.ap().rearrange("(p a) e -> p a e", p=128))
        dPG.inc(ins, 16)
    LD_PG = dPG.n

    @block.gpsimd
    def _(g):
        g.wait_ge(dPG.h, LD_PG)
        sPO.inc(g.engine_nop())
    PO_INIT = sPO.n

    @block.vector
    def _(v):
        v.wait_ge(dPG.h, LD_PG)
        sVE.inc(v.memset(b_lo[:], 0))
        sVE.inc(v.memset(b_kself[:], float(KSEL)))
    VE_INIT = sVE.n

    def count_round(op, thr_ap):
        po_c = sPO.n

        @block.vector
        def _(v):
            v.wait_ge(sPO.h, po_c)
            v.wait_ge(sVE.h, sVE.n)
            sVE.inc(v.tensor_tensor(
                b_maskb[:], pg[:],
                thr_ap.unsqueeze(1).to_broadcast([128, 128, NE]), op))
            v.wait_ge(sVE.h, sVE.n)
            with nc.allow_low_precision(reason="counts <=128 are bf16-exact"):
                sVE.inc(v.tensor_reduce(b_cntb[:], b_maskb[:].rearrange("p a e -> p e a"),
                                        axis=AX.X, op=ALU.add))
        ve_c = sVE.n

        @block.tensor
        def _(t):
            t.wait_ge(sVE.h, ve_c)
            sPE.inc(t.matmul(pA[0:8, 0:1], b_cntb[:], onescol[:], start=True, stop=True))
        pe1 = sPE.n

        @block.scalar
        def _(a):
            a.wait_ge(sPE.h, pe1)
            sACT.inc(a.activation(b_cnt8[:], pA[0:8, 0:1], AF.Copy))
        ac1 = sACT.n

        @block.tensor
        def _(t):
            t.wait_ge(sACT.h, ac1)
            sPE.inc(t.transpose(pA[0:1, 8:16], b_cnt8[:], ident_f[0:8, 0:8]))
        pe2 = sPE.n

        @block.scalar
        def _(a):
            a.wait_ge(sPE.h, pe2)
            sACT.inc(a.activation(b_cntrow[:], pA[0:1, 8:16], AF.Copy))
        ac2 = sACT.n

        @block.tensor
        def _(t):
            t.wait_ge(sACT.h, ac2)
            sPE.inc(t.matmul(pB[:, 0:8], onesrow_f[:], b_cntrow[:], start=True, stop=True))
        pe3 = sPE.n

        @block.scalar
        def _(a):
            a.wait_ge(sPE.h, pe3)
            sACT.inc(a.activation(b_cntall[:], pB[:, 0:8], AF.Copy))
        return sACT.n

    for r_ in range(30):
        kbit = 29 - r_
        @block.vector
        def _(v, kbit=kbit, first=(r_ == 0)):
            if first:
                v.wait_ge(sPO.h, PO_INIT)
            v.wait_ge(sVE.h, sVE.n)
            sVE.inc(v.memset(b_t1[:], 1 << kbit))
            v.wait_ge(sVE.h, sVE.n)
            sVE.inc(v.tensor_tensor(b_mid[:], b_lo[:], b_t1[:], ALU.bitwise_or))
        actn = count_round(ALU.is_ge, b_mid[:].bitcast(F32))
        @block.vector
        def _(v, actn=actn):
            v.wait_ge(sACT.h, actn)
            v.wait_ge(sVE.h, sVE.n)
            sVE.inc(v.tensor_tensor(b_ge[:], b_cntall[:], b_kself[:], ALU.is_ge))
            v.wait_ge(sVE.h, sVE.n)
            sVE.inc(v.tensor_copy(b_gei[:], b_ge[:]))
            v.wait_ge(sVE.h, sVE.n)
            sVE.inc(v.select(b_lo[:], b_gei[:], b_mid[:], b_lo[:], add_drain=True))

    @block.vector
    def _(v):
        v.wait_ge(sVE.h, sVE.n)
        sVE.inc(v.tensor_copy(b_theta[:], b_lo[:].bitcast(F32)))
    actn = count_round(ALU.is_gt, b_theta[:])
    @block.vector
    def _(v, actn=actn):
        v.wait_ge(sACT.h, actn)
        v.wait_ge(sVE.h, sVE.n)
        sVE.inc(v.tensor_tensor(b_rtake[:], b_kself[:], b_cntall[:], ALU.subtract))
    PO_RT = sPO.n
    VE_RT = sVE.n

    g_gt = sbtS("g_gt", [128, 128, NE], BF16)
    g_tie = sbtS("g_tie", [128, 128, NE], BF16)
    g_scan = sbtS("g_scan", [128, 128, NE], F32)
    g_cnt = sbtS("g_cnt", [128, NE], BF16)
    g_base = sbtS("g_base", [128, NE], F32)
    g_sel = sbtS("g_sel", [128, 128, NE], F32)

    @block.vector
    def _(v):
        v.wait_ge(sVE.h, VE_RT)
        v.wait_ge(sVE.h, sVE.n)
        th = b_theta[:].unsqueeze(1).to_broadcast([128, 128, NE])
        sVE.inc(v.tensor_tensor(g_gt[:], pg[:], th, ALU.is_gt))
        sVE.inc(v.tensor_tensor(g_tie[:], pg[:], th, ALU.is_equal))
    PO_TIE = sPO.n
    VE_TIE = sVE.n

    @block.vector
    def _(v):
        v.wait_ge(sVE.h, VE_TIE)
        with nc.allow_low_precision(reason="counts <=128 are bf16-exact"):
            sVE.inc(v.tensor_reduce(g_cnt[:], g_tie[:].rearrange("p a e -> p e a"),
                                    axis=AX.X, op=ALU.add))
    VE_GCNT = sVE.n

    @block.tensor
    def _(t):
        t.wait_ge(sVE.h, VE_GCNT)
        sPE.inc(t.matmul(pA[:, 0:8], lstrict[:], g_cnt[:], start=True, stop=True))
    PE_TP = sPE.n

    @block.scalar
    def _(a):
        a.wait_ge(sPE.h, PE_TP)
        sACT.inc(a.activation(g_base[:], pA[:, 0:8], AF.Copy))
    ACT_TB = sACT.n

    @block.vector
    def _(v):
        v.wait_ge(sVE.h, VE_TIE)
        for e in range(NE):
            ins = v.tensor_tensor_scan(
                g_scan[:].rearrange("p a e -> p e a")[:, e, :],
                g_tie[:].rearrange("p a e -> p e a")[:, e, :],
                g_tie[:].rearrange("p a e -> p e a")[:, e, :],
                0.0, ALU.add, ALU.bypass)
        sVE.inc(ins)
        v.wait_ge(sVE.h, sVE.n)
        sVE.inc(v.tensor_sub(g_scan[:], g_scan[:], g_tie[:]))
        v.wait_ge(sVE.h, sVE.n)
        v.wait_ge(sACT.h, ACT_TB)
        for e in range(NE):
            ins = v.tensor_scalar(g_scan[:].rearrange("p a e -> p e a")[:, e, :],
                                  g_scan[:].rearrange("p a e -> p e a")[:, e, :],
                                  g_base[:, e:e+1], None, ALU.add)
        sVE.inc(ins)
        v.wait_ge(sVE.h, sVE.n)
        for e in range(NE):
            ins = v.tensor_scalar(g_scan[:].rearrange("p a e -> p e a")[:, e, :],
                                  g_scan[:].rearrange("p a e -> p e a")[:, e, :],
                                  b_rtake[:, e:e+1], None, ALU.is_lt)
        sVE.inc(ins)
        v.wait_ge(sVE.h, sVE.n)
        sVE.inc(v.tensor_mul(g_scan[:], g_scan[:], g_tie[:]))
        v.wait_ge(sVE.h, sVE.n)
        sVE.inc(v.tensor_add(g_sel[:], g_scan[:], g_gt[:]))
    VE_SEL = sVE.n

    @block.sync
    def _(s):
        s.wait_ge(sVE.h, VE_SEL)
        ins = s.dma_start(out=sel_dram.ap().rearrange("(p a) e -> p a e", p=128),
                          in_=g_sel[:])
        dSEL.inc(ins, 16)
    ST_SEL = dSEL.n



    selL = sbtS("selL", [128, 16, NE], F32)
    slot2 = sbtS("slot2", [128, 16, NE], F32)
    lcnt = sbtS("lcnt", [128, NE], BF16)
    lbase = sbtS("lbase", [128, NE], F32)

    @block.gpsimd
    def _(g):
        g.wait_ge(dSEL.h, ST_SEL)
        for a_ in range(16):
            ins = g.indirect_dma_start(
                out=selL[:, a_, :], out_offset=None,
                in_=sel_dram.ap(),
                in_offset=bass.IndirectOffsetOnAxis(ap=selrows[:, a_:a_+1], axis=0))
            dSL.inc(ins, 16)
    LD_SELL_G = dSL.n

    @block.vector
    def _(v):
        v.wait_ge(dSL.h, LD_SELL_G)
        with nc.allow_low_precision(reason="counts <=16 are bf16-exact"):
            sVE.inc(v.tensor_reduce(lcnt[:], selL[:].rearrange("p a e -> p e a"),
                                    axis=AX.X, op=ALU.add))
    VE_LC = sVE.n

    @block.tensor
    def _(t):
        t.wait_ge(sVE.h, VE_LC)
        sPE.inc(t.matmul(pA[:, 0:8], lstrict[:], lcnt[:], start=True, stop=True))
    PE_LB = sPE.n

    @block.scalar
    def _(a):
        a.wait_ge(sPE.h, PE_LB)
        sACT.inc(a.activation(lbase[:], pA[:, 0:8], AF.Copy))
    ACT_LB = sACT.n

    @block.vector
    def _(v):
        v.wait_ge(dSL.h, LD_SELL_G)
        for e in range(NE):
            ins = v.tensor_tensor_scan(
                slot2[:].rearrange("p a e -> p e a")[:, e, :],
                selL[:].rearrange("p a e -> p e a")[:, e, :],
                selL[:].rearrange("p a e -> p e a")[:, e, :],
                0.0, ALU.add, ALU.bypass)
        sVE.inc(ins)
        v.wait_ge(sVE.h, sVE.n)
        sVE.inc(v.tensor_sub(slot2[:], slot2[:], selL[:]))
        v.wait_ge(sVE.h, sVE.n)
        v.wait_ge(sACT.h, ACT_LB)
        for e in range(NE):
            ins = v.tensor_scalar(slot2[:].rearrange("p a e -> p e a")[:, e, :],
                                  slot2[:].rearrange("p a e -> p e a")[:, e, :],
                                  lbase[:, e:e+1], None, ALU.add)
        sVE.inc(ins)
        v.wait_ge(sVE.h, sVE.n)
        sVE.inc(v.tensor_scalar(g_scan[:, 0:16, :], slot2[:], float(CAP), None, ALU.is_lt))
        v.wait_ge(sVE.h, sVE.n)
        sVE.inc(v.tensor_mul(selL[:], selL[:], g_scan[:, 0:16, :]))
        v.wait_ge(sVE.h, sVE.n)
        sVE.inc(v.tensor_scalar(slot2[:], slot2[:], 9999.0, None, ALU.subtract))
        v.wait_ge(sVE.h, sVE.n)
        sVE.inc(v.tensor_mul(slot2[:], slot2[:], selL[:]))
        v.wait_ge(sVE.h, sVE.n)
        sVE.inc(v.tensor_scalar(slot2[:], slot2[:], 9999.0, None, ALU.add))
    VE_SLOT = sVE.n

    oh = sbtS("oh", [128, 16, CAP], BF16)
    rhsmm = sbtS("rhsmm", [128, 16, 3], BF16)
    pgrow = sbtS("pgrow", [128, NE], F32)
    idxf = sbtS("idxf", [128, 1], F32)
    idx3 = sbtS("idx3", [128, 3], F32)

    @block.vector
    def _(v):
        v.wait_ge(sVE.h, VE_SLOT)
        for a_ in range(16):
            v.tensor_copy(rhsmm[:, a_, 0:1], tloc_p[:])
            v.memset(rhsmm[:, a_, 1:2], float(a_))
        ins = v.memset(rhsmm[:, :, 2:3], 1.0)
        sVE.inc(ins)
    VE_RHS = sVE.n

    for e in range(NE):
        ve_oh0 = sVE.n
        @block.vector
        def _(v, e=e):
            if e == 0:
                v.wait_ge(sVE.h, VE_RHS)
            else:
                v.wait_ge(sPE.h, sPE.n)
            for a_ in range(16):
                ins = v.tensor_scalar(oh[:, a_, :], iota_slot[:], slot2[:, a_, e:e+1],
                                      None, ALU.is_equal)
            sVE.inc(ins)
        ve_oh = sVE.n

        for sc in range(3):
            @block.tensor
            def _(t, e=e, sc=sc, ve_oh=ve_oh):
                t.wait_ge(sVE.h, ve_oh)
                if sc > 0:
                    t.wait_ge(sVE.h, sVE.n)
                for a_ in range(16):
                    ins = t.matmul(pC[:, 0:3], oh[:, a_, sc*128:(sc+1)*128],
                                   rhsmm[:, a_, :], start=(a_ == 0), stop=(a_ == 15))
                sPE.inc(ins)
            pen = sPE.n

            @block.vector
            def _(v, e=e, sc=sc, pen=pen):
                v.wait_ge(sPE.h, pen)
                sVE.inc(v.tensor_copy(idx3[:], pC[:, 0:3]))
                v.wait_ge(sVE.h, sVE.n)
                sVE.inc(v.scalar_tensor_tensor(idxf[:], idx3[:, 1:2], 128.0, idx3[:, 0:1],
                                               ALU.mult, ALU.add))
                v.wait_ge(sVE.h, sVE.n)
                sVE.inc(v.tensor_copy(idxi[:, sc, e:e+1], idxf[:]))
                sVE.inc(v.tensor_copy(gval[:, sc, e:e+1], idx3[:, 2:3]))
    VE_IDX = sVE.n
    PE_IDX = sPE.n

    for e in range(NE):
        for sc in range(3):
            @block.gpsimd
            def _(g, e=e, sc=sc):
                if e == 0 and sc == 0:
                    g.wait_ge(sVE.h, VE_IDX)
                g.wait_ge(sVE.h, sVE.n)
                ins = g.indirect_dma_start(
                    out=pgrow[:], out_offset=None,
                    in_=probs_dram.ap(),
                    in_offset=bass.IndirectOffsetOnAxis(ap=idxi[:, sc, e:e+1], axis=0))
                dGG.inc(ins, 16)
            dgn = dGG.n

            @block.vector
            def _(v, e=e, sc=sc, dgn=dgn):
                v.wait_ge(dGG.h, dgn)
                sVE.inc(v.tensor_mul(gval[:, sc, e:e+1], gval[:, sc, e:e+1],
                                     pgrow[:, e:e+1]))
    VE_GV = sVE.n
    DG_SEL_END = dGG.n

    if "sel" in dbg_names:
        od = nc.dram_tensor("dbg_sel", [128, 16, NE], F32, kind="ExternalOutput").ap()
        @block.scalar
        def _(a):
            a.wait_ge(sVE.h, VE_GV)
            dA.inc(a.dma_start(out=od, in_=selL[:]), 16)
        od2 = nc.dram_tensor("dbg_slot", [128, 16, NE], F32, kind="ExternalOutput").ap()
        @block.scalar
        def _(a):
            a.wait_ge(sVE.h, VE_GV)
            dA.inc(a.dma_start(out=od2, in_=slot2[:]), 16)
        od3 = nc.dram_tensor("dbg_idx", [128, 3, NE], I32, kind="ExternalOutput").ap()
        @block.scalar
        def _(a):
            a.wait_ge(sVE.h, VE_GV)
            dA.inc(a.dma_start(out=od3, in_=idxi[:]), 16)
        od4 = nc.dram_tensor("dbg_gval", [128, 3, NE], F32, kind="ExternalOutput").ap()
        @block.scalar
        def _(a):
            a.wait_ge(sVE.h, VE_GV)
            dA.inc(a.dma_start(out=od4, in_=gval[:]), 16)

    if "probs" in dbg_names:
        od = nc.dram_tensor("dbg_probs", [128, 16, NE], F32, kind="ExternalOutput").ap()
        @block.scalar
        def _(a):
            a.wait_ge(sVE.h, VE_PROB)
            dA.inc(a.dma_start(out=od, in_=ptok[:]), 16)
    if "qt" in dbg_names:
        od = nc.dram_tensor("dbg_qt", [128, 8, TLOC], BF16, kind="ExternalOutput").ap()
        @block.scalar
        def _(a):
            a.wait_ge(sVE.h, VE_QT)
            dA.inc(a.dma_start(out=od, in_=qt[:]), 16)
    if "theta" in dbg_names:
        od = nc.dram_tensor("dbg_theta", [128, NE], F32, kind="ExternalOutput").ap()
        @block.scalar
        def _(a):
            a.wait_ge(sVE.h, VE_RT)
            dA.inc(a.dma_start(out=od, in_=b_theta[:]), 16)
        od2 = nc.dram_tensor("dbg_rtake", [128, NE], F32, kind="ExternalOutput").ap()
        @block.scalar
        def _(a):
            a.wait_ge(sVE.h, VE_RT)
            dA.inc(a.dma_start(out=od2, in_=b_rtake[:]), 16)

    if upto <= 6:
        @block.scalar
        def _(a):
            a.wait_ge(dA.h, dA.n)
            a.wait_ge(dVC.h, dVC.n)
            a.wait_ge(dGG.h, dGG.n)
            a.wait_ge(sCC.h, sCC.n)
            a.wait_ge(sVE.h, sVE.n)
        ctxS.close()
        ctxQ.close()
        ctx.close()
        return nc

    ctxS.close()

    ctxB = contextlib.ExitStack()
    sbtB = lambda name, shape, dt: ctxB.enter_context(nc.sbuf_tensor(name, shape, dt))
    ktb = [sbtB("ktb0", [128, S], BF16), sbtB("ktb1", [128, S], BF16)]
    vtb = [sbtB("vtb0", [128, 64, 130], BF16), sbtB("vtb1", [128, 64, 130], BF16)]
    ebuf = [sbtB("ebuf0", [128, 2048], BF16), sbtB("ebuf1", [128, 2048], BF16)]
    vtail = sbtB("vtail", [128, VT_BLKS, 128], BF16)
    bfp = sbtB("bfp", [128, 16 * 128], F32)
    btl = [sbtB("btl0", [128, 32, 128], BF16), sbtB("btl1", [128, 32, 128], BF16)]
    zrow = sbtB("zrow", [1, 1024], F32)
    pvs = sbtB("pvs", [64, 1024], BF16)
    rz = sbtB("rz", [128, 8], F32)
    ttmp = sbtB("ttmp", [128, 128], F32)

    @block.vector
    def _(v):
        v.wait_ge(sVE.h, sVE.n)
        v.memset(attn_tok[:], 0.0)
        for bi in range(2):
            v.memset(vtb[bi][:, :, 64:65], 1.0)
            ins = v.memset(vtb[bi][:, :, 129:130], 1.0)
        sVE.inc(ins)
    VE_ONES = sVE.n

    kt_ready = []
    bt_ready = []
    vtl_ready = []
    hp_pe_end = []
    exp_base = sACT.n
    qk_base = sPE.n

    exp_n = [0]
    pe_n = lambda: sPE.n
    tail_ve_done = [sVE.n]

    def emit_loads(hp, wait_pe):
        bi = hp % 2
        @block.sync
        def _(s, hp=hp, bi=bi, wait_pe=wait_pe):
            s.wait_ge(sCC.h, CC_KV)
            s.wait_ge(dZ.h, ST_ZE)
            s.wait_ge(sVE.h, VE_GV)
            s.wait_ge(sPE.h, PE_IDX)
            s.wait_ge(dGG.h, DG_SEL_END)
            s.wait_ge(dSEL.h, ST_SEL)
            s.wait_ge(dPG.h, LD_PG)
            if wait_pe is not None:
                s.wait_ge(sPE.h, wait_pe)
            ins = s.dma_start_transpose(ktb[bi][:], k_full.ap()[:, hp*128:(hp+1)*128])
            dKT[bi].inc(ins, 16)
            ins = s.dma_start(
                out=vtb[bi][:, :, 0:64],
                in_=v_pad.ap()[0:S, hp*128:hp*128+64]
                    .rearrange("(kc p) d -> p kc d", p=128))
            dKT[bi].inc(ins, 16)
            ins = s.dma_start(
                out=vtb[bi][:, :, 65:129],
                in_=v_pad.ap()[0:S, hp*128+64:hp*128+128]
                    .rearrange("(kc p) d -> p kc d", p=128))
            dKT[bi].inc(ins, 16)
        kt_ready.append(dKT[bi].n)

    def emit_bt(hp, wait_pe_tail):
        for hh in range(2):
            for half in range(2):
                @block.sync
                def _(s, hp=hp, hh=hh, half=half, wait_pe_tail=wait_pe_tail):
                    if hh == 0 and half == 0 and wait_pe_tail is not None:
                        s.wait_ge(sPE.h, wait_pe_tail)
                    if hh + half > 0:
                        s.wait_ge(sVE.h, sVE.n)
                    h = hp * 2 + hh
                    ins = s.dma_start(
                        out=bfp[:],
                        in_=btail_in[h][16*half:16*half+16].rearrange("m u r -> u m r"))
                    dBF.inc(ins, 16)
                dsn = dBF.n
                @block.vector
                def _(v, hh=hh, half=half, dsn=dsn):
                    v.wait_ge(dBF.h, dsn)
                    sVE.inc(v.tensor_copy(
                        btl[hh][:, 16*half:16*half+16, :],
                        bfp[:].rearrange("p (m r) -> p m r", r=128)))
        @block.sync
        def _(s, hp=hp):
            s.wait_ge(dVC.h, CP_VT)
            ins = s.dma_start(
                out=vtail[:],
                in_=vt_dram.ap().rearrange("(w p) d -> p w d", p=128)
                    [:, :, hp*128:(hp+1)*128])
            dVT.inc(ins, 16)
        vtl_ready.append(dVT.n)
        bt_ready.append(sVE.n)

    emit_loads(0, None)
    emit_loads(1, None)

    PE_ATT0 = sPE.n
    carry_rz_ve = None
    carry_pvcopy_ve = None
    for hp in range(nhp):
        bi = hp % 2
        emit_bt(hp, hp_pe_end[hp - 1] if hp >= 1 else None)

        qb_rz_ve = carry_rz_ve
        qb_pvcopy_ve = carry_pvcopy_ve
        for qb in (range(4) if "qb" not in skip else range(0)):
            pe_qk_ids = []
            exp_ids = []
            pv_ids = []
            qk0 = sPE.n
            NP = nkc // 2
            @block.tensor
            def _(t, hp=hp, qb=qb, bi=bi, qb_rz_ve=qb_rz_ve, qb_pvcopy_ve=qb_pvcopy_ve,
                  NP=NP):
                t.wait_ge(dKT[bi].h, kt_ready[hp])
                t.wait_ge(sVE.h, VE_ONES)
                for p in range(NP):
                    if p >= 1:
                        t.wait_ge(sACT.h, exp_base + exp_n[0] + p)
                    elif qb_rz_ve is not None:
                        t.wait_ge(sVE.h, qb_rz_ve)
                    for kc, off in ((2*p, 0), (2*p + 1, 1024)):
                        t.matmul(pS[:, off:off+512], ktb[bi][0:64, kc*128:(kc+1)*128],
                                 qt[0:64, hp, qb*512:(qb+1)*512],
                                 start=True, stop=True, tile_position=(0, 0))
                        ins = t.matmul(pS[:, off+512:off+1024],
                                       ktb[bi][64:128, kc*128:(kc+1)*128],
                                       qt[64:128, hp, qb*512:(qb+1)*512],
                                       start=True, stop=True, tile_position=(64, 0))
                    sPE.inc(ins)
                    if p >= 1 and "pv" not in skip:
                        t.wait_ge(sACT.h, exp_base + exp_n[0] + p)
                        eb = ebuf[(p - 1) % 2]
                        if p == 1 and qb_pvcopy_ve is not None:
                            t.wait_ge(sVE.h, qb_pvcopy_ve)
                        for j, kc in enumerate((2*(p-1), 2*(p-1) + 1)):
                            t.matmul(pC[0:65, :], vtb[bi][:, kc, 0:65],
                                     eb[:, j*1024:j*1024+512],
                                     start=(kc == 0), stop=False)
                            ins = t.matmul(pD[0:65, :], vtb[bi][:, kc, 65:130],
                                           eb[:, j*1024+512:j*1024+1024],
                                           start=(kc == 0), stop=False)
                        sPE.inc(ins)
                if "pv" in skip:
                    return
                t.wait_ge(sACT.h, exp_base + exp_n[0] + NP)
                eb = ebuf[(NP - 1) % 2]
                for j, kc in enumerate((nkc - 2, nkc - 1)):
                    t.matmul(pC[0:65, :], vtb[bi][:, kc, 0:65],
                             eb[:, j*1024:j*1024+512],
                             start=False, stop=(kc == nkc - 1))
                    ins = t.matmul(pD[0:65, :], vtb[bi][:, kc, 65:130],
                                   eb[:, j*1024+512:j*1024+1024],
                                   start=False, stop=(kc == nkc - 1))
                sPE.inc(ins)
            pe_qb_end = sPE.n

            @block.scalar
            def _(a, hp=hp, qb=qb, qk0=qk0, NP=NP):
                for p in range(NP):
                    a.wait_ge(sPE.h, qk0 + ((2 * p if p >= 1 else 1) if "pv" not in skip else p + 1))
                    ins = a.activation(ebuf[p % 2][:], pS[:], AF.Exp, scale=1.0 / 32.0)
                    sACT.inc(ins)
                    exp_n[0] += 1
            if "pv" in skip or "bnd" in skip:
                @block.vector
                def _(v, pe_qb_end=pe_qb_end):
                    v.wait_ge(sPE.h, pe_qb_end)
                    sVE.inc(v.tensor_copy(zrow[0:1, 0:512], pC[64:65, :] if "pv" not in skip else ebuf[0][0:1, 0:512]))
                qb_pvcopy_ve = sVE.n
                qb_rz_ve = sVE.n
                continue
            ve0 = sVE.n
            @block.vector
            def _(v, pe_qb_end=pe_qb_end):
                v.wait_ge(sPE.h, pe_qb_end)
                v.tensor_copy(zrow[0:1, 0:512], pC[64:65, :])
                sVE.inc(v.tensor_copy(zrow[0:1, 512:1024], pD[64:65, :]))
                v.tensor_copy(pvs[0:64, 0:512], pC[0:64, :])
                sVE.inc(v.tensor_copy(pvs[0:64, 512:1024], pD[0:64, :]))
            ve_zpv = sVE.n
            qb_pvcopy_ve = ve_zpv

            pe_z0 = sPE.n
            if "brz" in skip:
                @block.vector
                def _(v, ve_zpv=ve_zpv):
                    v.wait_ge(sVE.h, ve_zpv)
                    sVE.inc(v.memset(rz[:], 1.0))
                ve_rz = sVE.n
                qb_rz_ve = ve_rz
            else:
                @block.tensor
                def _(t, ve_zpv=ve_zpv):
                    t.wait_ge(sVE.h, ve_zpv)
                    for c in range(8):
                        ins = t.transpose(pA[:, 8+c:9+c],
                                          zrow[0:1, c*128:(c+1)*128], ident_f[0:1, 0:1])
                    sPE.inc(ins)
                pe_z = sPE.n

                @block.vector
                def _(v, pe_z=pe_z):
                    v.wait_ge(sPE.h, pe_z)
                    sVE.inc(v.reciprocal(rz[:], pA[:, 8:16]))
                ve_rz = sVE.n
                qb_rz_ve = ve_rz

            for sub in (range(4) if "bst" not in skip else range(0)):
                pe_s0 = sPE.n
                @block.tensor
                def _(t, sub=sub, ve_zpv=ve_zpv):
                    t.wait_ge(sVE.h, ve_zpv)
                    if sub >= 2:
                        t.wait_ge(sVE.h, sVE.n)
                    pb_ = pBF if sub % 2 == 0 else pBF2
                    t.transpose(pb_[:, 0:64], pvs[0:64, sub*128:(sub+1)*128],
                                ident_bf[0:64, 0:64])
                    ins = t.transpose(pb_[:, 64:128], pvs[0:64, 512+sub*128:512+(sub+1)*128],
                                      ident_bf[0:64, 0:64])
                    sPE.inc(ins)
                pe_sT = sPE.n

                @block.vector
                def _(v, hp=hp, qb=qb, sub=sub, pe_sT=pe_sT, ve_rz=ve_rz):
                    v.wait_ge(sPE.h, pe_sT)
                    v.wait_ge(sVE.h, ve_rz)
                    pb_ = pBF if sub % 2 == 0 else pBF2
                    at = qb * 4 + sub
                    v.scalar_tensor_tensor(
                        attn_tok[:, at, hp*128:hp*128+64], pb_[:, 0:64],
                        rz[:, sub:sub+1], attn_tok[:, at, hp*128:hp*128+64],
                        ALU.mult, ALU.add)
                    ins = v.scalar_tensor_tensor(
                        attn_tok[:, at, hp*128+64:hp*128+128], pb_[:, 64:128],
                        rz[:, 4+sub:5+sub], attn_tok[:, at, hp*128+64:hp*128+128],
                        ALU.mult, ALU.add)
                    sVE.inc(ins)

        @block.tensor
        def _(t, hp=hp):
            t.wait_ge(dVT.h, vtl_ready[hp])
            t.wait_ge(sVE.h, bt_ready[hp])
        for a_ in (range(16) if "tail" not in skip else range(0)):
            pe_t0 = sPE.n
            @block.tensor
            def _(t, hp=hp, a_=a_):
                if a_ >= 1:
                    t.wait_ge(sVE.h, sVE.n)
                nm = 32 - a_
                for m in range(nm):
                    t.matmul(pB[0:64, 0:128], vtail[:, a_+m, 0:64], btl[0][:, m, :],
                             start=(m == 0), stop=(m == nm - 1), tile_position=(0, 0))
                    ins = t.matmul(pB[64:128, 512:640], vtail[:, a_+m, 64:128], btl[1][:, m, :],
                                   start=(m == 0), stop=(m == nm - 1), tile_position=(0, 64))
                sPE.inc(ins)
            pe_tm = sPE.n

            @block.vector
            def _(v, pe_tm=pe_tm):
                v.wait_ge(sPE.h, pe_tm)
                v.tensor_copy(ttmp[0:64, :], pB[0:64, 0:128])
                sVE.inc(v.tensor_copy(ttmp[64:128, :], pB[64:128, 512:640]))
            ve_tc = sVE.n

            @block.tensor
            def _(t, ve_tc=ve_tc):
                t.wait_ge(sVE.h, ve_tc)
                sPE.inc(t.transpose(pA[:, 256:384], ttmp[:], ident_f[:]))
            pe_tt = sPE.n

            @block.vector
            def _(v, hp=hp, a_=a_, pe_tt=pe_tt):
                v.wait_ge(sPE.h, pe_tt)
                sVE.inc(v.tensor_add(attn_tok[:, a_, hp*128:(hp+1)*128],
                                     attn_tok[:, a_, hp*128:(hp+1)*128],
                                     pA[:, 256:384]))
        hp_pe_end.append(sPE.n)
        carry_rz_ve = sVE.n
        carry_pvcopy_ve = qb_pvcopy_ve
        if hp + 2 < 8:
            emit_loads(hp + 2, hp_pe_end[hp])
    ATT_VE_END = sVE.n
    ATT_PE_END = sPE.n

    if upto == 7:
        od7 = nc.dram_tensor("dbg_attn", [128, 16, D], BF16, kind="ExternalOutput").ap()
        @block.scalar
        def _(a):
            a.wait_ge(sVE.h, ATT_VE_END)
            a.wait_ge(sPE.h, ATT_PE_END)
            dA.inc(a.dma_start(out=od7, in_=attn_tok[:]), 16)
            a.wait_ge(dA.h, dA.n)
        ctxB.close()
        ctxQ.close()
        ctx.close()
        return nc

    ctxB.close()
    ctxQ.close()

    ctxC = contextlib.ExitStack()
    sbtC = lambda name, shape, dt: ctxC.enter_context(nc.sbuf_tensor(name, shape, dt))
    vps = sbtC("vps", [128, 4, D], BF16)
    bex_sb = sbtC("bex_sb", [16, D], BF16)

    @block.sync
    def _(s):
        s.wait_ge(sPE.h, ATT_PE_END)
        s.wait_ge(sVE.h, ATT_VE_END)
        for kb in range(4):
            ins = s.dma_start(
                out=vps[:, kb % 4, :],
                in_=v_pad.ap()[kb*128:(kb+1)*128, :])
            dPS[kb % 4].inc(ins, 16)
    PE_BEX0 = sPE.n
    for kb in range(64):
        if kb >= 4:
            @block.sync
            def _(s, kb=kb):
                s.wait_ge(sPE.h, PE_BEX0 + (kb - 4) + 1)
                ins = s.dma_start(out=vps[:, kb % 4, :],
                                  in_=v_pad.ap()[kb*128:(kb+1)*128, :])
                dPS[kb % 4].inc(ins, 16)
        @block.tensor
        def _(t, kb=kb, dw=dPS[kb % 4].n):
            t.wait_ge(dPS[kb % 4].h, dw)
            t.matmul(pA[0:16, 0:512], bexw[:, kb, :], vps[:, kb % 4, 0:512],
                     start=(kb == 0), stop=(kb == 63))
            ins = t.matmul(pA[0:16, 512:1024], bexw[:, kb, :], vps[:, kb % 4, 512:1024],
                           start=(kb == 0), stop=(kb == 63))
            sPE.inc(ins)
    PE_BEX = sPE.n

    @block.scalar
    def _(a):
        a.wait_ge(sPE.h, PE_BEX)
        sACT.inc(a.activation(bex_sb[:], pA[0:16, :], AF.Copy))
    ACT_BEX = sACT.n

    vps2 = sbtC("vps2", [128, 3, D], BF16)
    @block.sync
    def _(s):
        s.wait_ge(sACT.h, ACT_BEX)
        for i in range(3):
            ins = s.dma_start(out=vps2[:, i, :],
                              in_=vt_dram.ap()[i*128:(i+1)*128, :])
            dP2[i].inc(ins, 16)
    PE_PS0 = sPE.n
    VE_PS0 = sVE.n
    for a_ in range(16):
        if a_ >= 1 and a_ + 2 <= 17:
            @block.sync
            def _(s, a_=a_):
                s.wait_ge(sPE.h, PE_PS0 + a_)
                ins = s.dma_start(out=vps2[:, (a_+2) % 3, :],
                                  in_=vt_dram.ap()[(a_+2)*128:(a_+3)*128, :])
                dP2[(a_+2) % 3].inc(ins, 16)
        @block.tensor
        def _(t, a_=a_, dwa=dP2[a_ % 3].n, dwb=dP2[(a_+1) % 3].n):
            t.wait_ge(dP2[a_ % 3].h, dwa)
            t.wait_ge(dP2[(a_+1) % 3].h, dwb)
            if a_ >= 1:
                t.wait_ge(sVE.h, VE_PS0 + a_ * 4)
            for ob in range(2):
                t.matmul(pB[:, ob*512:(ob+1)*512], trishift[:],
                         vps2[:, a_ % 3, ob*512:(ob+1)*512], start=True, stop=False)
                t.matmul(pB[:, ob*512:(ob+1)*512], onehotD[:],
                         vps2[:, (a_+1) % 3, ob*512:(ob+1)*512], start=False, stop=False)
                ins = t.matmul(pB[:, ob*512:(ob+1)*512],
                               selm[:, a_*128:(a_+1)*128],
                               bex_sb[:, ob*512:(ob+1)*512], start=False, stop=True)
            sPE.inc(ins)
        pe_ps = sPE.n

        @block.vector
        def _(v, a_=a_, pe_ps=pe_ps):
            v.wait_ge(sPE.h, pe_ps)
            for ob in range(2):
                v.wait_ge(sVE.h, sVE.n)
                sVE.inc(v.tensor_mul(scr1k[:, ob*512:(ob+1)*512], pB[:, ob*512:(ob+1)*512],
                                     ctile[:, ob*512:(ob+1)*512]))
                v.wait_ge(sVE.h, sVE.n)
                ins = v.tensor_add(attn_tok[:, a_, ob*512:(ob+1)*512],
                                   attn_tok[:, a_, ob*512:(ob+1)*512],
                                   scr1k[:, ob*512:(ob+1)*512])
                sVE.inc(ins)
    VE_PS = sVE.n

    @block.vector
    def _(v):
        v.wait_ge(sVE.h, VE_PS)
        for i in range(16):
            ins = v.reduce_sum(stat_a[:, i:i+1], attn_tok[:, i, :], axis=AX.X, negate=True)
        sVE.inc(ins)
    VE_NS2 = sVE.n

    @block.scalar
    def _(a):
        a.wait_ge(sVE.h, VE_PS)
        for i in range(16):
            ins = a.activation(scr1k[:], attn_tok[:, i, :], AF.Square,
                               accum_out=stat_b[:, i:i+1])
            sACT.inc(ins)
            a.wait_ge(sACT.h, sACT.n)
    ACT_SQ2 = sACT.n

    @block.vector
    def _(v):
        v.wait_ge(sACT.h, ACT_SQ2)
        v.wait_ge(sVE.h, VE_NS2)
        sVE.inc(v.tensor_scalar(stat_c[:], stat_a[:], 1.0 / D, None, ALU.mult))
        v.wait_ge(sVE.h, sVE.n)
        sVE.inc(v.tensor_mul(stat_d[:], stat_c[:], stat_c[:]))
        v.wait_ge(sVE.h, sVE.n)
        sVE.inc(v.tensor_scalar(stat_a[:], stat_b[:], 1.0 / D, None, ALU.mult))
        v.wait_ge(sVE.h, sVE.n)
        sVE.inc(v.tensor_sub(stat_d[:], stat_a[:], stat_d[:]))
    VE_VAR2 = sVE.n

    @block.vector
    def _(v):
        v.wait_ge(sVE.h, VE_VAR2)
        sVE.inc(v.tensor_scalar(stat_d[:], stat_d[:], EPS, None, ALU.add))
    VE_VEPS2 = sVE.n

    @block.scalar
    def _(a):
        a.wait_ge(sVE.h, VE_VEPS2)
        sACT.inc(a.activation(stat_d[:], stat_d[:], AF.Ln))
        a.wait_ge(sACT.h, sACT.n)
        sACT.inc(a.activation(stat_d[:], stat_d[:], AF.Exp, scale=-0.5))
    ACT_RSTD2 = sACT.n

    @block.vector
    def _(v):
        v.wait_ge(sACT.h, ACT_RSTD2)
        sVE.inc(v.tensor_mul(stat_e[:], stat_c[:], stat_d[:]))
    VE_MRS2 = sVE.n

    for i in range(16):
        @block.scalar
        def _(a, i=i):
            a.wait_ge(sVE.h, VE_MRS2)
            if i > 0:
                a.wait_ge(sVE.h, sVE.n)
            sACT.inc(a.activation(scr1kb[:], attn_tok[:, i, :], AF.Identity,
                                  bias=stat_e[:, i:i+1], scale=stat_d[:, i:i+1]))
        acn = sACT.n

        @block.vector
        def _(v, i=i, acn=acn):
            v.wait_ge(sACT.h, acn)
            sVE.inc(v.tensor_mul(scr1k[:], scr1kb[:], gtile[:]))
            v.wait_ge(sVE.h, sVE.n)
            ins = v.tensor_add(attn_tok[:, i, :], scr1k[:], btile[:])
            sVE.inc(ins)
    VE_LN2 = sVE.n

    xnsel = sbtC("xnsel", [128, D], BF16)
    xnselT = sbtC("xnselT", [128, 8, CAP], BF16)
    midT = sbtC("midT", [128, NFC + 1, CAP], BF16)
    wupb = [sbtC("wupb0", [128, 8, 128], BF16), sbtC("wupb1", [128, 8, 128], BF16)]
    wdnb = [sbtC("wdnb0", [128, D], BF16), sbtC("wdnb1", [128, D], BF16)]
    bdnrow = sbtC("bdnrow", [1, D], BF16)
    buprow = sbtC("buprow", [1, FPAD], BF16)
    yout3 = sbtC("yout3", [128, 3, D], F32)

    @block.vector
    def _(v):
        v.wait_ge(sVE.h, sVE.n)
        sVE.inc(v.memset(midT[0:1, NFC, :], 1.0))
    VE_MID1 = sVE.n

    @block.gpsimd
    def _(g):
        g.wait_ge(sPE.h, ATT_PE_END)
        g.wait_ge(sVE.h, ATT_VE_END)

    bupt = sbtC("bupt", [NFC, 128], F32)
    bupall = sbtC("bupall", [128, 4, NFC], F32)
    for gi in range(4):
        @block.gpsimd
        def _(g, gi=gi):
            g.wait_ge(dG.h, dG.n)
            if gi > 0:
                g.wait_ge(sPE.h, sPE.n)
            ins = g.dma_start(out=bupt[:],
                              in_=bup_in[gi:gi+1, :].rearrange("o (c p) -> (o c) p", p=128))
            dG.inc(ins, 16)
        dgn = dG.n

        @block.tensor
        def _(t, gi=gi, dgn=dgn):
            t.wait_ge(dG.h, dgn)
            if gi > 0:
                t.wait_ge(sVE.h, sVE.n)
            sPE.inc(t.transpose(pA[:, 0:NFC], bupt[:], ident_f[0:NFC, 0:NFC]))
        pen = sPE.n

        @block.vector
        def _(v, gi=gi, pen=pen):
            v.wait_ge(sPE.h, pen)
            sVE.inc(v.tensor_copy(bupall[:, gi, :], pA[:, 0:NFC]))
    VE_BUP = sVE.n

    for e in range(NE):
        ge_ = e // 2
        @block.gpsimd
        def _(g, e=e, ge_=ge_):
            if e > 0:
                g.wait_ge(sPE.h, sPE.n)
            ins = g.dma_start(out=bdnrow[:], in_=bdn_in[e:e+1, :])
            dBD.inc(ins, 16)
            ins = g.dma_start(out=buprow[:], in_=bup_in[ge_:ge_+1, :])
            dBD.inc(ins, 16)
        dg_b = dBD.n

        for sc in range(3):
            @block.gpsimd
            def _(g, e=e, sc=sc):
                g.wait_ge(sVE.h, VE_GV)
                if e == 0 and sc == 0:
                    for j in range(4):
                        g.wait_ge(dXN[j].h, dXN[j].n)
                else:
                    g.wait_ge(sPE.h, sPE.n)
                ins = g.indirect_dma_start(
                    out=xnsel[:], out_offset=None,
                    in_=xn_dram.ap(),
                    in_offset=bass.IndirectOffsetOnAxis(ap=idxi[:, sc, e:e+1], axis=0))
                dXG.inc(ins, 16)
            dg_x = dXG.n

            @block.tensor
            def _(t, e=e, sc=sc, dg_x=dg_x):
                t.wait_ge(dXG.h, dg_x)
                if e == 0 and sc == 0:
                    t.wait_ge(sVE.h, VE_MID1)
                for c in range(8):
                    t.wait_ge(sVE.h, sVE.n)
                    buf = pBF if c % 2 == 0 else pBF2
                    ins = t.transpose(buf[:, 0:128], xnsel[:, c*128:(c+1)*128], ident_bf[:])
                    sPE.inc(ins)
                    @block.vector
                    def _(v, sc=sc, c=c, pen=sPE.n, buf=buf):
                        v.wait_ge(sPE.h, pen)
                        sVE.inc(v.tensor_copy(xnselT[:, c, sc*128:(sc+1)*128],
                                              buf[:, 0:128]))
        ve_xt = sVE.n

        for fc in range(NFC):
            wbu = wupb[fc % 2]
            @block.gpsimd
            def _(g, e=e, ge_=ge_, fc=fc, wbu=wbu):
                if fc >= 2:
                    g.wait_ge(sPE.h, sPE.n)
                ins = g.dma_start(
                    out=wbu[:],
                    in_=wup_in[ge_:ge_+1, :, fc*128:(fc+1)*128]
                        .rearrange("o (c p) f -> p (o c) f", p=128))
                dWU[fc % 2].inc(ins, 16)
            dg_w = dWU[fc % 2].n

            @block.tensor
            def _(t, e=e, fc=fc, wbu=wbu, dg_w=dg_w, ve_xt=ve_xt, ve_now=sVE.n,
                  dg_b=dg_b):
                t.wait_ge(dWU[fc % 2].h, dg_w)
                t.wait_ge(sVE.h, ve_xt)
                if fc == 0:
                    t.wait_ge(dBD.h, dg_b)
                if fc < 2:
                    t.wait_ge(sVE.h, ve_now)
                if fc >= 2:
                    t.wait_ge(sACT.h, sACT.n)
                buf = pC if fc % 2 == 0 else pD
                for k in range(8):
                    t.matmul(buf[:, 0:CAP], wbu[:, k, :], xnselT[:, k, :],
                             start=(k == 0), stop=False)
                ins = t.matmul(buf[:, 0:CAP], buprow[0:1, fc*128:(fc+1)*128],
                               midT[0:1, NFC, 0:CAP], start=False, stop=True)
                sPE.inc(ins)
            pe_up = sPE.n

            @block.scalar
            def _(a, fc=fc, pe_up=pe_up):
                a.wait_ge(sPE.h, pe_up)
                buf = pC if fc % 2 == 0 else pD
                sACT.inc(a.activation(midT[:, fc, :], buf[:, 0:CAP], AF.Silu))
        act_up = sACT.n

        accs = [pA[:, 0:512], pA[:, 512:1024], pB[:, 0:512], pB[:, 512:1024],
                pC[:, 0:512], pD[:, 0:512]]
        ve_prev_evac = sVE.n
        for fc in range(NFC):
            wbd = wdnb[fc % 2]
            @block.gpsimd
            def _(g, e=e, fc=fc, wbd=wbd):
                if fc >= 2:
                    g.wait_ge(sPE.h, sPE.n)
                elif e > 0:
                    g.wait_ge(sPE.h, sPE.n)
                ins = g.dma_start(
                    out=wbd[:],
                    in_=wdn_in.rearrange("e f h -> (e f) h")
                        [e*FPAD + fc*128: e*FPAD + (fc+1)*128, :])
                dWD[fc % 2].inc(ins, 16)
            dg_w = dWD[fc % 2].n

            @block.tensor
            def _(t, e=e, fc=fc, wbd=wbd, dg_w=dg_w, act_up=act_up,
                  ve_prev_evac=ve_prev_evac):
                t.wait_ge(dWD[fc % 2].h, dg_w)
                if fc == 0:
                    t.wait_ge(sACT.h, act_up)
                    t.wait_ge(sVE.h, ve_prev_evac)
                for sc in range(3):
                    for ob in range(2):
                        ins = t.matmul(accs[sc*2+ob], midT[:, fc, sc*128:(sc+1)*128],
                                       wbd[:, ob*512:(ob+1)*512],
                                       start=(fc == 0), stop=False)
                sPE.inc(ins)
        @block.tensor
        def _(t, e=e, dg_b=dg_b):
            t.wait_ge(dBD.h, dg_b)
            for sc in range(3):
                for ob in range(2):
                    ins = t.matmul(accs[sc*2+ob], midT[0:1, NFC, sc*128:(sc+1)*128],
                                   bdnrow[0:1, ob*512:(ob+1)*512],
                                   start=False, stop=True)
            sPE.inc(ins)
        pe_dn = sPE.n

        for sc in range(3):
            @block.vector
            def _(v, e=e, sc=sc, pe_dn=pe_dn, dgn=dY.n):
                v.wait_ge(sPE.h, pe_dn)
                if sc == 0:
                    v.wait_ge(dY.h, dgn)
                for ob in range(2):
                    ins = v.tensor_scalar(yout3[:, sc, ob*512:(ob+1)*512],
                                          accs[sc*2+ob],
                                          gval[:, sc, e:e+1], None, ALU.mult)
                sVE.inc(ins)
        ve_y3 = sVE.n

        @block.gpsimd
        def _(g, e=e, ve_y3=ve_y3):
            g.wait_ge(sVE.h, ve_y3)
            g.wait_ge(dZ.h, ST_ZE)
            for sc in range(3):
                g.wait_ge(dY.h, dY.n)
                ins = g.indirect_dma_start(
                    out=y_dram.ap(),
                    out_offset=bass.IndirectOffsetOnAxis(ap=idxi[:, sc, e:e+1], axis=0),
                    in_=yout3[:, sc, :], in_offset=None,
                    compute_op=ALU.add)
                dY.inc(ins, 16)
    DG_FFN = dY.n

    xch = sbtC("xch", [128, 4, D], F32)
    ych = sbtC("ych", [128, 4, D], F32)
    och = sbtC("och", [128, 4, D], F32)

    for ch in range(4):
        @block.sync
        def _(s, ch=ch):
            s.wait_ge(dY.h, DG_FFN)
            if ch > 0:
                s.wait_ge(dA.h, dA.n)
            ins = s.dma_start(out=xch[:],
                              in_=x_in.rearrange("(a p) d -> p a d", p=128)
                              [:, ch*4:(ch+1)*4, :])
            dFX.inc(ins, 16)
            ins = s.dma_start(out=ych[:],
                              in_=y_dram.ap().rearrange("(a p) d -> p a d", p=128)
                              [:, ch*4:(ch+1)*4, :])
            dFX.inc(ins, 16)
        ds_f = dFX.n

        @block.vector
        def _(v, ch=ch, ds_f=ds_f):
            v.wait_ge(dFX.h, ds_f)
            v.wait_ge(sVE.h, VE_LN2)
            for j in range(4):
                v.wait_ge(sVE.h, sVE.n)
                sVE.inc(v.tensor_add(och[:, j, :], xch[:, j, :], ych[:, j, :]))
                v.wait_ge(sVE.h, sVE.n)
                ins = v.tensor_add(och[:, j, :], och[:, j, :],
                                   attn_tok[:, ch*4+j, :])
                sVE.inc(ins)
        ve_f = sVE.n

        @block.scalar
        def _(a, ch=ch, ve_f=ve_f):
            a.wait_ge(sVE.h, ve_f)
            ins = a.dma_start(out=out.rearrange("(a p) d -> p a d", p=128)
                              [:, ch*4:(ch+1)*4, :],
                              in_=och[:])
            dA.inc(ins, 16)

    @block.scalar
    def _(a):
        a.wait_ge(dA.h, dA.n)
        a.wait_ge(dY.h, dY.n)
        a.wait_ge(sCC.h, sCC.n)

    ctxC.close()
    ctx.close()
    return nc


def _host_prep(inputs):
    x = np.ascontiguousarray(np.asarray(inputs["L0_attn_x"], dtype=np.float32).reshape(NT, D))
    Wq = np.asarray(inputs["L0_attn_Wq"], np.float32)
    Wk = np.asarray(inputs["L0_attn_Wk"], np.float32)
    Wv = np.asarray(inputs["L0_attn_Wv"], np.float32)
    T = np.asarray(inputs["L0_attn_rel_bias_table"], np.float32)
    g2 = np.asarray(inputs["L0_attn_ln_gamma"], np.float32)
    b2 = np.asarray(inputs["L0_attn_ln_beta"], np.float32)
    Wr = np.asarray(inputs["L0_ffn_Wr"], np.float32)
    brv = np.asarray(inputs["L0_ffn_br"], np.float32)
    W_up = np.asarray(inputs["L0_ffn_W_up"], np.float32)
    b_up = np.asarray(inputs["L0_ffn_b_up"], np.float32)
    W_down = np.asarray(inputs["L0_ffn_W_down"], np.float32)
    b_down = np.asarray(inputs["L0_ffn_b_down"], np.float32)

    wqt = np.ascontiguousarray(Wq.T)
    wkt = np.ascontiguousarray(Wk.T)
    wvt = np.ascontiguousarray(Wv.T)
    wup = np.zeros((4, D, FPAD), np.float32); wup[:, :, :2730] = W_up
    bup = np.zeros((4, FPAD), np.float32); bup[:, :2730] = b_up
    wdn = np.zeros((NE, FPAD, D), np.float32); wdn[:, :2730, :] = W_down
    uu_, rr_ = np.meshgrid(np.arange(128), np.arange(128), indexing="ij")
    btail = np.zeros((H, 32, 128, 128), np.float32)
    for m_ in range(32):
        idxm = 4095 - 128 * m_ - uu_ + rr_
        valid = (idxm >= 0) & (idxm <= 4093)
        idxc = np.clip(idxm, 0, 4093)
        btail[:, m_, :, :] = np.where(valid[None], T[idxc, :].transpose(2, 0, 1), 0.0)
    crow = np.ascontiguousarray(np.repeat(T[4094, :], HD)[None, :].astype(np.float32))
    ident = np.eye(128, dtype=np.float32)
    uu, rr = np.meshgrid(np.arange(128), np.arange(128), indexing="ij")
    trishift = (uu <= rr + 1).astype(np.float32)
    onehotD = np.zeros((128, 128), np.float32); onehotD[0, 127] = 1.0
    lstrict = (uu < rr).astype(np.float32)
    onescol = np.ones((128, 1), np.float32)
    iota_slot = np.broadcast_to(np.arange(CAP, dtype=np.float32), (128, CAP)).copy()
    tloc_p = np.arange(128, dtype=np.float32)[:, None].copy()
    selmv = np.zeros((16, 16 * 128), np.float32)
    for a in range(16):
        selmv[a, a*128:(a+1)*128] = 1.0

    in_maps = []
    for c in range(NCORES):
        cp = c % 4
        C0 = 16 * cp + 32
        bexw = np.zeros((64, 16), np.float32)
        for a in range(16):
            bexw[:, a] = (np.arange(64) < C0 + a).astype(np.float32)
        vtrows = ((C0 + np.arange(VT_BLKS, dtype=np.int32))[None, :] * 128
                  + np.arange(128, dtype=np.int32)[:, None]).astype(np.int32).copy()
        selrows = (2048 * c + np.arange(16, dtype=np.int32)[None, :] * 128
                   + np.arange(128, dtype=np.int32)[:, None]).astype(np.int32).copy()
        in_maps.append(dict(
            x_loc=x[TLOC*c:TLOC*(c+1)],
            wqt=wqt, wkt=wkt, wvt=wvt, wr=Wr, br=brv[:, None].copy(),
            wup=wup, bup=bup, wdn=wdn, bdn=b_down,
            btail=btail, crow=crow,
            grow=np.ascontiguousarray(g2[None, :]),
            brow=np.ascontiguousarray(b2[None, :]),
            ident=ident, trishift=trishift, onehotD=onehotD,
            lstrict=lstrict, onescol=onescol, iota_slot=iota_slot,
            tloc_p=tloc_p, bexw=bexw.reshape(1, -1).copy(), selm=selmv,
            vtrows=vtrows, selrows=selrows,
        ))
    return in_maps


def kernel(**inputs):
    in_maps = _host_prep(inputs)
    nc = build_program()
    res = run_bass_kernel_spmd(nc, in_maps, core_ids=list(range(NCORES)))
    shards = [np.asarray(res.results[c]["out"]) for c in range(NCORES)]
    return np.concatenate(shards, axis=0).reshape(2, S, D)
